# version 12
# baseline (speedup 1.0000x reference)
import contextlib

import numpy as np

import concourse.bass as bass
import concourse.mybir as mybir
from concourse.bass_utils import run_bass_kernel_spmd

F32 = mybir.dt.float32
BF16 = mybir.dt.bfloat16
F16 = mybir.dt.float16
AF = mybir.ActivationFunctionType
OP = mybir.AluOpType
AX = mybir.AxisListType

LN_EPS = 1e-5
FFN_EPS = 1e-10


def _tchunks(total, step=512):
    return [(s, min(s + step, total)) for s in range(0, total, step)]


def build_nc(C, T_OWN, LANE):
    CH = C // 128
    HC = C // 2
    OH = HC // 128
    NL = T_OWN // LANE
    T_ALL = LANE + T_OWN
    T_STG = T_ALL + 1
    TC_ALL = _tchunks(T_ALL)
    TC_OWN = _tchunks(T_OWN)
    TC_STG = _tchunks(T_STG)
    NB = 4

    nc = bass.Bass("TRN2", target_bir_lowering=False)

    x_d = nc.dram_tensor("x", [C, T_STG], F32, kind="ExternalInput").ap()
    w_d = {k: nc.dram_tensor(k, [C, C], F32, kind="ExternalInput").ap()
           for k in ("wk", "wv", "wr", "wg", "wo", "wf")}
    vnames = ("tmk", "tmv", "tmr", "tmg", "td", "tf", "lng", "lnb", "bfv")
    vec_d = {k: nc.dram_tensor(k, [128, CH], F32, kind="ExternalInput").ap()
             for k in vnames}
    cs_d = nc.dram_tensor("cs", [128, 1], F32, kind="ExternalInput").ap()
    y_d = nc.dram_tensor("y", [CH, 128, T_OWN], F32, kind="ExternalOutput").ap()

    sxk = nc.dram_tensor("sxk", [CH, 128, T_ALL], BF16).ap()
    sxv = nc.dram_tensor("sxv", [CH, 128, T_ALL], BF16).ap()
    sxr = nc.dram_tensor("sxr", [CH, 128, T_OWN], BF16).ap()
    sxg = nc.dram_tensor("sxg", [CH, 128, T_OWN], BF16).ap()
    sek = nc.dram_tensor("sek", [CH, 128, T_ALL], F32).ap()
    svd = nc.dram_tensor("svd", [CH, 128, T_ALL], F32).ap()
    sg2 = nc.dram_tensor("sg2", [CH, 128, T_OWN], BF16).ap()

    es0 = contextlib.ExitStack()
    e0 = es0.enter_context

    sem = {n: e0(nc.semaphore(n)) for n in
           ("sl", "sw0", "sw1", "scv", "scs", "smm", "sev", "st0", "st1",
            "sv", "ss", "sden", "srd", "sg0", "sg1")}
    cnt = {n: 0 for n in sem}

    def snap():
        return dict(cnt)

    def inc(instr, name, amt=1):
        instr.then_inc(sem[name], amt)
        cnt[name] += amt

    vec_sb = {k: e0(nc.sbuf_tensor("v_" + k, [128, CH], F32)) for k in vnames}
    cs_sb = e0(nc.sbuf_tensor("cs_sb", [128, 1], F32))
    w_sb = e0(nc.sbuf_tensor("w_sb", [128, CH], F32))
    eu_sb = e0(nc.sbuf_tensor("eu_sb", [128, CH], F32))
    w_rep = e0(nc.sbuf_tensor("w_rep", [128, CH, LANE], F32))
    ones_lane = e0(nc.sbuf_tensor("ones_lane", [128, LANE], F32))
    ones16 = e0(nc.sbuf_tensor("ones16", [128, 1], F16))
    onesK1 = e0(nc.sbuf_tensor("onesK1", [1, 128], F32))
    scale_sb = e0(nc.sbuf_tensor("scale_sb", [128, 1], F32))
    thr_sb = e0(nc.sbuf_tensor("thr_sb", [128, 1], F32))
    nthr_sb = e0(nc.sbuf_tensor("nthr_sb", [128, 1], F32))

    with nc.Block() as blk:
        b = snap()

        @blk.sync
        def _(e):
            for k in vnames:
                inc(e.dma_start(vec_sb[k][:, :], vec_d[k][:, :]), "sl", 16)
            inc(e.dma_start(cs_sb[:, :], cs_d[:, :]), "sl", 16)

        @blk.vector
        def _(e):
            e.memset(ones_lane[:, :], 1.0)
            e.memset(ones16[:, :], 1.0)
            inc(e.memset(onesK1[:, :], 1.0), "sv")
            e.wait_ge(sem["ss"], b["ss"] + 1)
            e.drain()
            for ci in range(CH):
                e.tensor_scalar(w_rep[:, ci, :], ones_lane[:, :],
                                w_sb[:, ci:ci + 1], None, OP.mult)

        @blk.scalar
        def _(e):
            e.wait_ge(sem["sl"], b["sl"] + 16 * (len(vnames) + 1))
            e.activation(w_sb[:, :], vec_sb["td"][:, :], AF.Exp)
            e.drain()
            e.activation(w_sb[:, :], w_sb[:, :], AF.Exp, scale=-1.0)
            e.drain()
            inc(e.activation(eu_sb[:, :], vec_sb["tf"][:, :], AF.Exp), "ss")

    group_id = [0]

    def mm_group(specs, shared, pre=None, tail_sync=None, tail_scalar=None):
        group_id[0] += 1
        P = len(specs)
        ntc = [len(s["tcs"]) for s in specs]
        ngh = [OH * n for n in ntc]
        GRP0 = [0] * P
        for p in range(1, P):
            GRP0[p] = GRP0[p - 1] + 2 * ngh[p - 1]
        NG = GRP0[-1] + 2 * ngh[-1]

        def g_info(g):
            p = 0
            while p + 1 < P and g >= GRP0[p + 1]:
                p += 1
            r = g - GRP0[p]
            h = r // ngh[p]
            r2 = r % ngh[p]
            oc = r2 // ntc[p]
            tj = r2 % ntc[p]
            return p, h, oc, tj

        NH = 2 * P
        GEND = [0] * NH
        for hh in range(NH):
            p, h = hh // 2, hh % 2
            GEND[hh] = GRP0[p] + (h + 1) * ngh[p]
        release = [0 if hh < 2 else GEND[hh - 2] for hh in range(NH)]

        preL = pre["preL"] if pre else 0
        castrec = list(pre["recs"]) if pre else []
        cv = cnt["scv"]
        cs_ = cnt["scs"]
        b0scv, b0scs = cnt["scv"], cnt["scs"]
        CV = [0] * NH
        CS = [0] * NH
        for hh in range(NH):
            p = hh // 2
            for ic in range(CH):
                jj = hh * CH + ic
                if jj < preL:
                    continue
                if specs[p]["quant"] or ic % 2 == 0:
                    cv += 1
                    castrec.append(("V", "scv", cv))
                else:
                    cs_ += 1
                    castrec.append(("S", "scs", cs_))
            CV[hh], CS[hh] = cv, cs_

        movpre = pre.get("mov_sl", {}) if pre else {}
        slt = {}
        run_sl = cnt["sl"]
        for p, s in enumerate(specs):
            if p in movpre:
                slt[p] = movpre[p]
            elif s["mov_load"] is not None:
                run_sl += 16 * CH
                slt[p] = run_sl

        signum = {}
        k = 0
        for g in range(NG):
            p, h, oc, tj = g_info(g)
            if specs[p]["evac"] in ("sig_attn", "sig_y_res"):
                signum[g] = k
                k += 1
        nsig = k
        rownum = {}
        g2num = {}
        sk = 0
        for p, s in enumerate(specs):
            if s["evac"] in ("exp_row", "copy_row"):
                for ocg in range(CH):
                    rownum[(p, ocg)] = sk
                    sk += 1
        for g in range(NG):
            p, h, oc, tj = g_info(g)
            if specs[p]["evac"] == "silu_g2":
                g2num[g] = sk
                sk += 1

        b = snap()
        NBK = len(shared["ps"])
        wbf = shared["wbf"]
        mwstg = shared["mwstg"]
        mwm2 = shared["mwm2"]
        rowst = shared["rowst"]
        sigst = shared["sigst"]
        g2st = shared["g2st"]
        ps = shared["ps"]
        blk = shared["es"].enter_context(nc.Block())

        mov_at = {}
        for p, s in enumerate(specs):
            if s["mov_load"] is not None and p not in movpre:
                mov_at.setdefault(GRP0[p], []).append(p)

        @blk.sync
        def _(e):
            for p, s in enumerate(specs):
                for h in range(2):
                    hh = 2 * p + h
                    for ic in range(CH):
                        jj = hh * CH + ic
                        if jj < preL:
                            continue
                        if jj >= 2:
                            eng, sname, tgt = castrec[jj - 2]
                            e.wait_ge(sem[sname], tgt)
                        inc(e.dma_start(mwstg[jj % 2][:, :],
                                        s["wt"][ic * 128:(ic + 1) * 128,
                                                (h * HC):(h + 1) * HC]),
                            "sw%d" % (jj % 2), 16)
            if tail_sync is not None:
                tail_sync(e)

        swtot = [b["sw0"], b["sw1"]]
        swtarget = [0] * (NH * CH)
        for jj in range(NH * CH):
            if jj < preL:
                continue
            swtot[jj % 2] += 16
            swtarget[jj] = swtot[jj % 2]
        cnt["sw0"] = swtot[0]
        cnt["sw1"] = swtot[1]

        def emit_cast(e, jj):
            hh, ic = jj // CH, jj % CH
            p, h = hh // 2, hh % 2
            s = specs[p]
            if release[hh] > 0:
                e.wait_ge(sem["smm"], b["smm"] + release[hh])
            e.wait_ge(sem["sw%d" % (jj % 2)], swtarget[jj])
            dstsl = wbf[hh % 2][:, ic, :]
            if s["quant"]:
                e.tensor_single_scalar(mwm2[jj % 2][:, :], mwstg[jj % 2][:, :],
                                       thr_sb[:, :], OP.is_gt)
                e.tensor_single_scalar(mwstg[jj % 2][:, :], mwstg[jj % 2][:, :],
                                       nthr_sb[:, :], OP.is_lt)
                e.drain()
                e.tensor_tensor(dstsl, mwm2[jj % 2][:, :], mwstg[jj % 2][:, :],
                                OP.subtract).then_inc(sem["scv"], 1)
            elif jj % 1 == 0 and castrec[jj][0] == "V":
                e.tensor_copy(dstsl, mwstg[jj % 2][:, :]).then_inc(sem["scv"], 1)
            else:
                e.activation(dstsl, mwstg[jj % 2][:, :],
                             AF.Copy).then_inc(sem["scs"], 1)

        v_casts_at = {}
        s_casts_at = {}
        for jj in range(preL, NH * CH):
            r = release[jj // CH]
            (v_casts_at if castrec[jj][0] == "V" else s_casts_at
             ).setdefault(r, []).append(jj)

        @blk.vector
        def _(e):
            prev_evac = None
            for g in range(NG):
                for jj in v_casts_at.get(g, []):
                    emit_cast(e, jj)
                p, h, oc, tj = g_info(g)
                s = specs[p]
                if s["evac"] in ("sig_attn", "sig_y_res"):
                    t0, t1 = s["tcs"][tj]
                    ocg = h * OH + oc
                    k = signum[g]
                    if s["evac"] != prev_evac:
                        e.drain()
                        prev_evac = s["evac"]
                    e.wait_ge(sem["sev"], b["sev"] + g + 1)
                    if s["evac"] == "sig_attn":
                        inc(e.tensor_tensor(s["wkv_bf"][:, ocg, t0:t1],
                                            sigst[k % 2][:, :t1 - t0],
                                            s["wkv_bf"][:, ocg, t0:t1],
                                            OP.mult), "sv")
                    else:
                        inc(e.tensor_tensor(s["y_dst"][:, ocg, t0:t1],
                                            sigst[k % 2][:, :t1 - t0],
                                            s["wkv_bf"][:, ocg, t0:t1],
                                            OP.mult), "sv")
            for jj in v_casts_at.get(NG, []):
                emit_cast(e, jj)

        @blk.scalar
        def _(e):
            for g in range(NG):
                for mp in mov_at.get(g, []):
                    for ci in range(CH):
                        inc(e.dma_start(specs[mp]["mov"][:, ci, :],
                                        specs[mp]["mov_load"][ci, :, :]),
                            "sl", 16)
                for jj in s_casts_at.get(g, []):
                    emit_cast(e, jj)
                p, h, oc, tj = g_info(g)
                s = specs[p]
                t0, t1 = s["tcs"][tj]
                ocg = h * OH + oc
                bank = ps[g % NBK]
                e.wait_ge(sem["smm"], b["smm"] + g + 1)
                ev = s["evac"]
                if ev in ("exp_row", "copy_row"):
                    ro = rownum[(p, ocg)]
                    if tj == 0 and ro >= 2:
                        stp = "st%d" % (ro % 2)
                        e.wait_ge(sem[stp], b[stp] + 16 * (ro // 2))
                    fn = AF.Exp if ev == "exp_row" else AF.Copy
                    inc(e.activation(rowst[ro % 2][:, t0:t1],
                                     bank[:, :t1 - t0], fn), "sev")
                elif ev == "silu_g2":
                    gk = g2num[g]
                    if gk >= 2:
                        stp = "st%d" % (gk % 2)
                        e.wait_ge(sem[stp], b[stp] + 16 * (gk // 2))
                    inc(e.activation(g2st[gk % 2][:, :t1 - t0],
                                     bank[:, :t1 - t0], AF.Silu,
                                     bias=s["bias_vec"][:, ocg:ocg + 1],
                                     scale=scale_sb[:, 0:1]), "sev")
                elif ev in ("sig_attn", "sig_y_res"):
                    k = signum[g]
                    if k >= 2:
                        e.wait_ge(sem["sv"], b["sv"] + k - 1)
                    inc(e.activation(sigst[k % 2][:, :t1 - t0],
                                     bank[:, :t1 - t0], AF.Sigmoid), "sev")
                elif ev == "copy_o":
                    inc(e.activation(s["dst"][:, ocg, t0:t1],
                                     bank[:, :t1 - t0], AF.Copy), "sev")
            for jj in s_casts_at.get(NG, []):
                emit_cast(e, jj)
            if tail_scalar is not None:
                tail_scalar(e)

        @blk.gpsimd
        def _(e):
            any_store = False
            for p, s in enumerate(specs):
                if s["evac"] in ("exp_row", "copy_row"):
                    any_store = True
                    for ocg in range(CH):
                        ro = rownum[(p, ocg)]
                        e.wait_ge(sem["sev"],
                                  b["sev"] + GRP0[p] + (ocg + 1) * ntc[p])
                        inc(e.dma_start(s["store_ap"][ocg, :, :],
                                        rowst[ro % 2][:, :]),
                            "st%d" % (ro % 2), 16)
                elif s["evac"] == "silu_g2":
                    any_store = True
                    for g in range(GRP0[p], GRP0[p] + 2 * ngh[p]):
                        _p, h, oc, tj = g_info(g)
                        t0, t1 = s["tcs"][tj]
                        ocg = h * OH + oc
                        gk = g2num[g]
                        e.wait_ge(sem["sev"], b["sev"] + g + 1)
                        inc(e.dma_start(s["store_ap"][ocg, :, t0:t1],
                                        g2st[gk % 2][:, :t1 - t0]),
                            "st%d" % (gk % 2), 16)
            if any_store:
                e.wait_ge(sem["st0"], cnt["st0"])
                e.wait_ge(sem["st1"], cnt["st1"])

        @blk.tensor
        def _(e):
            for p, s in enumerate(specs):
                if s["mov_load"] is not None:
                    e.wait_ge(sem["sl"], slt[p])
                if s.get("wait_y"):
                    e.wait_ge(sem["sv"], b["sv"] + nsig)
                for h in range(2):
                    hh = 2 * p + h
                    e.wait_ge(sem["scv"], CV[hh])
                    if CS[hh] > b0scs:
                        e.wait_ge(sem["scs"], CS[hh])
                    for oc in range(OH):
                        for tj, (t0, t1) in enumerate(s["tcs"]):
                            g = GRP0[p] + h * ngh[p] + oc * ntc[p] + tj
                            bank = ps[g % NBK]
                            if g >= NBK:
                                e.wait_ge(sem["sev"], b["sev"] + g - NBK + 1)
                            for ic in range(CH):
                                mm = e.matmul(bank[:, :t1 - t0],
                                              wbf[hh % 2][:, ic,
                                                          oc * 128:(oc + 1) * 128],
                                              s["mov"][:, ic, t0:t1],
                                              start=(ic == 0),
                                              stop=(ic == CH - 1))
                            inc(mm, "smm")
        cnt["scv"] = cv
        cnt["scs"] = cs_

    def spec(wt, tcs, evac, mov, mov_load=None, quant=False, store_ap=None,
             bias_vec=None, dst=None, wkv_bf=None, y_dst=None, wait_y=False):
        return dict(wt=wt, tcs=tcs, evac=evac, mov=mov, mov_load=mov_load,
                    quant=quant, store_ap=store_ap, bias_vec=bias_vec,
                    dst=dst, wkv_bf=wkv_bf, y_dst=y_dst, wait_y=wait_y)

    def make_shared(es, tag, n_tok, want_quant, want_row, want_sig, want_g2,
                    nps=8):
        ec = es.enter_context
        sh = {"es": es}
        sh["wbf"] = [ec(nc.sbuf_tensor(tag + "wbf%d" % i, [128, CH, HC], BF16))
                     for i in range(2)]
        sh["mwstg"] = [ec(nc.sbuf_tensor(tag + "mwstg%d" % i, [128, HC], F32))
                       for i in range(2)]
        sh["mwm2"] = ([ec(nc.sbuf_tensor(tag + "mwm2%d" % i, [128, HC], F32))
                       for i in range(2)] if want_quant else None)
        sh["rowst"] = ([ec(nc.sbuf_tensor(tag + "row%d" % i, [128, n_tok], F32))
                        for i in range(2)] if want_row else None)
        sh["sigst"] = ([ec(nc.sbuf_tensor(tag + "sig%d" % i, [128, 512], F32))
                        for i in range(2)] if want_sig else None)
        sh["g2st"] = ([ec(nc.sbuf_tensor(tag + "g2s%d" % i, [128, 512], BF16))
                       for i in range(2)] if want_g2 else None)
        sh["ps"] = [ec(nc.psum_tensor(tag + "ps%d" % i, [128, 512], F32))
                    for i in range(nps)]
        return sh

    es1 = contextlib.ExitStack()
    xn_sb = es1.enter_context(nc.sbuf_tensor("xn_sb", [128, CH, T_OWN], BF16))

    es_pre = contextlib.ExitStack()
    g1wbf0 = es_pre.enter_context(
        nc.sbuf_tensor("g1wbf0", [128, CH, HC], BF16))
    g1mwstg = [es_pre.enter_context(
        nc.sbuf_tensor("g1mwstg%d" % i, [128, HC], F32)) for i in range(2)]
    g1_pre_recs = []

    with contextlib.ExitStack() as es:
        ec = es.enter_context
        xh16 = ec(nc.sbuf_tensor("xh16", [128, CH, T_STG], F16))
        xstg = [ec(nc.sbuf_tensor("xstg%d" % i, [128, T_STG], F32))
                for i in range(2)]
        xq16 = [ec(nc.sbuf_tensor("xq16_%d" % i, [128, T_STG], F16))
                for i in range(2)]
        stat = ec(nc.sbuf_tensor("stat", [1, T_STG], F32))
        stat2 = ec(nc.sbuf_tensor("stat2", [1, T_STG], F32))
        stat3 = ec(nc.sbuf_tensor("stat3", [1, T_STG], F32))
        rb_sb = ec(nc.sbuf_tensor("rb_sb", [128, T_STG], F32))
        m2b_sb = ec(nc.sbuf_tensor("m2b_sb", [128, T_STG], F32))
        xnf = ec(nc.sbuf_tensor("xnf", [128, T_STG], F32))
        xxf = ec(nc.sbuf_tensor("xxf", [128, T_ALL], F32))
        xn_stg = [ec(nc.sbuf_tensor("xn_stg%d" % i, [128, T_OWN], BF16))
                  for i in range(2)]
        ik_stg = [ec(nc.sbuf_tensor("ik_stg%d" % i, [128, T_ALL], BF16))
                  for i in range(2)]
        iv_stg = [ec(nc.sbuf_tensor("iv_stg%d" % i, [128, T_ALL], BF16))
                  for i in range(2)]
        ir_stg = [ec(nc.sbuf_tensor("ir_stg%d" % i, [128, T_OWN], BF16))
                  for i in range(2)]
        ig_stg = [ec(nc.sbuf_tensor("ig_stg%d" % i, [128, T_OWN], BF16))
                  for i in range(2)]

        with contextlib.ExitStack() as esA:
            ecA = esA.enter_context
            ps1 = [ecA(nc.psum_tensor("ps1_%d" % j, [1, 512], F32))
                   for j in range(len(TC_STG))]
            ps2 = [ecA(nc.psum_tensor("ps2_%d" % j, [1, 512], F32))
                   for j in range(len(TC_STG))]
            blk = esA.enter_context(nc.Block())
            b = snap()

            @blk.sync
            def _(e):
                for ci in range(CH):
                    if ci >= 2:
                        e.wait_ge(sem["ss"], b["ss"] + ci - 1)
                    inc(e.dma_start(xstg[ci % 2][:, :],
                                    x_d[ci * 128:(ci + 1) * 128, :]),
                        "sw%d" % (ci % 2), 16)

            @blk.scalar
            def _(e):
                for ci in range(CH):
                    swp = "sw%d" % (ci % 2)
                    e.wait_ge(sem[swp], b[swp] + 16 * (ci // 2 + 1))
                    if ci >= 2:
                        e.wait_ge(sem["smm"], b["smm"] + ci - 1)
                    e.activation(xh16[:, ci, :], xstg[ci % 2][:, :], AF.Copy)
                    inc(e.activation(xq16[ci % 2][:, :], xstg[ci % 2][:, :],
                                     AF.Square), "ss")
                e.wait_ge(sem["sv"], b["sv"] + 1)
                e.activation(stat[:, :], stat[:, :], AF.Ln)
                e.drain()
                inc(e.activation(stat[:, :], stat[:, :], AF.Exp, scale=-0.5),
                    "ss")

            @blk.tensor
            def _(e):
                for ci in range(CH):
                    e.wait_ge(sem["ss"], b["ss"] + ci + 1)
                    for j, (t0, t1) in enumerate(TC_STG):
                        e.matmul(ps1[j][:, :t1 - t0], ones16[:, :],
                                 xh16[:, ci, t0:t1],
                                 start=(ci == 0), stop=(ci == CH - 1))
                        mm = e.matmul(ps2[j][:, :t1 - t0], ones16[:, :],
                                      xq16[ci % 2][:, t0:t1],
                                      start=(ci == 0), stop=(ci == CH - 1))
                    inc(mm, "smm")

            @blk.vector
            def _(e):
                e.wait_ge(sem["smm"], b["smm"] + CH)
                for j, (t0, t1) in enumerate(TC_STG):
                    e.tensor_scalar(stat2[:, t0:t1], ps1[j][:, :t1 - t0],
                                    1.0 / C, None, OP.mult)
                    e.tensor_scalar(stat[:, t0:t1], ps2[j][:, :t1 - t0],
                                    1.0 / C, None, OP.mult)
                e.drain()
                e.tensor_tensor(stat3[:, :], stat2[:, :], stat2[:, :], OP.mult)
                e.drain()
                e.tensor_tensor(stat[:, :], stat[:, :], stat3[:, :], OP.subtract)
                e.drain()
                inc(e.tensor_scalar(stat[:, :], stat[:, :], LN_EPS, None,
                                    OP.add), "sv")
                e.wait_ge(sem["ss"], b["ss"] + CH + 1)
                e.drain()
                inc(e.tensor_tensor(stat2[:, :], stat2[:, :], stat[:, :],
                                    OP.mult), "sv")

        with contextlib.ExitStack() as esB:
            ecB = esB.enter_context
            pb1 = ecB(nc.psum_tensor("pb1", [128, 512], F32))
            pb2 = ecB(nc.psum_tensor("pb2", [128, 512], F32))
            SCW = min(512, HC)
            ps_sc = ecB(nc.psum_tensor("ps_sc", [1, SCW], F32))
            ps_bc = ecB(nc.psum_tensor("ps_bc", [128, 4], F32))
            habs = [ecB(nc.sbuf_tensor("fs_a%d" % i, [128, HC], F16))
                    for i in range(2)]
            sctile = ecB(nc.sbuf_tensor("sctile", [1, SCW], F32))
            sctiny = ecB(nc.sbuf_tensor("sctiny", [1, 4], F32))
            blk = esB.enter_context(nc.Block())
            b = snap()
            QC = _tchunks(HC)
            nbc = len(TC_STG)
            NSL = 2 * CH
            NSLK = NSL + CH

            @blk.tensor
            def _(e):
                for j, (t0, t1) in enumerate(TC_STG):
                    if j >= 1:
                        e.wait_ge(sem["sev"], b["sev"] + j)
                    e.matmul(pb1[:, :t1 - t0], onesK1[:, :], stat[:, t0:t1],
                             start=True, stop=True)
                    inc(e.matmul(pb2[:, :t1 - t0], onesK1[:, :],
                                 stat2[:, t0:t1], start=True, stop=True), "smm")
                for sl_ in range(NSL):
                    e.wait_ge(sem["scv"], b["scv"] + sl_ + 1)
                    for j, (q0, q1) in enumerate(QC):
                        mm = e.matmul(ps_sc[:, :q1 - q0], ones16[:, :],
                                      habs[sl_ % 2][:, q0:q1],
                                      start=(sl_ == 0 and j == 0),
                                      stop=(sl_ == NSL - 1 and j == len(QC) - 1))
                    inc(mm, "smm")
                e.wait_ge(sem["srd"], b["srd"] + 1)
                e.matmul(ps_bc[:, 0:1], onesK1[:, :], sctiny[:, 0:1],
                         start=True, stop=True)
                e.matmul(ps_bc[:, 1:2], onesK1[:, :], sctiny[:, 1:2],
                         start=True, stop=True)
                inc(e.matmul(ps_bc[:, 2:3], onesK1[:, :], sctiny[:, 2:3],
                             start=True, stop=True), "smm")

            @blk.sync
            def _(e):
                for sl_ in range(NSLK):
                    if sl_ >= 2:
                        if sl_ - 2 < NSL:
                            e.wait_ge(sem["scv"], b["scv"] + sl_ - 1)
                        else:
                            e.wait_ge(sem["scs"],
                                      b["scs"] + (sl_ - 2 - NSL) + 1)
                    if sl_ < NSL:
                        ic, h = sl_ // 2, sl_ % 2
                        src = w_d["wf"][ic * 128:(ic + 1) * 128,
                                        h * HC:(h + 1) * HC]
                    else:
                        ic = sl_ - NSL
                        src = w_d["wk"][ic * 128:(ic + 1) * 128, 0:HC]
                    inc(e.dma_start(g1mwstg[sl_ % 2][:, :], src),
                        "sw%d" % (sl_ % 2), 16)

            @blk.scalar
            def _(e):
                for j, (t0, t1) in enumerate(TC_STG):
                    e.wait_ge(sem["smm"], b["smm"] + j + 1)
                    e.activation(rb_sb[:, t0:t1], pb1[:, :t1 - t0], AF.Copy)
                    inc(e.activation(m2b_sb[:, t0:t1], pb2[:, :t1 - t0],
                                     AF.Copy), "sev")
                for sl_ in range(NSLK):
                    swp = "sw%d" % (sl_ % 2)
                    e.wait_ge(sem[swp], b[swp] + 16 * (sl_ // 2 + 1))
                    if sl_ < NSL:
                        if sl_ >= 2:
                            e.wait_ge(sem["smm"], b["smm"] + nbc + sl_ - 1)
                        inc(e.activation(habs[sl_ % 2][:, :],
                                         g1mwstg[sl_ % 2][:, :], AF.Abs),
                            "scv")
                    else:
                        ic = sl_ - NSL
                        inc(e.activation(g1wbf0[:, ic, :],
                                         g1mwstg[sl_ % 2][:, :], AF.Copy),
                            "scs")
                        g1_pre_recs.append(("S", "scs", cnt["scs"]))
                e.wait_ge(sem["smm"], b["smm"] + nbc + NSL + 1)
                e.activation(scale_sb[:, :], ps_bc[:, 0:1], AF.Copy)
                e.activation(thr_sb[:, :], ps_bc[:, 1:2], AF.Copy)
                e.activation(nthr_sb[:, :], ps_bc[:, 2:3], AF.Copy)

            @blk.vector
            def _(e):
                e.wait_ge(sem["sev"], b["sev"] + len(TC_STG))
                for ci in range(CH):
                    if ci >= 2:
                        stp = "st%d" % (ci % 2)
                        e.wait_ge(sem[stp], b[stp] + 80 * (ci // 2))
                    s = ci % 2
                    e.drain()
                    e.tensor_tensor(xnf[:, :], xh16[:, ci, :], rb_sb[:, :],
                                    OP.mult)
                    e.drain()
                    e.tensor_tensor(xnf[:, :], xnf[:, :], m2b_sb[:, :],
                                    OP.subtract)
                    e.drain()
                    e.tensor_tensor(xxf[:, :], xnf[:, 0:T_ALL], xnf[:, 1:T_STG],
                                    OP.subtract)
                    e.drain()
                    e.scalar_tensor_tensor(ik_stg[s][:, :], xxf[:, :],
                                           vec_sb["tmk"][:, ci:ci + 1],
                                           xnf[:, 1:T_STG], OP.mult, OP.add)
                    e.scalar_tensor_tensor(iv_stg[s][:, :], xxf[:, :],
                                           vec_sb["tmv"][:, ci:ci + 1],
                                           xnf[:, 1:T_STG], OP.mult, OP.add)
                    e.scalar_tensor_tensor(ir_stg[s][:, :], xxf[:, LANE:],
                                           vec_sb["tmr"][:, ci:ci + 1],
                                           xnf[:, 1 + LANE:T_STG], OP.mult,
                                           OP.add)
                    e.scalar_tensor_tensor(ig_stg[s][:, :], xxf[:, LANE:],
                                           vec_sb["tmg"][:, ci:ci + 1],
                                           xnf[:, 1 + LANE:T_STG], OP.mult,
                                           OP.add)
                    inc(e.tensor_copy(xn_stg[s][:, :], xnf[:, 1 + LANE:T_STG]),
                        "sv")
                e.wait_ge(sem["smm"], b["smm"] + nbc + NSL)
                e.tensor_copy(sctile[:, :], ps_sc[:, :])
                e.drain()
                e.reduce_sum(sctiny[:, 3:4], sctile[:, :], axis=AX.X)
                e.drain()
                e.tensor_scalar(sctiny[:, 0:1], sctiny[:, 3:4], 1.0 / (C * C),
                                None, OP.mult)
                e.drain()
                e.tensor_scalar(sctiny[:, 1:2], sctiny[:, 0:1], FFN_EPS, 0.5,
                                OP.add, OP.mult)
                e.drain()
                inc(e.tensor_scalar(sctiny[:, 2:3], sctiny[:, 1:2], -1.0, None,
                                    OP.mult), "srd")

            @blk.gpsimd
            def _(e):
                for ci in range(CH):
                    e.wait_ge(sem["sv"], b["sv"] + ci + 1)
                    s = ci % 2
                    stp = "st%d" % s
                    inc(e.dma_start(sxk[ci, :, :], ik_stg[s][:, :]), stp, 16)
                    inc(e.dma_start(sxv[ci, :, :], iv_stg[s][:, :]), stp, 16)
                    inc(e.dma_start(sxr[ci, :, :], ir_stg[s][:, :]), stp, 16)
                    inc(e.dma_start(sxg[ci, :, :], ig_stg[s][:, :]), stp, 16)
                    inc(e.dma_start(xn_sb[:, ci, :], xn_stg[s][:, :]), stp, 16)
                e.wait_ge(sem["st0"], cnt["st0"])
                e.wait_ge(sem["st1"], cnt["st1"])

    with contextlib.ExitStack() as es:
        ec = es.enter_context
        mov_kv = ec(nc.sbuf_tensor("mov_kv", [128, CH, T_ALL], BF16))
        sh = {"es": es,
              "wbf": [g1wbf0,
                      ec(nc.sbuf_tensor("g1wbf1", [128, CH, HC], BF16))],
              "mwstg": g1mwstg,
              "mwm2": [ec(nc.sbuf_tensor("g1mwm2%d" % i, [128, HC], F32))
                       for i in range(2)],
              "rowst": [ec(nc.sbuf_tensor("g1row%d" % i, [128, T_ALL], F32))
                        for i in range(2)],
              "sigst": None,
              "g2st": [ec(nc.sbuf_tensor("g1g2s%d" % i, [128, 512], BF16))
                       for i in range(2)],
              "ps": [ec(nc.psum_tensor("g1ps%d" % i, [128, 512], F32))
                     for i in range(8)]}
        mm_group([
            spec(w_d["wk"], TC_ALL, "exp_row", mov_kv, mov_load=sxk,
                 store_ap=sek),
            spec(w_d["wv"], TC_ALL, "copy_row", mov_kv, mov_load=sxv,
                 store_ap=svd),
            spec(w_d["wf"], TC_OWN, "silu_g2", xn_sb, quant=True,
                 bias_vec=vec_sb["bfv"], store_ap=sg2),
        ], sh, pre=dict(preL=CH, recs=g1_pre_recs))
    es_pre.close()
    es1.close()

    es_o = contextlib.ExitStack()
    sr_sb = es_o.enter_context(
        nc.sbuf_tensor("sr_sb", [128, CH, T_OWN], BF16, side="right"))

    with contextlib.ExitStack() as es:
        ec = es.enter_context
        wkv_sb = ec(nc.sbuf_tensor("wkv_sb", [128, CH, T_OWN], BF16))
        mov_rg = ec(nc.sbuf_tensor("mov_rg", [128, CH, T_OWN], BF16))
        SW = min(512, C)
        NSLAB = C // SW
        OSL = SW // 128
        wbf = [ec(nc.sbuf_tensor("rgo_wbf%d" % i, [128, CH, SW], BF16))
               for i in range(2)]
        mwstg = [ec(nc.sbuf_tensor("rgo_mw%d" % i, [128, SW], F32))
                 for i in range(2)]
        NSG = 6
        sgst = [ec(nc.sbuf_tensor("rgo_sg%d" % i, [128, 512], BF16))
                for i in range(NSG)]
        ekb = ec(nc.sbuf_tensor("r_ekb", [128, CH, LANE], F32))
        vkb = ec(nc.sbuf_tensor("r_vkb", [128, CH, LANE], F32))
        za = ec(nc.sbuf_tensor("za", [128, CH, 1 + LANE], F32))
        zb = ec(nc.sbuf_tensor("zb", [128, CH, 1 + LANE], F32))
        carry_a = ec(nc.sbuf_tensor("carry_a", [128, CH], F32))
        carry_b = ec(nc.sbuf_tensor("carry_b", [128, CH], F32))
        ps = [ec(nc.psum_tensor("rgo_ps%d" % i, [128, 512], F32))
              for i in range(8)]
        blk = es.enter_context(nc.Block())
        b = snap()
        NBK = len(ps)
        cnt_scv0 = cnt["scv"]

        ntc = len(TC_OWN)
        NGp = NSLAB * OSL * ntc
        NR, NRG, NG = NGp, 2 * NGp, 3 * NGp
        phs = [dict(wt=w_d["wr"], mov=mov_rg, mov_load=sxr),
               dict(wt=w_d["wg"], mov=mov_rg, mov_load=sxg),
               dict(wt=w_d["wo"], mov=wkv_sb, mov_load=None)]

        def g_info(g):
            p, r = g // NGp, g % NGp
            return p, r // (OSL * ntc), (r // ntc) % OSL, r % ntc

        NSS = 3 * NSLAB
        GENDs = [(ss + 1) * OSL * ntc for ss in range(NSS)]
        release = [0 if ss < 2 else GENDs[ss - 2] for ss in range(NSS)]

        castrec = []
        cv, cs_ = cnt["scv"], cnt["scs"]
        CVs, CSs = [0] * NSS, [0] * NSS
        for ss in range(NSS):
            for ic in range(CH):
                cs_ += 1
                castrec.append(("S", "scs", cs_))
            CVs[ss], CSs[ss] = cv, cs_

        PC, SR, PQ = {0: 0}, {}, {}
        for l in range(1, NL + 1):
            PC[l] = 1 if l == 1 else PQ[l - 1]
            SR[l] = min(2 * l, NG - 1)
            PQ[l] = min(max(PC[l] + 1, SR[l] + 1), NG)
        PSR = {}
        prevpos = 0
        for l in range(1, NL + 1):
            PSR[l] = min(max(NR + l, PQ[l] + 1, prevpos + 1), NG)
            prevpos = PSR[l]

        v_lane_at = {}
        v_lane_at.setdefault(PC[0], []).append(("core", 0))
        for l in range(1, NL + 1):
            v_lane_at.setdefault(PC[l], []).append(("core", l))
            v_lane_at.setdefault(PQ[l], []).append(("q", l))
            v_lane_at.setdefault(PSR[l], []).append(("sr", l))
        gpos = 0
        for k in range(NGp):
            _p, sl_g, oc_g, tj_g = g_info(NR + k)
            t1k = TC_OWN[tj_g][1]
            Lk = (t1k + LANE - 1) // LANE
            gpos = min(max(gpos, NR + k + 1, PSR[Lk]), NG)
            v_lane_at.setdefault(gpos, []).append(("gmul", k))

        svord = {}
        i = 0
        for g in range(NG + 1):
            for kind, l in v_lane_at.get(g, []):
                if kind != "core" or l == 0:
                    i += 1
                    svord[(kind, l)] = i
        NSV = i

        swtot = [b["sw0"], b["sw1"]]
        swtarget = [0] * (NSS * CH)
        for jj in range(NSS * CH):
            swtot[jj % 2] += 16
            swtarget[jj] = swtot[jj % 2]
        cnt["sw0"], cnt["sw1"] = swtot[0], swtot[1]

        sttot = [b["st0"], b["st1"]]
        st_lane = {}
        for l in range(NL + 1):
            sttot[l % 2] += 32
            st_lane[l] = sttot[l % 2]
        cnt["st0"], cnt["st1"] = sttot[0], sttot[1]

        def emit_cast(e, jj):
            ss, ic = jj // CH, jj % CH
            if release[ss] > 0:
                e.wait_ge(sem["smm"], b["smm"] + release[ss])
            e.wait_ge(sem["sw%d" % (jj % 2)], swtarget[jj])
            if castrec[jj][0] == "V":
                e.tensor_copy(wbf[ss % 2][:, ic, :],
                              mwstg[jj % 2][:, :]).then_inc(sem["scv"], 1)
            else:
                e.activation(wbf[ss % 2][:, ic, :], mwstg[jj % 2][:, :],
                             AF.Copy).then_inc(sem["scs"], 1)

        v_casts_at, s_casts_at = {}, {}
        for jj in range(NSS * CH):
            (v_casts_at if castrec[jj][0] == "V" else s_casts_at
             ).setdefault(release[jj // CH], []).append(jj)

        @blk.sync
        def _(e):
            for jj in range(NSS * CH):
                ss, ic = jj // CH, jj % CH
                p, sl = ss // NSLAB, ss % NSLAB
                if jj >= 2:
                    eng, sname, tgt = castrec[jj - 2]
                    e.wait_ge(sem[sname], tgt)
                e.dma_start(mwstg[jj % 2][:, :],
                            phs[p]["wt"][ic * 128:(ic + 1) * 128,
                                         sl * SW:(sl + 1) * SW]
                            ).then_inc(sem["sw%d" % (jj % 2)], 16)

        @blk.gpsimd
        def _(e):
            for l in range(NL + 1):
                if l >= 1:
                    key = ("q", l - 1) if l - 1 >= 1 else ("core", 0)
                    e.wait_ge(sem["sv"], b["sv"] + svord[key])
                sl_ = slice(l * LANE, (l + 1) * LANE)
                stp = "st%d" % (l % 2)
                e.dma_start(ekb[:, :, :],
                            sek[:, :, sl_].transpose([1, 0, 2])
                            ).then_inc(sem[stp], 16)
                e.dma_start(vkb[:, :, :],
                            svd[:, :, sl_].transpose([1, 0, 2])
                            ).then_inc(sem[stp], 16)

        def lane_core(e, l):
            stp = "st%d" % (l % 2)
            e.wait_ge(sem[stp], st_lane[l])
            e.drain()
            e.tensor_tensor(vkb[:, :, :], ekb[:, :, :],
                            vkb[:, :, :], OP.mult)
            e.drain()
            e.tensor_copy(za[:, :, 0], carry_a[:, :])
            e.tensor_copy(zb[:, :, 0], carry_b[:, :])
            for ci in range(CH):
                e.tensor_tensor_scan(za[:, ci, 1:1 + LANE], w_rep[:, ci, :],
                                     vkb[:, ci, :], carry_a[:, ci:ci + 1],
                                     OP.mult, OP.add)
            for ci in range(CH):
                e.tensor_tensor_scan(zb[:, ci, 1:1 + LANE], w_rep[:, ci, :],
                                     ekb[:, ci, :], carry_b[:, ci:ci + 1],
                                     OP.mult, OP.add)
            e.drain()
            if l == 0:
                e.tensor_scalar(carry_a[:, :], za[:, :, LANE],
                                cs_sb[:, 0:1], None, OP.mult)
                inc(e.tensor_scalar(carry_b[:, :], zb[:, :, LANE],
                                    cs_sb[:, 0:1], None, OP.mult), "sv")
            else:
                e.tensor_copy(carry_a[:, :], za[:, :, LANE])
                e.tensor_copy(carry_b[:, :], zb[:, :, LANE])
                for ci in range(CH):
                    e.scalar_tensor_tensor(vkb[:, ci, :], vkb[:, ci, :],
                                           eu_sb[:, ci:ci + 1],
                                           za[:, ci, 0:LANE], OP.mult, OP.add)
                for ci in range(CH):
                    last = e.scalar_tensor_tensor(
                        ekb[:, ci, :], ekb[:, ci, :],
                        eu_sb[:, ci:ci + 1], zb[:, ci, 0:LANE],
                        OP.mult, OP.add)
                inc(last, "sden")

        @blk.vector
        def _(e):
            e.memset(carry_a[:, :], 0.0)
            e.memset(carry_b[:, :], 0.0)
            e.drain()
            for g in range(NG + 1):
                for jj in v_casts_at.get(g, []):
                    emit_cast(e, jj)
                for kind, l in v_lane_at.get(g, []):
                    if kind == "gmul":
                        _p, sl_g, oc_g, tj_g = g_info(NR + l)
                        t0k, t1k = TC_OWN[tj_g]
                        ocg_k = sl_g * OSL + oc_g
                        e.wait_ge(sem["sev"], b["sev"] + NR + l + 1)
                        e.drain()
                        inc(e.tensor_tensor(wkv_sb[:, ocg_k, t0k:t1k],
                                            wkv_sb[:, ocg_k, t0k:t1k],
                                            sgst[l % NSG][:, :t1k - t0k],
                                            OP.mult), "sv")
                        continue
                    sl_ = slice((l - 1) * LANE, l * LANE)
                    if kind == "core":
                        lane_core(e, l)
                    elif kind == "q":
                        e.wait_ge(sem["srd"], b["srd"] + l)
                        e.drain()
                        inc(e.tensor_tensor(wkv_sb[:, :, sl_],
                                            vkb[:, :, :], ekb[:, :, :],
                                            OP.mult), "sv")
                    elif kind == "sr":
                        e.wait_ge(sem["sev"], b["sev"] + NR)
                        e.drain()
                        inc(e.tensor_tensor(wkv_sb[:, :, sl_],
                                            wkv_sb[:, :, sl_],
                                            sr_sb[:, :, sl_], OP.mult), "sv")

        @blk.scalar
        def _(e):
            recip_at = {}
            for l in range(1, NL + 1):
                recip_at.setdefault(SR[l], []).append(l)
            for g in range(NG):
                p, sl, oc, tj = g_info(g)
                if g == 0:
                    for ci in range(CH):
                        inc(e.dma_start(mov_rg[:, ci, :], sxr[ci, :, :]),
                            "sl", 16)
                if g == NR:
                    for ci in range(CH):
                        inc(e.dma_start(mov_rg[:, ci, :], sxg[ci, :, :]),
                            "sl", 16)
                for jj in s_casts_at.get(g, []):
                    emit_cast(e, jj)
                for l in recip_at.get(g, []):
                    e.wait_ge(sem["sden"], b["sden"] + l)
                    e.activation(ekb[:, :, :], ekb[:, :, :], AF.Ln)
                    e.drain()
                    inc(e.activation(ekb[:, :, :], ekb[:, :, :],
                                     AF.Exp, scale=-1.0), "srd")
                t0, t1 = TC_OWN[tj]
                ocg = sl * OSL + oc
                bank = ps[g % NBK]
                e.wait_ge(sem["smm"], b["smm"] + g + 1)
                if p == 0:
                    inc(e.activation(sr_sb[:, ocg, t0:t1], bank[:, :t1 - t0],
                                     AF.Sigmoid), "sev")
                elif p == 1:
                    k = g - NR
                    if k >= NSG:
                        e.wait_ge(sem["sv"],
                                  b["sv"] + svord[("gmul", k - NSG)])
                    inc(e.activation(sgst[k % NSG][:, :t1 - t0],
                                     bank[:, :t1 - t0], AF.Sigmoid), "sev")
                else:
                    inc(e.activation(sr_sb[:, ocg, t0:t1], bank[:, :t1 - t0],
                                     AF.Copy), "sev")
            for jj in s_casts_at.get(NG, []):
                emit_cast(e, jj)

        @blk.tensor
        def _(e):
            for p in range(3):
                if phs[p]["mov_load"] is not None:
                    e.wait_ge(sem["sl"], b["sl"] + 16 * CH * (p + 1))
                if p == 2:
                    e.wait_ge(sem["sv"], b["sv"] + NSV)
                for sl in range(NSLAB):
                    ss = p * NSLAB + sl
                    if CVs[ss] > cnt_scv0:
                        e.wait_ge(sem["scv"], CVs[ss])
                    e.wait_ge(sem["scs"], CSs[ss])
                    for oc in range(OSL):
                        for tj, (t0, t1) in enumerate(TC_OWN):
                            g = p * NGp + sl * OSL * ntc + oc * ntc + tj
                            bank = ps[g % NBK]
                            if g >= NBK:
                                e.wait_ge(sem["sev"], b["sev"] + g - NBK + 1)
                            for ic in range(CH):
                                mm = e.matmul(bank[:, :t1 - t0],
                                              wbf[ss % 2][:, ic,
                                                          oc * 128:(oc + 1) * 128],
                                              phs[p]["mov"][:, ic, t0:t1],
                                              start=(ic == 0),
                                              stop=(ic == CH - 1))
                            inc(mm, "smm")
        cnt["scv"], cnt["scs"] = cv, cs_

    o_sb = sr_sb

    with contextlib.ExitStack() as es:
        ec = es.enter_context
        oh16 = [ec(nc.sbuf_tensor("oh16_%d" % i, [128, T_OWN], F16))
                for i in range(2)]
        oq16 = [ec(nc.sbuf_tensor("oq16_%d" % i, [128, T_OWN], F16))
                for i in range(2)]
        stv = ec(nc.sbuf_tensor("stv", [1, T_OWN], F32))
        stm = ec(nc.sbuf_tensor("stm", [1, T_OWN], F32))
        stq = ec(nc.sbuf_tensor("stq", [1, T_OWN], F32))
        rb2 = ec(nc.sbuf_tensor("rb2", [128, T_OWN], F32))
        m2b2 = ec(nc.sbuf_tensor("m2b2", [128, T_OWN], F32))
        ost = [ec(nc.sbuf_tensor("ost%d" % i, [128, T_OWN], F32))
               for i in range(2)]

        with contextlib.ExitStack() as esA:
            ecA = esA.enter_context
            ps1 = [ecA(nc.psum_tensor("qs1_%d" % j, [1, 512], F32))
                   for j in range(len(TC_OWN))]
            ps2 = [ecA(nc.psum_tensor("qs2_%d" % j, [1, 512], F32))
                   for j in range(len(TC_OWN))]
            blk = esA.enter_context(nc.Block())
            b = snap()

            @blk.scalar
            def _(e):
                for ci in range(CH):
                    if ci >= 2:
                        e.wait_ge(sem["smm"], b["smm"] + ci - 1)
                    e.activation(oh16[ci % 2][:, :], o_sb[:, ci, :], AF.Copy)
                    inc(e.activation(oq16[ci % 2][:, :], o_sb[:, ci, :],
                                     AF.Square), "ss")
                e.wait_ge(sem["sv"], b["sv"] + 1)
                e.activation(stv[:, :], stv[:, :], AF.Ln)
                e.drain()
                inc(e.activation(stv[:, :], stv[:, :], AF.Exp, scale=-0.5),
                    "ss")

            @blk.tensor
            def _(e):
                for ci in range(CH):
                    e.wait_ge(sem["ss"], b["ss"] + ci + 1)
                    for j, (t0, t1) in enumerate(TC_OWN):
                        e.matmul(ps1[j][:, :t1 - t0], ones16[:, :],
                                 oh16[ci % 2][:, t0:t1],
                                 start=(ci == 0), stop=(ci == CH - 1))
                        mm = e.matmul(ps2[j][:, :t1 - t0], ones16[:, :],
                                      oq16[ci % 2][:, t0:t1],
                                      start=(ci == 0), stop=(ci == CH - 1))
                    inc(mm, "smm")

            @blk.vector
            def _(e):
                e.wait_ge(sem["smm"], b["smm"] + CH)
                for j, (t0, t1) in enumerate(TC_OWN):
                    e.tensor_scalar(stm[:, t0:t1], ps1[j][:, :t1 - t0],
                                    1.0 / C, None, OP.mult)
                    e.tensor_scalar(stv[:, t0:t1], ps2[j][:, :t1 - t0],
                                    1.0 / C, None, OP.mult)
                e.drain()
                e.tensor_tensor(stq[:, :], stm[:, :], stm[:, :], OP.mult)
                e.drain()
                e.tensor_tensor(stv[:, :], stv[:, :], stq[:, :], OP.subtract)
                e.drain()
                inc(e.tensor_scalar(stv[:, :], stv[:, :], LN_EPS, None,
                                    OP.add), "sv")
                e.wait_ge(sem["ss"], b["ss"] + CH + 1)
                e.drain()
                inc(e.tensor_tensor(stm[:, :], stm[:, :], stv[:, :], OP.mult),
                    "sv")

        with contextlib.ExitStack() as esB:
            ecB = esB.enter_context
            pb1 = ecB(nc.psum_tensor("qb1", [128, 512], F32))
            pb2 = ecB(nc.psum_tensor("qb2", [128, 512], F32))
            sstg = [ecB(nc.sbuf_tensor("sstg%d" % i, [128, T_OWN], F32))
                    for i in range(2)]
            xst = [ecB(nc.sbuf_tensor("xst%d" % i, [128, T_OWN], F32))
                   for i in range(2)]
            g2ld = [ecB(nc.sbuf_tensor("g2ld%d" % i, [128, T_OWN], BF16))
                    for i in range(2)]
            outst = [ecB(nc.sbuf_tensor("outst%d" % i, [128, T_OWN], F32))
                     for i in range(2)]
            blk = esB.enter_context(nc.Block())
            b = snap()
            ntc2 = len(TC_OWN)

            @blk.tensor
            def _(e):
                for j, (t0, t1) in enumerate(TC_OWN):
                    if j >= 1:
                        e.wait_ge(sem["sev"], b["sev"] + j)
                    e.matmul(pb1[:, :t1 - t0], onesK1[:, :], stv[:, t0:t1],
                             start=True, stop=True)
                    inc(e.matmul(pb2[:, :t1 - t0], onesK1[:, :], stm[:, t0:t1],
                                 start=True, stop=True), "smm")

            @blk.sync
            def _(e):
                for ci in range(CH):
                    if ci >= 2:
                        e.wait_ge(sem["sden"], b["sden"] + ci - 1)
                    inc(e.dma_start(xst[ci % 2][:, :],
                                    x_d[ci * 128:(ci + 1) * 128,
                                        1 + LANE:T_STG]),
                        "sw%d" % (ci % 2), 16)
                    inc(e.dma_start(g2ld[ci % 2][:, :], sg2[ci, :, :]),
                        "sw%d" % (ci % 2), 16)

            @blk.scalar
            def _(e):
                for j, (t0, t1) in enumerate(TC_OWN):
                    e.wait_ge(sem["smm"], b["smm"] + j + 1)
                    e.activation(rb2[:, t0:t1], pb1[:, :t1 - t0], AF.Copy)
                    inc(e.activation(m2b2[:, t0:t1], pb2[:, :t1 - t0],
                                     AF.Copy), "sev")
                for ci in range(CH):
                    e.wait_ge(sem["sv"], b["sv"] + ci + 1)
                    if ci >= 2:
                        e.wait_ge(sem["sden"], b["sden"] + ci - 1)
                    inc(e.activation(sstg[ci % 2][:, :], ost[ci % 2][:, :],
                                     AF.Identity,
                                     bias=vec_sb["lnb"][:, ci:ci + 1],
                                     scale=vec_sb["lng"][:, ci:ci + 1]), "ss")

            def final_part(e, ci):
                swp = "sw%d" % (ci % 2)
                e.wait_ge(sem["ss"], b["ss"] + ci + 1)
                e.wait_ge(sem[swp], b[swp] + 32 * (ci // 2 + 1))
                if ci >= 2:
                    stp = "st%d" % (ci % 2)
                    e.wait_ge(sem[stp], b[stp] + 16 * (ci // 2))
                e.drain()
                e.tensor_tensor(outst[ci % 2][:, :], sstg[ci % 2][:, :],
                                g2ld[ci % 2][:, :], OP.mult)
                e.drain()
                inc(e.tensor_tensor(outst[ci % 2][:, :], outst[ci % 2][:, :],
                                    xst[ci % 2][:, :], OP.add), "sden")

            @blk.vector
            def _(e):
                e.wait_ge(sem["sev"], b["sev"] + ntc2)
                for ci in range(CH):
                    if ci >= 2:
                        e.wait_ge(sem["ss"], b["ss"] + ci - 1)
                    e.drain()
                    e.tensor_tensor(ost[ci % 2][:, :], o_sb[:, ci, :],
                                    rb2[:, :], OP.mult)
                    e.drain()
                    inc(e.tensor_tensor(ost[ci % 2][:, :], ost[ci % 2][:, :],
                                        m2b2[:, :], OP.subtract), "sv")
                    if ci >= 1:
                        final_part(e, ci - 1)
                final_part(e, CH - 1)

            @blk.gpsimd
            def _(e):
                for ci in range(CH):
                    e.wait_ge(sem["sden"], b["sden"] + ci + 1)
                    inc(e.dma_start(y_d[ci, :, :], outst[ci % 2][:, :]),
                        "st%d" % (ci % 2), 16)
                e.wait_ge(sem["st0"], cnt["st0"])
                e.wait_ge(sem["st1"], cnt["st1"])

    es_o.close()
    es0.close()
    return nc



def stage_inputs(inputs, B, T, NCHUNK, C, T_OWN, LANE):
    CH = C // 128
    T_ALL = LANE + T_OWN
    T_STG = T_ALL + 1
    x = np.ascontiguousarray(np.asarray(inputs["L0_attn_x"], np.float32))

    wmap = {"wk": "L0_attn_Wk", "wv": "L0_attn_Wv", "wr": "L0_attn_Wr",
            "wg": "L0_attn_Wg", "wo": "L0_attn_Wo", "wf": "L0_ffn_weight"}
    vmap = {"tmk": "L0_attn_tm_k", "tmv": "L0_attn_tm_v", "tmr": "L0_attn_tm_r",
            "tmg": "L0_attn_tm_g", "td": "L0_attn_time_decay",
            "tf": "L0_attn_time_first", "lng": "L0_attn_ln_g",
            "lnb": "L0_attn_ln_b", "bfv": "L0_ffn_bias"}
    wts = {k: np.ascontiguousarray(np.asarray(inputs[n], np.float32).T)
           for k, n in wmap.items()}
    vecs = {k: np.ascontiguousarray(
        np.asarray(inputs[n], np.float32).reshape(CH, 128).T)
        for k, n in vmap.items()}

    in_maps = []
    for core in range(B * NCHUNK):
        bi, ch = core // NCHUNK, core % NCHUNK
        t0 = ch * T_OWN
        stg = np.zeros((T_STG, C), np.float32)
        if ch == 0:
            stg[LANE] = x[bi, 1]
        else:
            stg[0] = x[bi, t0 - LANE - 1]
            stg[1:1 + LANE] = x[bi, t0 - LANE:t0]
        stg[1 + LANE:] = x[bi, t0:t0 + T_OWN]
        m = {"x": np.ascontiguousarray(stg.T),
             "cs": np.full((128, 1), 0.0 if ch == 0 else 1.0, np.float32)}
        m.update(wts)
        m.update(vecs)
        in_maps.append(m)
    return in_maps


_CACHE = {}


def kernel(**inputs):
    C, B, T = 2048, 2, 4096
    NCHUNK, LANE = 4, 128
    T_OWN = T // NCHUNK
    key = (C, T_OWN, LANE)
    if key not in _CACHE:
        _CACHE[key] = build_nc(C, T_OWN, LANE)
    nc = _CACHE[key]
    in_maps = stage_inputs(inputs, B, T, NCHUNK, C, T_OWN, LANE)
    res = run_bass_kernel_spmd(nc, in_maps, core_ids=list(range(B * NCHUNK)))
    out = np.empty((B, T, C), np.float32)
    for core in range(B * NCHUNK):
        bi, ch = core // NCHUNK, core % NCHUNK
        t0 = ch * T_OWN
        yc = np.asarray(res.results[core]["y"], np.float32).reshape(C, T_OWN)
        out[bi, t0:t0 + T_OWN, :] = yc.T
    return out


# version 13
# speedup vs baseline: 1.1167x; 1.1167x over previous
import contextlib

import numpy as np

import concourse.bass as bass
import concourse.mybir as mybir
from concourse.bass_utils import run_bass_kernel_spmd

F32 = mybir.dt.float32
BF16 = mybir.dt.bfloat16
F16 = mybir.dt.float16
AF = mybir.ActivationFunctionType
OP = mybir.AluOpType
AX = mybir.AxisListType

LN_EPS = 1e-5
FFN_EPS = 1e-10


def _tchunks(total, step=512):
    return [(s, min(s + step, total)) for s in range(0, total, step)]


def build_nc(C, T_OWN, LANE):
    CH = C // 128
    HC = C // 2
    OH = HC // 128
    NL = T_OWN // LANE
    T_ALL = LANE + T_OWN
    T_STG = T_ALL + 1
    TC_ALL = _tchunks(T_ALL)
    TC_OWN = _tchunks(T_OWN)
    TC_STG = _tchunks(T_STG)
    NB = 4

    nc = bass.Bass("TRN2", target_bir_lowering=False)

    x_d = nc.dram_tensor("x", [C, T_STG], F32, kind="ExternalInput").ap()
    w_d = {k: nc.dram_tensor(k, [C, C], F32, kind="ExternalInput").ap()
           for k in ("wk", "wv", "wr", "wg", "wo", "wf")}
    vnames = ("tmk", "tmv", "tmr", "tmg", "td", "tf", "lng", "lnb", "bfv")
    vec_d = {k: nc.dram_tensor(k, [128, CH], F32, kind="ExternalInput").ap()
             for k in vnames}
    cs_d = nc.dram_tensor("cs", [128, 1], F32, kind="ExternalInput").ap()
    y_d = nc.dram_tensor("y", [CH, 128, T_OWN], F32, kind="ExternalOutput").ap()

    sxk = nc.dram_tensor("sxk", [CH, 128, T_ALL], BF16).ap()
    sxv = nc.dram_tensor("sxv", [CH, 128, T_ALL], BF16).ap()
    sxr = nc.dram_tensor("sxr", [CH, 128, T_OWN], BF16).ap()
    sxg = nc.dram_tensor("sxg", [CH, 128, T_OWN], BF16).ap()
    sek = nc.dram_tensor("sek", [CH, 128, T_ALL], F16).ap()
    svd = nc.dram_tensor("svd", [CH, 128, T_ALL], F16).ap()
    sg2 = nc.dram_tensor("sg2", [CH, 128, T_OWN], BF16).ap()

    es0 = contextlib.ExitStack()
    e0 = es0.enter_context

    sem = {n: e0(nc.semaphore(n)) for n in
           ("sl", "sw0", "sw1", "scv", "scs", "smm", "sev", "st0", "st1",
            "sv", "ss", "sden", "srd", "sg0", "sg1")}
    cnt = {n: 0 for n in sem}

    def snap():
        return dict(cnt)

    def inc(instr, name, amt=1):
        instr.then_inc(sem[name], amt)
        cnt[name] += amt

    vec_sb = {k: e0(nc.sbuf_tensor("v_" + k, [128, CH], F32)) for k in vnames}
    cs_sb = e0(nc.sbuf_tensor("cs_sb", [128, 1], F32))
    w_sb = e0(nc.sbuf_tensor("w_sb", [128, CH], F32))
    eu_sb = e0(nc.sbuf_tensor("eu_sb", [128, CH], F32))
    w_rep = e0(nc.sbuf_tensor("w_rep", [128, CH, LANE], F32))
    ones_lane = e0(nc.sbuf_tensor("ones_lane", [128, LANE], F32))
    ones16 = e0(nc.sbuf_tensor("ones16", [128, 1], F16))
    onesK1 = e0(nc.sbuf_tensor("onesK1", [1, 128], F32))
    scale_sb = e0(nc.sbuf_tensor("scale_sb", [128, 1], F32))
    thr_sb = e0(nc.sbuf_tensor("thr_sb", [128, 1], F32))
    nthr_sb = e0(nc.sbuf_tensor("nthr_sb", [128, 1], F32))

    with nc.Block() as blk:
        b = snap()

        @blk.sync
        def _(e):
            for k in vnames:
                inc(e.dma_start(vec_sb[k][:, :], vec_d[k][:, :]), "sl", 16)
            inc(e.dma_start(cs_sb[:, :], cs_d[:, :]), "sl", 16)

        @blk.vector
        def _(e):
            e.memset(ones_lane[:, :], 1.0)
            e.memset(ones16[:, :], 1.0)
            inc(e.memset(onesK1[:, :], 1.0), "sv")
            e.wait_ge(sem["ss"], b["ss"] + 1)
            e.drain()
            for ci in range(CH):
                e.tensor_scalar(w_rep[:, ci, :], ones_lane[:, :],
                                w_sb[:, ci:ci + 1], None, OP.mult)

        @blk.scalar
        def _(e):
            e.wait_ge(sem["sl"], b["sl"] + 16 * (len(vnames) + 1))
            e.activation(w_sb[:, :], vec_sb["td"][:, :], AF.Exp)
            e.drain()
            e.activation(w_sb[:, :], w_sb[:, :], AF.Exp, scale=-1.0)
            e.drain()
            inc(e.activation(eu_sb[:, :], vec_sb["tf"][:, :], AF.Exp), "ss")

    group_id = [0]

    def mm_group(specs, shared, pre=None, tail_sync=None, tail_scalar=None):
        group_id[0] += 1
        P = len(specs)
        ntc = [len(s["tcs"]) for s in specs]
        ngh = [OH * n for n in ntc]
        GRP0 = [0] * P
        for p in range(1, P):
            GRP0[p] = GRP0[p - 1] + 2 * ngh[p - 1]
        NG = GRP0[-1] + 2 * ngh[-1]

        def g_info(g):
            p = 0
            while p + 1 < P and g >= GRP0[p + 1]:
                p += 1
            r = g - GRP0[p]
            h = r // ngh[p]
            r2 = r % ngh[p]
            oc = r2 // ntc[p]
            tj = r2 % ntc[p]
            return p, h, oc, tj

        NH = 2 * P
        GEND = [0] * NH
        for hh in range(NH):
            p, h = hh // 2, hh % 2
            GEND[hh] = GRP0[p] + (h + 1) * ngh[p]
        release = [0 if hh < 2 else GEND[hh - 2] for hh in range(NH)]

        preL = pre["preL"] if pre else 0
        castrec = list(pre["recs"]) if pre else []
        cv = cnt["scv"]
        cs_ = cnt["scs"]
        b0scv, b0scs = cnt["scv"], cnt["scs"]
        CV = [0] * NH
        CS = [0] * NH
        for hh in range(NH):
            p = hh // 2
            for ic in range(CH):
                jj = hh * CH + ic
                if jj < preL:
                    continue
                if specs[p]["quant"] or ic % 2 == 0:
                    cv += 1
                    castrec.append(("V", "scv", cv))
                else:
                    cs_ += 1
                    castrec.append(("S", "scs", cs_))
            CV[hh], CS[hh] = cv, cs_

        movpre = pre.get("mov_sl", {}) if pre else {}
        slt = {}
        run_sl = cnt["sl"]
        for p, s in enumerate(specs):
            if p in movpre:
                slt[p] = movpre[p]
            elif s["mov_load"] is not None:
                run_sl += 16 * CH
                slt[p] = run_sl

        signum = {}
        k = 0
        for g in range(NG):
            p, h, oc, tj = g_info(g)
            if specs[p]["evac"] in ("sig_attn", "sig_y_res"):
                signum[g] = k
                k += 1
        nsig = k
        rownum = {}
        g2num = {}
        sk = 0
        for p, s in enumerate(specs):
            if s["evac"] in ("exp_row", "copy_row"):
                for ocg in range(CH):
                    rownum[(p, ocg)] = sk
                    sk += 1
        for g in range(NG):
            p, h, oc, tj = g_info(g)
            if specs[p]["evac"] == "silu_g2":
                g2num[g] = sk
                sk += 1

        b = snap()
        NBK = len(shared["ps"])
        wbf = shared["wbf"]
        mwstg = shared["mwstg"]
        mwm2 = shared["mwm2"]
        rowst = shared["rowst"]
        sigst = shared["sigst"]
        g2st = shared["g2st"]
        ps = shared["ps"]
        blk = shared["es"].enter_context(nc.Block())

        mov_at = {}
        for p, s in enumerate(specs):
            if s["mov_load"] is not None and p not in movpre:
                mov_at.setdefault(GRP0[p], []).append(p)

        @blk.sync
        def _(e):
            for p, s in enumerate(specs):
                for h in range(2):
                    hh = 2 * p + h
                    for ic in range(CH):
                        jj = hh * CH + ic
                        if jj < preL:
                            continue
                        if jj >= 2:
                            eng, sname, tgt = castrec[jj - 2]
                            e.wait_ge(sem[sname], tgt)
                        inc(e.dma_start(mwstg[jj % 2][:, :],
                                        s["wt"][ic * 128:(ic + 1) * 128,
                                                (h * HC):(h + 1) * HC]),
                            "sw%d" % (jj % 2), 16)
            if tail_sync is not None:
                tail_sync(e)

        swtot = [b["sw0"], b["sw1"]]
        swtarget = [0] * (NH * CH)
        for jj in range(NH * CH):
            if jj < preL:
                continue
            swtot[jj % 2] += 16
            swtarget[jj] = swtot[jj % 2]
        cnt["sw0"] = swtot[0]
        cnt["sw1"] = swtot[1]

        def emit_cast(e, jj):
            hh, ic = jj // CH, jj % CH
            p, h = hh // 2, hh % 2
            s = specs[p]
            if release[hh] > 0:
                e.wait_ge(sem["smm"], b["smm"] + release[hh])
            e.wait_ge(sem["sw%d" % (jj % 2)], swtarget[jj])
            dstsl = wbf[hh % 2][:, ic, :]
            if s["quant"]:
                e.tensor_single_scalar(mwm2[jj % 2][:, :], mwstg[jj % 2][:, :],
                                       thr_sb[:, :], OP.is_gt)
                e.tensor_single_scalar(mwstg[jj % 2][:, :], mwstg[jj % 2][:, :],
                                       nthr_sb[:, :], OP.is_lt)
                e.drain()
                e.tensor_tensor(dstsl, mwm2[jj % 2][:, :], mwstg[jj % 2][:, :],
                                OP.subtract).then_inc(sem["scv"], 1)
            elif jj % 1 == 0 and castrec[jj][0] == "V":
                e.tensor_copy(dstsl, mwstg[jj % 2][:, :]).then_inc(sem["scv"], 1)
            else:
                e.activation(dstsl, mwstg[jj % 2][:, :],
                             AF.Copy).then_inc(sem["scs"], 1)

        v_casts_at = {}
        s_casts_at = {}
        for jj in range(preL, NH * CH):
            r = release[jj // CH]
            (v_casts_at if castrec[jj][0] == "V" else s_casts_at
             ).setdefault(r, []).append(jj)

        @blk.vector
        def _(e):
            prev_evac = None
            for g in range(NG):
                for jj in v_casts_at.get(g, []):
                    emit_cast(e, jj)
                p, h, oc, tj = g_info(g)
                s = specs[p]
                if s["evac"] in ("sig_attn", "sig_y_res"):
                    t0, t1 = s["tcs"][tj]
                    ocg = h * OH + oc
                    k = signum[g]
                    if s["evac"] != prev_evac:
                        e.drain()
                        prev_evac = s["evac"]
                    e.wait_ge(sem["sev"], b["sev"] + g + 1)
                    if s["evac"] == "sig_attn":
                        inc(e.tensor_tensor(s["wkv_bf"][:, ocg, t0:t1],
                                            sigst[k % 2][:, :t1 - t0],
                                            s["wkv_bf"][:, ocg, t0:t1],
                                            OP.mult), "sv")
                    else:
                        inc(e.tensor_tensor(s["y_dst"][:, ocg, t0:t1],
                                            sigst[k % 2][:, :t1 - t0],
                                            s["wkv_bf"][:, ocg, t0:t1],
                                            OP.mult), "sv")
            for jj in v_casts_at.get(NG, []):
                emit_cast(e, jj)

        @blk.scalar
        def _(e):
            for g in range(NG):
                for mp in mov_at.get(g, []):
                    for ci in range(CH):
                        inc(e.dma_start(specs[mp]["mov"][:, ci, :],
                                        specs[mp]["mov_load"][ci, :, :]),
                            "sl", 16)
                for jj in s_casts_at.get(g, []):
                    emit_cast(e, jj)
                p, h, oc, tj = g_info(g)
                s = specs[p]
                t0, t1 = s["tcs"][tj]
                ocg = h * OH + oc
                bank = ps[g % NBK]
                e.wait_ge(sem["smm"], b["smm"] + g + 1)
                ev = s["evac"]
                if ev in ("exp_row", "copy_row"):
                    ro = rownum[(p, ocg)]
                    if tj == 0 and ro >= 2:
                        stp = "st%d" % (ro % 2)
                        e.wait_ge(sem[stp], b[stp] + 16 * (ro // 2))
                    fn = AF.Exp if ev == "exp_row" else AF.Copy
                    inc(e.activation(rowst[ro % 2][:, t0:t1],
                                     bank[:, :t1 - t0], fn), "sev")
                elif ev == "silu_g2":
                    gk = g2num[g]
                    if gk >= 2:
                        stp = "st%d" % (gk % 2)
                        e.wait_ge(sem[stp], b[stp] + 16 * (gk // 2))
                    inc(e.activation(g2st[gk % 2][:, :t1 - t0],
                                     bank[:, :t1 - t0], AF.Silu,
                                     bias=s["bias_vec"][:, ocg:ocg + 1],
                                     scale=scale_sb[:, 0:1]), "sev")
                elif ev in ("sig_attn", "sig_y_res"):
                    k = signum[g]
                    if k >= 2:
                        e.wait_ge(sem["sv"], b["sv"] + k - 1)
                    inc(e.activation(sigst[k % 2][:, :t1 - t0],
                                     bank[:, :t1 - t0], AF.Sigmoid), "sev")
                elif ev == "copy_o":
                    inc(e.activation(s["dst"][:, ocg, t0:t1],
                                     bank[:, :t1 - t0], AF.Copy), "sev")
            for jj in s_casts_at.get(NG, []):
                emit_cast(e, jj)
            if tail_scalar is not None:
                tail_scalar(e)

        @blk.gpsimd
        def _(e):
            any_store = False
            for p, s in enumerate(specs):
                if s["evac"] in ("exp_row", "copy_row"):
                    any_store = True
                    for ocg in range(CH):
                        ro = rownum[(p, ocg)]
                        e.wait_ge(sem["sev"],
                                  b["sev"] + GRP0[p] + (ocg + 1) * ntc[p])
                        inc(e.dma_start(s["store_ap"][ocg, :, :],
                                        rowst[ro % 2][:, :]),
                            "st%d" % (ro % 2), 16)
                elif s["evac"] == "silu_g2":
                    any_store = True
                    for g in range(GRP0[p], GRP0[p] + 2 * ngh[p]):
                        _p, h, oc, tj = g_info(g)
                        t0, t1 = s["tcs"][tj]
                        ocg = h * OH + oc
                        gk = g2num[g]
                        e.wait_ge(sem["sev"], b["sev"] + g + 1)
                        inc(e.dma_start(s["store_ap"][ocg, :, t0:t1],
                                        g2st[gk % 2][:, :t1 - t0]),
                            "st%d" % (gk % 2), 16)
            if any_store:
                e.wait_ge(sem["st0"], cnt["st0"])
                e.wait_ge(sem["st1"], cnt["st1"])

        @blk.tensor
        def _(e):
            for p, s in enumerate(specs):
                if s["mov_load"] is not None:
                    e.wait_ge(sem["sl"], slt[p])
                if s.get("wait_y"):
                    e.wait_ge(sem["sv"], b["sv"] + nsig)
                for h in range(2):
                    hh = 2 * p + h
                    e.wait_ge(sem["scv"], CV[hh])
                    if CS[hh] > b0scs:
                        e.wait_ge(sem["scs"], CS[hh])
                    for oc in range(OH):
                        for tj, (t0, t1) in enumerate(s["tcs"]):
                            g = GRP0[p] + h * ngh[p] + oc * ntc[p] + tj
                            bank = ps[g % NBK]
                            if g >= NBK:
                                e.wait_ge(sem["sev"], b["sev"] + g - NBK + 1)
                            for ic in range(CH):
                                mm = e.matmul(bank[:, :t1 - t0],
                                              wbf[hh % 2][:, ic,
                                                          oc * 128:(oc + 1) * 128],
                                              s["mov"][:, ic, t0:t1],
                                              start=(ic == 0),
                                              stop=(ic == CH - 1))
                            inc(mm, "smm")
        cnt["scv"] = cv
        cnt["scs"] = cs_

    def spec(wt, tcs, evac, mov, mov_load=None, quant=False, store_ap=None,
             bias_vec=None, dst=None, wkv_bf=None, y_dst=None, wait_y=False):
        return dict(wt=wt, tcs=tcs, evac=evac, mov=mov, mov_load=mov_load,
                    quant=quant, store_ap=store_ap, bias_vec=bias_vec,
                    dst=dst, wkv_bf=wkv_bf, y_dst=y_dst, wait_y=wait_y)

    def make_shared(es, tag, n_tok, want_quant, want_row, want_sig, want_g2,
                    nps=8):
        ec = es.enter_context
        sh = {"es": es}
        sh["wbf"] = [ec(nc.sbuf_tensor(tag + "wbf%d" % i, [128, CH, HC], BF16))
                     for i in range(2)]
        sh["mwstg"] = [ec(nc.sbuf_tensor(tag + "mwstg%d" % i, [128, HC], F32))
                       for i in range(2)]
        sh["mwm2"] = ([ec(nc.sbuf_tensor(tag + "mwm2%d" % i, [128, HC], F32))
                       for i in range(2)] if want_quant else None)
        sh["rowst"] = ([ec(nc.sbuf_tensor(tag + "row%d" % i, [128, n_tok], F32))
                        for i in range(2)] if want_row else None)
        sh["sigst"] = ([ec(nc.sbuf_tensor(tag + "sig%d" % i, [128, 512], F32))
                        for i in range(2)] if want_sig else None)
        sh["g2st"] = ([ec(nc.sbuf_tensor(tag + "g2s%d" % i, [128, 512], BF16))
                       for i in range(2)] if want_g2 else None)
        sh["ps"] = [ec(nc.psum_tensor(tag + "ps%d" % i, [128, 512], F32))
                    for i in range(nps)]
        return sh

    es1 = contextlib.ExitStack()
    xn_sb = es1.enter_context(nc.sbuf_tensor("xn_sb", [128, CH, T_OWN], BF16))

    es_pre = contextlib.ExitStack()
    g1wbf0 = es_pre.enter_context(
        nc.sbuf_tensor("g1wbf0", [128, CH, HC], BF16))
    g1mwstg = [es_pre.enter_context(
        nc.sbuf_tensor("g1mwstg%d" % i, [128, HC], F32)) for i in range(2)]
    g1_pre_recs = []

    with contextlib.ExitStack() as es:
        ec = es.enter_context
        xh16 = ec(nc.sbuf_tensor("xh16", [128, CH, T_STG], F16))
        xstg = [ec(nc.sbuf_tensor("xstg%d" % i, [128, T_STG], F32))
                for i in range(2)]
        xq16 = [ec(nc.sbuf_tensor("xq16_%d" % i, [128, T_STG], F16))
                for i in range(2)]
        stat = ec(nc.sbuf_tensor("stat", [1, T_STG], F32))
        stat2 = ec(nc.sbuf_tensor("stat2", [1, T_STG], F32))
        stat3 = ec(nc.sbuf_tensor("stat3", [1, T_STG], F32))
        rb_sb = ec(nc.sbuf_tensor("rb_sb", [128, T_STG], F32))
        m2b_sb = ec(nc.sbuf_tensor("m2b_sb", [128, T_STG], F32))
        xnf = ec(nc.sbuf_tensor("xnf", [128, T_STG], F32))
        xxf = ec(nc.sbuf_tensor("xxf", [128, T_ALL], F32))
        xn_stg = [ec(nc.sbuf_tensor("xn_stg%d" % i, [128, T_OWN], BF16))
                  for i in range(2)]
        ik_stg = [ec(nc.sbuf_tensor("ik_stg%d" % i, [128, T_ALL], BF16))
                  for i in range(2)]
        iv_stg = [ec(nc.sbuf_tensor("iv_stg%d" % i, [128, T_ALL], BF16))
                  for i in range(2)]
        ir_stg = [ec(nc.sbuf_tensor("ir_stg%d" % i, [128, T_OWN], BF16))
                  for i in range(2)]
        ig_stg = [ec(nc.sbuf_tensor("ig_stg%d" % i, [128, T_OWN], BF16))
                  for i in range(2)]

        with contextlib.ExitStack() as esA:
            ecA = esA.enter_context
            ps1 = [ecA(nc.psum_tensor("ps1_%d" % j, [1, 512], F32))
                   for j in range(len(TC_STG))]
            ps2 = [ecA(nc.psum_tensor("ps2_%d" % j, [1, 512], F32))
                   for j in range(len(TC_STG))]
            blk = esA.enter_context(nc.Block())
            b = snap()

            @blk.sync
            def _(e):
                for ci in range(CH):
                    if ci >= 2:
                        e.wait_ge(sem["ss"], b["ss"] + ci - 1)
                    inc(e.dma_start(xstg[ci % 2][:, :],
                                    x_d[ci * 128:(ci + 1) * 128, :]),
                        "sw%d" % (ci % 2), 16)

            @blk.scalar
            def _(e):
                for ci in range(CH):
                    swp = "sw%d" % (ci % 2)
                    e.wait_ge(sem[swp], b[swp] + 16 * (ci // 2 + 1))
                    if ci >= 2:
                        e.wait_ge(sem["smm"], b["smm"] + ci - 1)
                    e.activation(xh16[:, ci, :], xstg[ci % 2][:, :], AF.Copy)
                    inc(e.activation(xq16[ci % 2][:, :], xstg[ci % 2][:, :],
                                     AF.Square), "ss")
                e.wait_ge(sem["sv"], b["sv"] + 1)
                e.activation(stat[:, :], stat[:, :], AF.Ln)
                e.drain()
                inc(e.activation(stat[:, :], stat[:, :], AF.Exp, scale=-0.5),
                    "ss")

            @blk.tensor
            def _(e):
                for ci in range(CH):
                    e.wait_ge(sem["ss"], b["ss"] + ci + 1)
                    for j, (t0, t1) in enumerate(TC_STG):
                        e.matmul(ps1[j][:, :t1 - t0], ones16[:, :],
                                 xh16[:, ci, t0:t1],
                                 start=(ci == 0), stop=(ci == CH - 1))
                        mm = e.matmul(ps2[j][:, :t1 - t0], ones16[:, :],
                                      xq16[ci % 2][:, t0:t1],
                                      start=(ci == 0), stop=(ci == CH - 1))
                    inc(mm, "smm")

            @blk.vector
            def _(e):
                e.wait_ge(sem["smm"], b["smm"] + CH)
                for j, (t0, t1) in enumerate(TC_STG):
                    e.tensor_scalar(stat2[:, t0:t1], ps1[j][:, :t1 - t0],
                                    1.0 / C, None, OP.mult)
                    e.tensor_scalar(stat[:, t0:t1], ps2[j][:, :t1 - t0],
                                    1.0 / C, None, OP.mult)
                e.drain()
                e.tensor_tensor(stat3[:, :], stat2[:, :], stat2[:, :], OP.mult)
                e.drain()
                e.tensor_tensor(stat[:, :], stat[:, :], stat3[:, :], OP.subtract)
                e.drain()
                inc(e.tensor_scalar(stat[:, :], stat[:, :], LN_EPS, None,
                                    OP.add), "sv")
                e.wait_ge(sem["ss"], b["ss"] + CH + 1)
                e.drain()
                inc(e.tensor_tensor(stat2[:, :], stat2[:, :], stat[:, :],
                                    OP.mult), "sv")

        with contextlib.ExitStack() as esB:
            ecB = esB.enter_context
            pb1 = ecB(nc.psum_tensor("pb1", [128, 512], F32))
            pb2 = ecB(nc.psum_tensor("pb2", [128, 512], F32))
            SCW = min(512, HC)
            ps_sc = ecB(nc.psum_tensor("ps_sc", [1, SCW], F32))
            ps_bc = ecB(nc.psum_tensor("ps_bc", [128, 4], F32))
            habs = [ecB(nc.sbuf_tensor("fs_a%d" % i, [128, HC], F16))
                    for i in range(2)]
            sctile = ecB(nc.sbuf_tensor("sctile", [1, SCW], F32))
            sctiny = ecB(nc.sbuf_tensor("sctiny", [1, 4], F32))
            blk = esB.enter_context(nc.Block())
            b = snap()
            QC = _tchunks(HC)
            nbc = len(TC_STG)
            NSL = 2 * CH
            NSLK = NSL + CH

            @blk.tensor
            def _(e):
                for j, (t0, t1) in enumerate(TC_STG):
                    if j >= 1:
                        e.wait_ge(sem["sev"], b["sev"] + j)
                    e.matmul(pb1[:, :t1 - t0], onesK1[:, :], stat[:, t0:t1],
                             start=True, stop=True)
                    inc(e.matmul(pb2[:, :t1 - t0], onesK1[:, :],
                                 stat2[:, t0:t1], start=True, stop=True), "smm")
                for sl_ in range(NSL):
                    e.wait_ge(sem["scv"], b["scv"] + sl_ + 1)
                    for j, (q0, q1) in enumerate(QC):
                        mm = e.matmul(ps_sc[:, :q1 - q0], ones16[:, :],
                                      habs[sl_ % 2][:, q0:q1],
                                      start=(sl_ == 0 and j == 0),
                                      stop=(sl_ == NSL - 1 and j == len(QC) - 1))
                    inc(mm, "smm")
                e.wait_ge(sem["srd"], b["srd"] + 1)
                e.matmul(ps_bc[:, 0:1], onesK1[:, :], sctiny[:, 0:1],
                         start=True, stop=True)
                e.matmul(ps_bc[:, 1:2], onesK1[:, :], sctiny[:, 1:2],
                         start=True, stop=True)
                inc(e.matmul(ps_bc[:, 2:3], onesK1[:, :], sctiny[:, 2:3],
                             start=True, stop=True), "smm")

            @blk.sync
            def _(e):
                for sl_ in range(NSLK):
                    if sl_ >= 2:
                        if sl_ - 2 < NSL:
                            e.wait_ge(sem["scv"], b["scv"] + sl_ - 1)
                        else:
                            e.wait_ge(sem["scs"],
                                      b["scs"] + (sl_ - 2 - NSL) + 1)
                    if sl_ < NSL:
                        ic, h = sl_ // 2, sl_ % 2
                        src = w_d["wf"][ic * 128:(ic + 1) * 128,
                                        h * HC:(h + 1) * HC]
                    else:
                        ic = sl_ - NSL
                        src = w_d["wk"][ic * 128:(ic + 1) * 128, 0:HC]
                    inc(e.dma_start(g1mwstg[sl_ % 2][:, :], src),
                        "sw%d" % (sl_ % 2), 16)

            @blk.scalar
            def _(e):
                for j, (t0, t1) in enumerate(TC_STG):
                    e.wait_ge(sem["smm"], b["smm"] + j + 1)
                    e.activation(rb_sb[:, t0:t1], pb1[:, :t1 - t0], AF.Copy)
                    inc(e.activation(m2b_sb[:, t0:t1], pb2[:, :t1 - t0],
                                     AF.Copy), "sev")
                for sl_ in range(NSLK):
                    swp = "sw%d" % (sl_ % 2)
                    e.wait_ge(sem[swp], b[swp] + 16 * (sl_ // 2 + 1))
                    if sl_ < NSL:
                        if sl_ >= 2:
                            e.wait_ge(sem["smm"], b["smm"] + nbc + sl_ - 1)
                        inc(e.activation(habs[sl_ % 2][:, :],
                                         g1mwstg[sl_ % 2][:, :], AF.Abs),
                            "scv")
                    else:
                        ic = sl_ - NSL
                        inc(e.activation(g1wbf0[:, ic, :],
                                         g1mwstg[sl_ % 2][:, :], AF.Copy),
                            "scs")
                        g1_pre_recs.append(("S", "scs", cnt["scs"]))
                e.wait_ge(sem["smm"], b["smm"] + nbc + NSL + 1)
                e.activation(scale_sb[:, :], ps_bc[:, 0:1], AF.Copy)
                e.activation(thr_sb[:, :], ps_bc[:, 1:2], AF.Copy)
                e.activation(nthr_sb[:, :], ps_bc[:, 2:3], AF.Copy)

            @blk.vector
            def _(e):
                e.wait_ge(sem["sev"], b["sev"] + len(TC_STG))
                for ci in range(CH):
                    if ci >= 2:
                        stp = "st%d" % (ci % 2)
                        e.wait_ge(sem[stp], b[stp] + 80 * (ci // 2))
                    s = ci % 2
                    e.drain()
                    e.tensor_tensor(xnf[:, :], xh16[:, ci, :], rb_sb[:, :],
                                    OP.mult)
                    e.drain()
                    e.tensor_tensor(xnf[:, :], xnf[:, :], m2b_sb[:, :],
                                    OP.subtract)
                    e.drain()
                    e.tensor_tensor(xxf[:, :], xnf[:, 0:T_ALL], xnf[:, 1:T_STG],
                                    OP.subtract)
                    e.drain()
                    e.scalar_tensor_tensor(ik_stg[s][:, :], xxf[:, :],
                                           vec_sb["tmk"][:, ci:ci + 1],
                                           xnf[:, 1:T_STG], OP.mult, OP.add)
                    e.scalar_tensor_tensor(iv_stg[s][:, :], xxf[:, :],
                                           vec_sb["tmv"][:, ci:ci + 1],
                                           xnf[:, 1:T_STG], OP.mult, OP.add)
                    e.scalar_tensor_tensor(ir_stg[s][:, :], xxf[:, LANE:],
                                           vec_sb["tmr"][:, ci:ci + 1],
                                           xnf[:, 1 + LANE:T_STG], OP.mult,
                                           OP.add)
                    e.scalar_tensor_tensor(ig_stg[s][:, :], xxf[:, LANE:],
                                           vec_sb["tmg"][:, ci:ci + 1],
                                           xnf[:, 1 + LANE:T_STG], OP.mult,
                                           OP.add)
                    inc(e.tensor_copy(xn_stg[s][:, :], xnf[:, 1 + LANE:T_STG]),
                        "sv")
                e.wait_ge(sem["smm"], b["smm"] + nbc + NSL)
                e.tensor_copy(sctile[:, :], ps_sc[:, :])
                e.drain()
                e.reduce_sum(sctiny[:, 3:4], sctile[:, :], axis=AX.X)
                e.drain()
                e.tensor_scalar(sctiny[:, 0:1], sctiny[:, 3:4], 1.0 / (C * C),
                                None, OP.mult)
                e.drain()
                e.tensor_scalar(sctiny[:, 1:2], sctiny[:, 0:1], FFN_EPS, 0.5,
                                OP.add, OP.mult)
                e.drain()
                inc(e.tensor_scalar(sctiny[:, 2:3], sctiny[:, 1:2], -1.0, None,
                                    OP.mult), "srd")

            @blk.gpsimd
            def _(e):
                for ci in range(CH):
                    e.wait_ge(sem["sv"], b["sv"] + ci + 1)
                    s = ci % 2
                    stp = "st%d" % s
                    inc(e.dma_start(sxk[ci, :, :], ik_stg[s][:, :]), stp, 16)
                    inc(e.dma_start(sxv[ci, :, :], iv_stg[s][:, :]), stp, 16)
                    inc(e.dma_start(sxr[ci, :, :], ir_stg[s][:, :]), stp, 16)
                    inc(e.dma_start(sxg[ci, :, :], ig_stg[s][:, :]), stp, 16)
                    inc(e.dma_start(xn_sb[:, ci, :], xn_stg[s][:, :]), stp, 16)
                e.wait_ge(sem["st0"], cnt["st0"])
                e.wait_ge(sem["st1"], cnt["st1"])

    with contextlib.ExitStack() as es:
        ec = es.enter_context
        mov_kv = ec(nc.sbuf_tensor("mov_kv", [128, CH, T_ALL], BF16))
        sh = {"es": es,
              "wbf": [g1wbf0,
                      ec(nc.sbuf_tensor("g1wbf1", [128, CH, HC], BF16))],
              "mwstg": g1mwstg,
              "mwm2": [ec(nc.sbuf_tensor("g1mwm2%d" % i, [128, HC], F32))
                       for i in range(2)],
              "rowst": [ec(nc.sbuf_tensor("g1row%d" % i, [128, T_ALL], F16))
                        for i in range(2)],
              "sigst": None,
              "g2st": [ec(nc.sbuf_tensor("g1g2s%d" % i, [128, 512], BF16))
                       for i in range(2)],
              "ps": [ec(nc.psum_tensor("g1ps%d" % i, [128, 512], F32))
                     for i in range(8)]}
        mm_group([
            spec(w_d["wk"], TC_ALL, "exp_row", mov_kv, mov_load=sxk,
                 store_ap=sek),
            spec(w_d["wv"], TC_ALL, "copy_row", mov_kv, mov_load=sxv,
                 store_ap=svd),
            spec(w_d["wf"], TC_OWN, "silu_g2", xn_sb, quant=True,
                 bias_vec=vec_sb["bfv"], store_ap=sg2),
        ], sh, pre=dict(preL=CH, recs=g1_pre_recs))
    es_pre.close()
    es1.close()

    es_o = contextlib.ExitStack()
    sr_sb = es_o.enter_context(
        nc.sbuf_tensor("sr_sb", [128, CH, T_OWN], BF16, side="right"))

    with contextlib.ExitStack() as es:
        ec = es.enter_context
        wkv_sb = ec(nc.sbuf_tensor("wkv_sb", [128, CH, T_OWN], BF16))
        mov_rg = ec(nc.sbuf_tensor("mov_rg", [128, CH, T_OWN], BF16))
        SW = min(512, C)
        NSLAB = C // SW
        OSL = SW // 128
        wbf = [ec(nc.sbuf_tensor("rgo_wbf%d" % i, [128, CH, SW], BF16))
               for i in range(2)]
        mwstg = [ec(nc.sbuf_tensor("rgo_mw%d" % i, [128, SW], F32))
                 for i in range(2)]
        NSG = 6
        sgst = [ec(nc.sbuf_tensor("rgo_sg%d" % i, [128, 512], BF16))
                for i in range(NSG)]
        ekb = [ec(nc.sbuf_tensor("r_ekb%d" % i, [128, CH, LANE], F16))
               for i in range(2)]
        vkb = [ec(nc.sbuf_tensor("r_vkb%d" % i, [128, CH, LANE], F16))
               for i in range(2)]
        za = ec(nc.sbuf_tensor("za", [128, CH, 1 + LANE], F32))
        zb = ec(nc.sbuf_tensor("zb", [128, CH, 1 + LANE], F32))
        carry_a = ec(nc.sbuf_tensor("carry_a", [128, CH], F32))
        carry_b = ec(nc.sbuf_tensor("carry_b", [128, CH], F32))
        ps = [ec(nc.psum_tensor("rgo_ps%d" % i, [128, 512], F32))
              for i in range(8)]
        blk = es.enter_context(nc.Block())
        b = snap()
        NBK = len(ps)
        cnt_scv0 = cnt["scv"]

        ntc = len(TC_OWN)
        NGp = NSLAB * OSL * ntc
        NR, NRG, NG = NGp, 2 * NGp, 3 * NGp
        phs = [dict(wt=w_d["wr"], mov=mov_rg, mov_load=sxr),
               dict(wt=w_d["wg"], mov=mov_rg, mov_load=sxg),
               dict(wt=w_d["wo"], mov=wkv_sb, mov_load=None)]

        def g_info(g):
            p, r = g // NGp, g % NGp
            return p, r // (OSL * ntc), (r // ntc) % OSL, r % ntc

        NSS = 3 * NSLAB
        GENDs = [(ss + 1) * OSL * ntc for ss in range(NSS)]
        release = [0 if ss < 2 else GENDs[ss - 2] for ss in range(NSS)]

        castrec = []
        cv, cs_ = cnt["scv"], cnt["scs"]
        CVs, CSs = [0] * NSS, [0] * NSS
        for ss in range(NSS):
            for ic in range(CH):
                cs_ += 1
                castrec.append(("S", "scs", cs_))
            CVs[ss], CSs[ss] = cv, cs_

        PC, SR, PQ = {0: 0}, {}, {}
        for l in range(1, NL + 1):
            PC[l] = 1 if l == 1 else PQ[l - 1]
            SR[l] = min(4 + 3 * l, NRG - 1)
            PQ[l] = min(max(PC[l] + 1, SR[l] + 1), NG)
        PSR = {}
        prevpos = 0
        for l in range(1, NL + 1):
            PSR[l] = min(max(NR + l, PQ[l] + 1, prevpos + 1), NG)
            prevpos = PSR[l]

        v_lane_at = {}
        v_lane_at.setdefault(PC[0], []).append(("core", 0))
        for l in range(1, NL + 1):
            v_lane_at.setdefault(PC[l], []).append(("core", l))
            v_lane_at.setdefault(PQ[l], []).append(("q", l))
            v_lane_at.setdefault(PSR[l], []).append(("sr", l))
        gpos = 0
        for k in range(NGp):
            _p, sl_g, oc_g, tj_g = g_info(NR + k)
            t1k = TC_OWN[tj_g][1]
            Lk = (t1k + LANE - 1) // LANE
            gpos = min(max(gpos, NR + k + 1, PSR[Lk]), NG)
            v_lane_at.setdefault(gpos, []).append(("gmul", k))

        svord = {}
        i = 0
        for g in range(NG + 1):
            for kind, l in v_lane_at.get(g, []):
                if kind != "core" or l == 0:
                    i += 1
                    svord[(kind, l)] = i
        NSV = i

        swtot = [b["sw0"], b["sw1"]]
        swtarget = [0] * (NSS * CH)
        for jj in range(NSS * CH):
            swtot[jj % 2] += 16
            swtarget[jj] = swtot[jj % 2]
        cnt["sw0"], cnt["sw1"] = swtot[0], swtot[1]

        sttot = [b["st0"], b["st1"]]
        st_lane = {}
        for l in range(NL + 1):
            sttot[l % 2] += 32
            st_lane[l] = sttot[l % 2]
        cnt["st0"], cnt["st1"] = sttot[0], sttot[1]

        def emit_cast(e, jj):
            ss, ic = jj // CH, jj % CH
            if release[ss] > 0:
                e.wait_ge(sem["smm"], b["smm"] + release[ss])
            e.wait_ge(sem["sw%d" % (jj % 2)], swtarget[jj])
            if castrec[jj][0] == "V":
                e.tensor_copy(wbf[ss % 2][:, ic, :],
                              mwstg[jj % 2][:, :]).then_inc(sem["scv"], 1)
            else:
                e.activation(wbf[ss % 2][:, ic, :], mwstg[jj % 2][:, :],
                             AF.Copy).then_inc(sem["scs"], 1)

        v_casts_at, s_casts_at = {}, {}
        for jj in range(NSS * CH):
            (v_casts_at if castrec[jj][0] == "V" else s_casts_at
             ).setdefault(release[jj // CH], []).append(jj)

        @blk.sync
        def _(e):
            for jj in range(NSS * CH):
                ss, ic = jj // CH, jj % CH
                p, sl = ss // NSLAB, ss % NSLAB
                if jj >= 2:
                    eng, sname, tgt = castrec[jj - 2]
                    e.wait_ge(sem[sname], tgt)
                e.dma_start(mwstg[jj % 2][:, :],
                            phs[p]["wt"][ic * 128:(ic + 1) * 128,
                                         sl * SW:(sl + 1) * SW]
                            ).then_inc(sem["sw%d" % (jj % 2)], 16)

        @blk.gpsimd
        def _(e):
            for l in range(NL + 1):
                if l >= 2:
                    key = ("q", l - 2) if l - 2 >= 1 else ("core", 0)
                    e.wait_ge(sem["sv"], b["sv"] + svord[key])
                sl_ = slice(l * LANE, (l + 1) * LANE)
                stp = "st%d" % (l % 2)
                e.dma_start(ekb[l % 2][:, :, :],
                            sek[:, :, sl_].transpose([1, 0, 2])
                            ).then_inc(sem[stp], 16)
                e.dma_start(vkb[l % 2][:, :, :],
                            svd[:, :, sl_].transpose([1, 0, 2])
                            ).then_inc(sem[stp], 16)

        def lane_core(e, l):
            eb, vb = ekb[l % 2], vkb[l % 2]
            stp = "st%d" % (l % 2)
            e.wait_ge(sem[stp], st_lane[l])
            e.drain()
            e.tensor_tensor(vb[:, :, :], eb[:, :, :],
                            vb[:, :, :], OP.mult)
            e.drain()
            e.tensor_copy(za[:, :, 0], carry_a[:, :])
            e.tensor_copy(zb[:, :, 0], carry_b[:, :])
            for ci in range(CH):
                e.tensor_tensor_scan(za[:, ci, 1:1 + LANE], w_rep[:, ci, :],
                                     vb[:, ci, :], carry_a[:, ci:ci + 1],
                                     OP.mult, OP.add)
            for ci in range(CH):
                e.tensor_tensor_scan(zb[:, ci, 1:1 + LANE], w_rep[:, ci, :],
                                     eb[:, ci, :], carry_b[:, ci:ci + 1],
                                     OP.mult, OP.add)
            e.drain()
            if l == 0:
                e.tensor_scalar(carry_a[:, :], za[:, :, LANE],
                                cs_sb[:, 0:1], None, OP.mult)
                inc(e.tensor_scalar(carry_b[:, :], zb[:, :, LANE],
                                    cs_sb[:, 0:1], None, OP.mult), "sv")
            else:
                e.tensor_copy(carry_a[:, :], za[:, :, LANE])
                e.tensor_copy(carry_b[:, :], zb[:, :, LANE])
                for ci in range(CH):
                    e.scalar_tensor_tensor(vb[:, ci, :], vb[:, ci, :],
                                           eu_sb[:, ci:ci + 1],
                                           za[:, ci, 0:LANE], OP.mult, OP.add)
                for ci in range(CH):
                    last = e.scalar_tensor_tensor(
                        eb[:, ci, :], eb[:, ci, :],
                        eu_sb[:, ci:ci + 1], zb[:, ci, 0:LANE],
                        OP.mult, OP.add)
                inc(last, "sden")

        @blk.vector
        def _(e):
            e.memset(carry_a[:, :], 0.0)
            e.memset(carry_b[:, :], 0.0)
            e.drain()
            for g in range(NG + 1):
                for jj in v_casts_at.get(g, []):
                    emit_cast(e, jj)
                for kind, l in v_lane_at.get(g, []):
                    if kind == "gmul":
                        _p, sl_g, oc_g, tj_g = g_info(NR + l)
                        t0k, t1k = TC_OWN[tj_g]
                        ocg_k = sl_g * OSL + oc_g
                        e.wait_ge(sem["sev"], b["sev"] + NR + l + 1)
                        e.drain()
                        inc(e.tensor_tensor(wkv_sb[:, ocg_k, t0k:t1k],
                                            wkv_sb[:, ocg_k, t0k:t1k],
                                            sgst[l % NSG][:, :t1k - t0k],
                                            OP.mult), "sv")
                        continue
                    sl_ = slice((l - 1) * LANE, l * LANE)
                    if kind == "core":
                        lane_core(e, l)
                    elif kind == "q":
                        e.wait_ge(sem["srd"], b["srd"] + l)
                        e.drain()
                        inc(e.tensor_tensor(wkv_sb[:, :, sl_],
                                            vkb[l % 2][:, :, :],
                                            ekb[l % 2][:, :, :],
                                            OP.mult), "sv")
                    elif kind == "sr":
                        e.wait_ge(sem["sev"], b["sev"] + NR)
                        e.drain()
                        inc(e.tensor_tensor(wkv_sb[:, :, sl_],
                                            wkv_sb[:, :, sl_],
                                            sr_sb[:, :, sl_], OP.mult), "sv")

        @blk.scalar
        def _(e):
            recip_at = {}
            for l in range(1, NL + 1):
                recip_at.setdefault(SR[l], []).append(l)
            for g in range(NG):
                p, sl, oc, tj = g_info(g)
                if g == 0:
                    for ci in range(CH):
                        inc(e.dma_start(mov_rg[:, ci, :], sxr[ci, :, :]),
                            "sl", 16)
                if g == NR:
                    for ci in range(CH):
                        inc(e.dma_start(mov_rg[:, ci, :], sxg[ci, :, :]),
                            "sl", 16)
                for jj in s_casts_at.get(g, []):
                    emit_cast(e, jj)
                for l in recip_at.get(g, []):
                    eb = ekb[l % 2]
                    e.wait_ge(sem["sden"], b["sden"] + l)
                    e.activation(eb[:, :, :], eb[:, :, :], AF.Ln)
                    e.drain()
                    inc(e.activation(eb[:, :, :], eb[:, :, :],
                                     AF.Exp, scale=-1.0), "srd")
                t0, t1 = TC_OWN[tj]
                ocg = sl * OSL + oc
                bank = ps[g % NBK]
                e.wait_ge(sem["smm"], b["smm"] + g + 1)
                if p == 0:
                    inc(e.activation(sr_sb[:, ocg, t0:t1], bank[:, :t1 - t0],
                                     AF.Sigmoid), "sev")
                elif p == 1:
                    k = g - NR
                    if k >= NSG:
                        e.wait_ge(sem["sv"],
                                  b["sv"] + svord[("gmul", k - NSG)])
                    inc(e.activation(sgst[k % NSG][:, :t1 - t0],
                                     bank[:, :t1 - t0], AF.Sigmoid), "sev")
                else:
                    inc(e.activation(sr_sb[:, ocg, t0:t1], bank[:, :t1 - t0],
                                     AF.Copy), "sev")
            for jj in s_casts_at.get(NG, []):
                emit_cast(e, jj)

        @blk.tensor
        def _(e):
            for p in range(3):
                if phs[p]["mov_load"] is not None:
                    e.wait_ge(sem["sl"], b["sl"] + 16 * CH * (p + 1))
                if p == 2:
                    e.wait_ge(sem["sv"], b["sv"] + NSV)
                for sl in range(NSLAB):
                    ss = p * NSLAB + sl
                    if CVs[ss] > cnt_scv0:
                        e.wait_ge(sem["scv"], CVs[ss])
                    e.wait_ge(sem["scs"], CSs[ss])
                    for oc in range(OSL):
                        for tj, (t0, t1) in enumerate(TC_OWN):
                            g = p * NGp + sl * OSL * ntc + oc * ntc + tj
                            bank = ps[g % NBK]
                            if g >= NBK:
                                e.wait_ge(sem["sev"], b["sev"] + g - NBK + 1)
                            for ic in range(CH):
                                mm = e.matmul(bank[:, :t1 - t0],
                                              wbf[ss % 2][:, ic,
                                                          oc * 128:(oc + 1) * 128],
                                              phs[p]["mov"][:, ic, t0:t1],
                                              start=(ic == 0),
                                              stop=(ic == CH - 1))
                            inc(mm, "smm")
        cnt["scv"], cnt["scs"] = cv, cs_

    o_sb = sr_sb

    with contextlib.ExitStack() as es:
        ec = es.enter_context
        oh16 = [ec(nc.sbuf_tensor("oh16_%d" % i, [128, T_OWN], F16))
                for i in range(2)]
        oq16 = [ec(nc.sbuf_tensor("oq16_%d" % i, [128, T_OWN], F16))
                for i in range(2)]
        stv = ec(nc.sbuf_tensor("stv", [1, T_OWN], F32))
        stm = ec(nc.sbuf_tensor("stm", [1, T_OWN], F32))
        stq = ec(nc.sbuf_tensor("stq", [1, T_OWN], F32))
        rb2 = ec(nc.sbuf_tensor("rb2", [128, T_OWN], F32))
        m2b2 = ec(nc.sbuf_tensor("m2b2", [128, T_OWN], F32))
        ost = [ec(nc.sbuf_tensor("ost%d" % i, [128, T_OWN], F32))
               for i in range(2)]

        with contextlib.ExitStack() as esA:
            ecA = esA.enter_context
            ps1 = [ecA(nc.psum_tensor("qs1_%d" % j, [1, 512], F32))
                   for j in range(len(TC_OWN))]
            ps2 = [ecA(nc.psum_tensor("qs2_%d" % j, [1, 512], F32))
                   for j in range(len(TC_OWN))]
            blk = esA.enter_context(nc.Block())
            b = snap()

            @blk.scalar
            def _(e):
                for ci in range(CH):
                    if ci >= 2:
                        e.wait_ge(sem["smm"], b["smm"] + ci - 1)
                    e.activation(oh16[ci % 2][:, :], o_sb[:, ci, :], AF.Copy)
                    inc(e.activation(oq16[ci % 2][:, :], o_sb[:, ci, :],
                                     AF.Square), "ss")
                e.wait_ge(sem["sv"], b["sv"] + 1)
                e.activation(stv[:, :], stv[:, :], AF.Ln)
                e.drain()
                inc(e.activation(stv[:, :], stv[:, :], AF.Exp, scale=-0.5),
                    "ss")

            @blk.tensor
            def _(e):
                for ci in range(CH):
                    e.wait_ge(sem["ss"], b["ss"] + ci + 1)
                    for j, (t0, t1) in enumerate(TC_OWN):
                        e.matmul(ps1[j][:, :t1 - t0], ones16[:, :],
                                 oh16[ci % 2][:, t0:t1],
                                 start=(ci == 0), stop=(ci == CH - 1))
                        mm = e.matmul(ps2[j][:, :t1 - t0], ones16[:, :],
                                      oq16[ci % 2][:, t0:t1],
                                      start=(ci == 0), stop=(ci == CH - 1))
                    inc(mm, "smm")

            @blk.vector
            def _(e):
                e.wait_ge(sem["smm"], b["smm"] + CH)
                for j, (t0, t1) in enumerate(TC_OWN):
                    e.tensor_scalar(stm[:, t0:t1], ps1[j][:, :t1 - t0],
                                    1.0 / C, None, OP.mult)
                    e.tensor_scalar(stv[:, t0:t1], ps2[j][:, :t1 - t0],
                                    1.0 / C, None, OP.mult)
                e.drain()
                e.tensor_tensor(stq[:, :], stm[:, :], stm[:, :], OP.mult)
                e.drain()
                e.tensor_tensor(stv[:, :], stv[:, :], stq[:, :], OP.subtract)
                e.drain()
                inc(e.tensor_scalar(stv[:, :], stv[:, :], LN_EPS, None,
                                    OP.add), "sv")
                e.wait_ge(sem["ss"], b["ss"] + CH + 1)
                e.drain()
                inc(e.tensor_tensor(stm[:, :], stm[:, :], stv[:, :], OP.mult),
                    "sv")

        with contextlib.ExitStack() as esB:
            ecB = esB.enter_context
            pb1 = ecB(nc.psum_tensor("qb1", [128, 512], F32))
            pb2 = ecB(nc.psum_tensor("qb2", [128, 512], F32))
            sstg = [ecB(nc.sbuf_tensor("sstg%d" % i, [128, T_OWN], F32))
                    for i in range(2)]
            xst = [ecB(nc.sbuf_tensor("xst%d" % i, [128, T_OWN], F32))
                   for i in range(2)]
            g2ld = [ecB(nc.sbuf_tensor("g2ld%d" % i, [128, T_OWN], BF16))
                    for i in range(2)]
            outst = [ecB(nc.sbuf_tensor("outst%d" % i, [128, T_OWN], F32))
                     for i in range(2)]
            blk = esB.enter_context(nc.Block())
            b = snap()
            ntc2 = len(TC_OWN)

            @blk.tensor
            def _(e):
                for j, (t0, t1) in enumerate(TC_OWN):
                    if j >= 1:
                        e.wait_ge(sem["sev"], b["sev"] + j)
                    e.matmul(pb1[:, :t1 - t0], onesK1[:, :], stv[:, t0:t1],
                             start=True, stop=True)
                    inc(e.matmul(pb2[:, :t1 - t0], onesK1[:, :], stm[:, t0:t1],
                                 start=True, stop=True), "smm")

            @blk.sync
            def _(e):
                for ci in range(CH):
                    if ci >= 2:
                        e.wait_ge(sem["sden"], b["sden"] + ci - 1)
                    inc(e.dma_start(xst[ci % 2][:, :],
                                    x_d[ci * 128:(ci + 1) * 128,
                                        1 + LANE:T_STG]),
                        "sw%d" % (ci % 2), 16)
                    inc(e.dma_start(g2ld[ci % 2][:, :], sg2[ci, :, :]),
                        "sw%d" % (ci % 2), 16)

            @blk.scalar
            def _(e):
                for j, (t0, t1) in enumerate(TC_OWN):
                    e.wait_ge(sem["smm"], b["smm"] + j + 1)
                    e.activation(rb2[:, t0:t1], pb1[:, :t1 - t0], AF.Copy)
                    inc(e.activation(m2b2[:, t0:t1], pb2[:, :t1 - t0],
                                     AF.Copy), "sev")
                for ci in range(CH):
                    e.wait_ge(sem["sv"], b["sv"] + ci + 1)
                    if ci >= 2:
                        e.wait_ge(sem["sden"], b["sden"] + ci - 1)
                    inc(e.activation(sstg[ci % 2][:, :], ost[ci % 2][:, :],
                                     AF.Identity,
                                     bias=vec_sb["lnb"][:, ci:ci + 1],
                                     scale=vec_sb["lng"][:, ci:ci + 1]), "ss")

            def final_part(e, ci):
                swp = "sw%d" % (ci % 2)
                e.wait_ge(sem["ss"], b["ss"] + ci + 1)
                e.wait_ge(sem[swp], b[swp] + 32 * (ci // 2 + 1))
                if ci >= 2:
                    stp = "st%d" % (ci % 2)
                    e.wait_ge(sem[stp], b[stp] + 16 * (ci // 2))
                e.drain()
                e.tensor_tensor(outst[ci % 2][:, :], sstg[ci % 2][:, :],
                                g2ld[ci % 2][:, :], OP.mult)
                e.drain()
                inc(e.tensor_tensor(outst[ci % 2][:, :], outst[ci % 2][:, :],
                                    xst[ci % 2][:, :], OP.add), "sden")

            @blk.vector
            def _(e):
                e.wait_ge(sem["sev"], b["sev"] + ntc2)
                for ci in range(CH):
                    if ci >= 2:
                        e.wait_ge(sem["ss"], b["ss"] + ci - 1)
                    e.drain()
                    e.tensor_tensor(ost[ci % 2][:, :], o_sb[:, ci, :],
                                    rb2[:, :], OP.mult)
                    e.drain()
                    inc(e.tensor_tensor(ost[ci % 2][:, :], ost[ci % 2][:, :],
                                        m2b2[:, :], OP.subtract), "sv")
                    if ci >= 1:
                        final_part(e, ci - 1)
                final_part(e, CH - 1)

            @blk.gpsimd
            def _(e):
                for ci in range(CH):
                    e.wait_ge(sem["sden"], b["sden"] + ci + 1)
                    inc(e.dma_start(y_d[ci, :, :], outst[ci % 2][:, :]),
                        "st%d" % (ci % 2), 16)
                e.wait_ge(sem["st0"], cnt["st0"])
                e.wait_ge(sem["st1"], cnt["st1"])

    es_o.close()
    es0.close()
    return nc



def stage_inputs(inputs, B, T, NCHUNK, C, T_OWN, LANE):
    CH = C // 128
    T_ALL = LANE + T_OWN
    T_STG = T_ALL + 1
    x = np.ascontiguousarray(np.asarray(inputs["L0_attn_x"], np.float32))

    wmap = {"wk": "L0_attn_Wk", "wv": "L0_attn_Wv", "wr": "L0_attn_Wr",
            "wg": "L0_attn_Wg", "wo": "L0_attn_Wo", "wf": "L0_ffn_weight"}
    vmap = {"tmk": "L0_attn_tm_k", "tmv": "L0_attn_tm_v", "tmr": "L0_attn_tm_r",
            "tmg": "L0_attn_tm_g", "td": "L0_attn_time_decay",
            "tf": "L0_attn_time_first", "lng": "L0_attn_ln_g",
            "lnb": "L0_attn_ln_b", "bfv": "L0_ffn_bias"}
    wts = {k: np.ascontiguousarray(np.asarray(inputs[n], np.float32).T)
           for k, n in wmap.items()}
    vecs = {k: np.ascontiguousarray(
        np.asarray(inputs[n], np.float32).reshape(CH, 128).T)
        for k, n in vmap.items()}

    in_maps = []
    for core in range(B * NCHUNK):
        bi, ch = core // NCHUNK, core % NCHUNK
        t0 = ch * T_OWN
        stg = np.zeros((T_STG, C), np.float32)
        if ch == 0:
            stg[LANE] = x[bi, 1]
        else:
            stg[0] = x[bi, t0 - LANE - 1]
            stg[1:1 + LANE] = x[bi, t0 - LANE:t0]
        stg[1 + LANE:] = x[bi, t0:t0 + T_OWN]
        m = {"x": np.ascontiguousarray(stg.T),
             "cs": np.full((128, 1), 0.0 if ch == 0 else 1.0, np.float32)}
        m.update(wts)
        m.update(vecs)
        in_maps.append(m)
    return in_maps


_CACHE = {}


def kernel(**inputs):
    C, B, T = 2048, 2, 4096
    NCHUNK, LANE = 4, 128
    T_OWN = T // NCHUNK
    key = (C, T_OWN, LANE)
    if key not in _CACHE:
        _CACHE[key] = build_nc(C, T_OWN, LANE)
    nc = _CACHE[key]
    in_maps = stage_inputs(inputs, B, T, NCHUNK, C, T_OWN, LANE)
    res = run_bass_kernel_spmd(nc, in_maps, core_ids=list(range(B * NCHUNK)))
    out = np.empty((B, T, C), np.float32)
    for core in range(B * NCHUNK):
        bi, ch = core // NCHUNK, core % NCHUNK
        t0 = ch * T_OWN
        yc = np.asarray(res.results[core]["y"], np.float32).reshape(C, T_OWN)
        out[bi, t0:t0 + T_OWN, :] = yc.T
    return out


# version 14
# speedup vs baseline: 1.1365x; 1.0177x over previous
import contextlib

import numpy as np

import concourse.bass as bass
import concourse.mybir as mybir
from concourse.bass_utils import run_bass_kernel_spmd

F32 = mybir.dt.float32
BF16 = mybir.dt.bfloat16
F16 = mybir.dt.float16
AF = mybir.ActivationFunctionType
OP = mybir.AluOpType
AX = mybir.AxisListType

LN_EPS = 1e-5
FFN_EPS = 1e-10


def _tchunks(total, step=512):
    return [(s, min(s + step, total)) for s in range(0, total, step)]


def build_nc(C, T_OWN, LANE):
    CH = C // 128
    HC = C // 2
    OH = HC // 128
    NL = T_OWN // LANE
    T_ALL = LANE + T_OWN
    T_STG = T_ALL + 1
    TC_ALL = _tchunks(T_ALL)
    TC_OWN = _tchunks(T_OWN)
    TC_STG = _tchunks(T_STG)
    NB = 4

    nc = bass.Bass("TRN2", target_bir_lowering=False)

    x_d = nc.dram_tensor("x", [C, T_STG], F32, kind="ExternalInput").ap()
    w_d = {k: nc.dram_tensor(k, [C, C], F32, kind="ExternalInput").ap()
           for k in ("wk", "wv", "wr", "wg", "wo", "wf")}
    vnames = ("tmk", "tmv", "tmr", "tmg", "td", "tf", "lng", "lnb", "bfv")
    vec_d = {k: nc.dram_tensor(k, [128, CH], F32, kind="ExternalInput").ap()
             for k in vnames}
    cs_d = nc.dram_tensor("cs", [128, 1], F32, kind="ExternalInput").ap()
    y_d = nc.dram_tensor("y", [CH, 128, T_OWN], F32, kind="ExternalOutput").ap()

    sxk = nc.dram_tensor("sxk", [CH, 128, T_ALL], BF16).ap()
    sxv = nc.dram_tensor("sxv", [CH, 128, T_ALL], BF16).ap()
    sxr = nc.dram_tensor("sxr", [CH, 128, T_OWN], BF16).ap()
    sxg = nc.dram_tensor("sxg", [CH, 128, T_OWN], BF16).ap()
    sek = nc.dram_tensor("sek", [CH, 128, T_ALL], F16).ap()
    svd = nc.dram_tensor("svd", [CH, 128, T_ALL], F16).ap()
    sg2 = nc.dram_tensor("sg2", [CH, 128, T_OWN], BF16).ap()

    es0 = contextlib.ExitStack()
    e0 = es0.enter_context

    sem = {n: e0(nc.semaphore(n)) for n in
           ("sl", "sw0", "sw1", "scv", "scs", "smm", "sev", "st0", "st1",
            "sv", "ss", "sden", "srd", "sg0", "sg1")}
    cnt = {n: 0 for n in sem}

    def snap():
        return dict(cnt)

    def inc(instr, name, amt=1):
        instr.then_inc(sem[name], amt)
        cnt[name] += amt

    vec_sb = {k: e0(nc.sbuf_tensor("v_" + k, [128, CH], F32)) for k in vnames}
    cs_sb = e0(nc.sbuf_tensor("cs_sb", [128, 1], F32))
    w_sb = e0(nc.sbuf_tensor("w_sb", [128, CH], F32))
    eu_sb = e0(nc.sbuf_tensor("eu_sb", [128, CH], F32))
    ones16 = e0(nc.sbuf_tensor("ones16", [128, 1], F16))
    onesK1 = e0(nc.sbuf_tensor("onesK1", [1, 128], F32))
    scale_sb = e0(nc.sbuf_tensor("scale_sb", [128, 1], F32))
    thr_sb = e0(nc.sbuf_tensor("thr_sb", [128, 1], F32))
    nthr_sb = e0(nc.sbuf_tensor("nthr_sb", [128, 1], F32))

    with nc.Block() as blk:
        b = snap()

        @blk.sync
        def _(e):
            for k in vnames:
                inc(e.dma_start(vec_sb[k][:, :], vec_d[k][:, :]), "sl", 16)
            inc(e.dma_start(cs_sb[:, :], cs_d[:, :]), "sl", 16)

        @blk.vector
        def _(e):
            e.memset(ones16[:, :], 1.0)
            inc(e.memset(onesK1[:, :], 1.0), "sv")

        @blk.scalar
        def _(e):
            e.wait_ge(sem["sl"], b["sl"] + 16 * (len(vnames) + 1))
            e.activation(w_sb[:, :], vec_sb["td"][:, :], AF.Exp)
            e.drain()
            e.activation(w_sb[:, :], w_sb[:, :], AF.Exp, scale=-1.0)
            e.drain()
            inc(e.activation(eu_sb[:, :], vec_sb["tf"][:, :], AF.Exp), "ss")

    group_id = [0]

    def mm_group(specs, shared, pre=None, tail_sync=None, tail_scalar=None):
        group_id[0] += 1
        P = len(specs)
        ntc = [len(s["tcs"]) for s in specs]
        ngh = [OH * n for n in ntc]
        GRP0 = [0] * P
        for p in range(1, P):
            GRP0[p] = GRP0[p - 1] + 2 * ngh[p - 1]
        NG = GRP0[-1] + 2 * ngh[-1]

        def g_info(g):
            p = 0
            while p + 1 < P and g >= GRP0[p + 1]:
                p += 1
            r = g - GRP0[p]
            h = r // ngh[p]
            r2 = r % ngh[p]
            oc = r2 // ntc[p]
            tj = r2 % ntc[p]
            return p, h, oc, tj

        NH = 2 * P
        GEND = [0] * NH
        for hh in range(NH):
            p, h = hh // 2, hh % 2
            GEND[hh] = GRP0[p] + (h + 1) * ngh[p]
        release = [0 if hh < 2 else GEND[hh - 2] for hh in range(NH)]

        preL = pre["preL"] if pre else 0
        castrec = list(pre["recs"]) if pre else []
        cv = cnt["scv"]
        cs_ = cnt["scs"]
        b0scv, b0scs = cnt["scv"], cnt["scs"]
        CV = [0] * NH
        CS = [0] * NH
        for hh in range(NH):
            p = hh // 2
            for ic in range(CH):
                jj = hh * CH + ic
                if jj < preL:
                    continue
                if specs[p]["quant"] or ic % 2 == 0:
                    cv += 1
                    castrec.append(("V", "scv", cv))
                else:
                    cs_ += 1
                    castrec.append(("S", "scs", cs_))
            CV[hh], CS[hh] = cv, cs_

        movpre = pre.get("mov_sl", {}) if pre else {}
        slt = {}
        run_sl = cnt["sl"]
        for p, s in enumerate(specs):
            if p in movpre:
                slt[p] = movpre[p]
            elif s["mov_load"] is not None:
                run_sl += 16 * CH
                slt[p] = run_sl

        signum = {}
        k = 0
        for g in range(NG):
            p, h, oc, tj = g_info(g)
            if specs[p]["evac"] in ("sig_attn", "sig_y_res"):
                signum[g] = k
                k += 1
        nsig = k
        rownum = {}
        g2num = {}
        sk = 0
        for p, s in enumerate(specs):
            if s["evac"] in ("exp_row", "copy_row"):
                for ocg in range(CH):
                    rownum[(p, ocg)] = sk
                    sk += 1
        for g in range(NG):
            p, h, oc, tj = g_info(g)
            if specs[p]["evac"] == "silu_g2":
                g2num[g] = sk
                sk += 1

        b = snap()
        NBK = len(shared["ps"])
        wbf = shared["wbf"]
        mwstg = shared["mwstg"]
        mwm2 = shared["mwm2"]
        rowst = shared["rowst"]
        sigst = shared["sigst"]
        g2st = shared["g2st"]
        ps = shared["ps"]
        blk = shared["es"].enter_context(nc.Block())

        mov_at = {}
        for p, s in enumerate(specs):
            if s["mov_load"] is not None and p not in movpre:
                mov_at.setdefault(GRP0[p], []).append(p)

        @blk.sync
        def _(e):
            for p, s in enumerate(specs):
                for h in range(2):
                    hh = 2 * p + h
                    for ic in range(CH):
                        jj = hh * CH + ic
                        if jj < preL:
                            continue
                        if jj >= 2:
                            eng, sname, tgt = castrec[jj - 2]
                            e.wait_ge(sem[sname], tgt)
                        inc(e.dma_start(mwstg[jj % 2][:, :],
                                        s["wt"][ic * 128:(ic + 1) * 128,
                                                (h * HC):(h + 1) * HC]),
                            "sw%d" % (jj % 2), 16)
            if tail_sync is not None:
                tail_sync(e)

        swtot = [b["sw0"], b["sw1"]]
        swtarget = [0] * (NH * CH)
        for jj in range(NH * CH):
            if jj < preL:
                continue
            swtot[jj % 2] += 16
            swtarget[jj] = swtot[jj % 2]
        cnt["sw0"] = swtot[0]
        cnt["sw1"] = swtot[1]

        def emit_cast(e, jj):
            hh, ic = jj // CH, jj % CH
            p, h = hh // 2, hh % 2
            s = specs[p]
            if release[hh] > 0:
                e.wait_ge(sem["smm"], b["smm"] + release[hh])
            e.wait_ge(sem["sw%d" % (jj % 2)], swtarget[jj])
            dstsl = wbf[hh % 2][:, ic, :]
            if s["quant"]:
                e.tensor_single_scalar(mwm2[jj % 2][:, :], mwstg[jj % 2][:, :],
                                       thr_sb[:, :], OP.is_gt)
                e.tensor_single_scalar(mwstg[jj % 2][:, :], mwstg[jj % 2][:, :],
                                       nthr_sb[:, :], OP.is_lt)
                e.drain()
                e.tensor_tensor(dstsl, mwm2[jj % 2][:, :], mwstg[jj % 2][:, :],
                                OP.subtract).then_inc(sem["scv"], 1)
            elif jj % 1 == 0 and castrec[jj][0] == "V":
                e.tensor_copy(dstsl, mwstg[jj % 2][:, :]).then_inc(sem["scv"], 1)
            else:
                e.activation(dstsl, mwstg[jj % 2][:, :],
                             AF.Copy).then_inc(sem["scs"], 1)

        v_casts_at = {}
        s_casts_at = {}
        for jj in range(preL, NH * CH):
            r = release[jj // CH]
            (v_casts_at if castrec[jj][0] == "V" else s_casts_at
             ).setdefault(r, []).append(jj)

        @blk.vector
        def _(e):
            prev_evac = None
            for g in range(NG):
                for jj in v_casts_at.get(g, []):
                    emit_cast(e, jj)
                p, h, oc, tj = g_info(g)
                s = specs[p]
                if s["evac"] in ("sig_attn", "sig_y_res"):
                    t0, t1 = s["tcs"][tj]
                    ocg = h * OH + oc
                    k = signum[g]
                    if s["evac"] != prev_evac:
                        e.drain()
                        prev_evac = s["evac"]
                    e.wait_ge(sem["sev"], b["sev"] + g + 1)
                    if s["evac"] == "sig_attn":
                        inc(e.tensor_tensor(s["wkv_bf"][:, ocg, t0:t1],
                                            sigst[k % 2][:, :t1 - t0],
                                            s["wkv_bf"][:, ocg, t0:t1],
                                            OP.mult), "sv")
                    else:
                        inc(e.tensor_tensor(s["y_dst"][:, ocg, t0:t1],
                                            sigst[k % 2][:, :t1 - t0],
                                            s["wkv_bf"][:, ocg, t0:t1],
                                            OP.mult), "sv")
            for jj in v_casts_at.get(NG, []):
                emit_cast(e, jj)

        @blk.scalar
        def _(e):
            for g in range(NG):
                for mp in mov_at.get(g, []):
                    for ci in range(CH):
                        inc(e.dma_start(specs[mp]["mov"][:, ci, :],
                                        specs[mp]["mov_load"][ci, :, :]),
                            "sl", 16)
                for jj in s_casts_at.get(g, []):
                    emit_cast(e, jj)
                p, h, oc, tj = g_info(g)
                s = specs[p]
                t0, t1 = s["tcs"][tj]
                ocg = h * OH + oc
                bank = ps[g % NBK]
                e.wait_ge(sem["smm"], b["smm"] + g + 1)
                ev = s["evac"]
                if ev in ("exp_row", "copy_row"):
                    ro = rownum[(p, ocg)]
                    if tj == 0 and ro >= 2:
                        stp = "st%d" % (ro % 2)
                        e.wait_ge(sem[stp], b[stp] + 16 * (ro // 2))
                    fn = AF.Exp if ev == "exp_row" else AF.Copy
                    inc(e.activation(rowst[ro % 2][:, t0:t1],
                                     bank[:, :t1 - t0], fn), "sev")
                elif ev == "silu_g2":
                    gk = g2num[g]
                    if gk >= 2:
                        stp = "st%d" % (gk % 2)
                        e.wait_ge(sem[stp], b[stp] + 16 * (gk // 2))
                    inc(e.activation(g2st[gk % 2][:, :t1 - t0],
                                     bank[:, :t1 - t0], AF.Silu,
                                     bias=s["bias_vec"][:, ocg:ocg + 1],
                                     scale=scale_sb[:, 0:1]), "sev")
                elif ev in ("sig_attn", "sig_y_res"):
                    k = signum[g]
                    if k >= 2:
                        e.wait_ge(sem["sv"], b["sv"] + k - 1)
                    inc(e.activation(sigst[k % 2][:, :t1 - t0],
                                     bank[:, :t1 - t0], AF.Sigmoid), "sev")
                elif ev == "copy_o":
                    inc(e.activation(s["dst"][:, ocg, t0:t1],
                                     bank[:, :t1 - t0], AF.Copy), "sev")
            for jj in s_casts_at.get(NG, []):
                emit_cast(e, jj)
            if tail_scalar is not None:
                tail_scalar(e)

        @blk.gpsimd
        def _(e):
            any_store = False
            for p, s in enumerate(specs):
                if s["evac"] in ("exp_row", "copy_row"):
                    any_store = True
                    for ocg in range(CH):
                        ro = rownum[(p, ocg)]
                        e.wait_ge(sem["sev"],
                                  b["sev"] + GRP0[p] + (ocg + 1) * ntc[p])
                        inc(e.dma_start(s["store_ap"][ocg, :, :],
                                        rowst[ro % 2][:, :]),
                            "st%d" % (ro % 2), 16)
                elif s["evac"] == "silu_g2":
                    any_store = True
                    for g in range(GRP0[p], GRP0[p] + 2 * ngh[p]):
                        _p, h, oc, tj = g_info(g)
                        t0, t1 = s["tcs"][tj]
                        ocg = h * OH + oc
                        gk = g2num[g]
                        e.wait_ge(sem["sev"], b["sev"] + g + 1)
                        inc(e.dma_start(s["store_ap"][ocg, :, t0:t1],
                                        g2st[gk % 2][:, :t1 - t0]),
                            "st%d" % (gk % 2), 16)
            if any_store:
                e.wait_ge(sem["st0"], cnt["st0"])
                e.wait_ge(sem["st1"], cnt["st1"])

        @blk.tensor
        def _(e):
            for p, s in enumerate(specs):
                if s["mov_load"] is not None:
                    e.wait_ge(sem["sl"], slt[p])
                if s.get("wait_y"):
                    e.wait_ge(sem["sv"], b["sv"] + nsig)
                for h in range(2):
                    hh = 2 * p + h
                    e.wait_ge(sem["scv"], CV[hh])
                    if CS[hh] > b0scs:
                        e.wait_ge(sem["scs"], CS[hh])
                    for oc in range(OH):
                        for tj, (t0, t1) in enumerate(s["tcs"]):
                            g = GRP0[p] + h * ngh[p] + oc * ntc[p] + tj
                            bank = ps[g % NBK]
                            if g >= NBK:
                                e.wait_ge(sem["sev"], b["sev"] + g - NBK + 1)
                            for ic in range(CH):
                                mm = e.matmul(bank[:, :t1 - t0],
                                              wbf[hh % 2][:, ic,
                                                          oc * 128:(oc + 1) * 128],
                                              s["mov"][:, ic, t0:t1],
                                              start=(ic == 0),
                                              stop=(ic == CH - 1))
                            inc(mm, "smm")
        cnt["scv"] = cv
        cnt["scs"] = cs_

    def spec(wt, tcs, evac, mov, mov_load=None, quant=False, store_ap=None,
             bias_vec=None, dst=None, wkv_bf=None, y_dst=None, wait_y=False):
        return dict(wt=wt, tcs=tcs, evac=evac, mov=mov, mov_load=mov_load,
                    quant=quant, store_ap=store_ap, bias_vec=bias_vec,
                    dst=dst, wkv_bf=wkv_bf, y_dst=y_dst, wait_y=wait_y)

    def make_shared(es, tag, n_tok, want_quant, want_row, want_sig, want_g2,
                    nps=8):
        ec = es.enter_context
        sh = {"es": es}
        sh["wbf"] = [ec(nc.sbuf_tensor(tag + "wbf%d" % i, [128, CH, HC], BF16))
                     for i in range(2)]
        sh["mwstg"] = [ec(nc.sbuf_tensor(tag + "mwstg%d" % i, [128, HC], F32))
                       for i in range(2)]
        sh["mwm2"] = ([ec(nc.sbuf_tensor(tag + "mwm2%d" % i, [128, HC], F32))
                       for i in range(2)] if want_quant else None)
        sh["rowst"] = ([ec(nc.sbuf_tensor(tag + "row%d" % i, [128, n_tok], F32))
                        for i in range(2)] if want_row else None)
        sh["sigst"] = ([ec(nc.sbuf_tensor(tag + "sig%d" % i, [128, 512], F32))
                        for i in range(2)] if want_sig else None)
        sh["g2st"] = ([ec(nc.sbuf_tensor(tag + "g2s%d" % i, [128, 512], BF16))
                       for i in range(2)] if want_g2 else None)
        sh["ps"] = [ec(nc.psum_tensor(tag + "ps%d" % i, [128, 512], F32))
                    for i in range(nps)]
        return sh

    es1 = contextlib.ExitStack()
    xn_sb = es1.enter_context(nc.sbuf_tensor("xn_sb", [128, CH, T_OWN], BF16))

    es_pre = contextlib.ExitStack()
    g1wbf0 = es_pre.enter_context(
        nc.sbuf_tensor("g1wbf0", [128, CH, HC], BF16))
    g1mwstg = [es_pre.enter_context(
        nc.sbuf_tensor("g1mwstg%d" % i, [128, HC], F32)) for i in range(2)]
    g1_pre_recs = []

    with contextlib.ExitStack() as es:
        ec = es.enter_context
        xh16 = ec(nc.sbuf_tensor("xh16", [128, CH, T_STG], F16))
        xstg = [ec(nc.sbuf_tensor("xstg%d" % i, [128, T_STG], F32))
                for i in range(2)]
        xq16 = [ec(nc.sbuf_tensor("xq16_%d" % i, [128, T_STG], F16))
                for i in range(2)]
        stat = ec(nc.sbuf_tensor("stat", [1, T_STG], F32))
        stat2 = ec(nc.sbuf_tensor("stat2", [1, T_STG], F32))
        stat3 = ec(nc.sbuf_tensor("stat3", [1, T_STG], F32))
        rb_sb = ec(nc.sbuf_tensor("rb_sb", [128, T_STG], F32))
        m2b_sb = ec(nc.sbuf_tensor("m2b_sb", [128, T_STG], F32))
        xnf = ec(nc.sbuf_tensor("xnf", [128, T_STG], F32))
        xxf = ec(nc.sbuf_tensor("xxf", [128, T_ALL], F32))
        xn_stg = [ec(nc.sbuf_tensor("xn_stg%d" % i, [128, T_OWN], BF16))
                  for i in range(2)]
        ik_stg = [ec(nc.sbuf_tensor("ik_stg%d" % i, [128, T_ALL], BF16))
                  for i in range(2)]
        iv_stg = [ec(nc.sbuf_tensor("iv_stg%d" % i, [128, T_ALL], BF16))
                  for i in range(2)]
        ir_stg = [ec(nc.sbuf_tensor("ir_stg%d" % i, [128, T_OWN], BF16))
                  for i in range(2)]
        ig_stg = [ec(nc.sbuf_tensor("ig_stg%d" % i, [128, T_OWN], BF16))
                  for i in range(2)]

        with contextlib.ExitStack() as esA:
            ecA = esA.enter_context
            ps1 = [ecA(nc.psum_tensor("ps1_%d" % j, [1, 512], F32))
                   for j in range(len(TC_STG))]
            ps2 = [ecA(nc.psum_tensor("ps2_%d" % j, [1, 512], F32))
                   for j in range(len(TC_STG))]
            blk = esA.enter_context(nc.Block())
            b = snap()

            @blk.sync
            def _(e):
                for ci in range(CH):
                    if ci >= 2:
                        e.wait_ge(sem["ss"], b["ss"] + ci - 1)
                    inc(e.dma_start(xstg[ci % 2][:, :],
                                    x_d[ci * 128:(ci + 1) * 128, :]),
                        "sw%d" % (ci % 2), 16)

            @blk.scalar
            def _(e):
                for ci in range(CH):
                    swp = "sw%d" % (ci % 2)
                    e.wait_ge(sem[swp], b[swp] + 16 * (ci // 2 + 1))
                    if ci >= 2:
                        e.wait_ge(sem["smm"], b["smm"] + ci - 1)
                    e.activation(xh16[:, ci, :], xstg[ci % 2][:, :], AF.Copy)
                    inc(e.activation(xq16[ci % 2][:, :], xstg[ci % 2][:, :],
                                     AF.Square), "ss")
                e.wait_ge(sem["sv"], b["sv"] + 1)
                e.activation(stat[:, :], stat[:, :], AF.Ln)
                e.drain()
                inc(e.activation(stat[:, :], stat[:, :], AF.Exp, scale=-0.5),
                    "ss")

            @blk.tensor
            def _(e):
                for ci in range(CH):
                    e.wait_ge(sem["ss"], b["ss"] + ci + 1)
                    for j, (t0, t1) in enumerate(TC_STG):
                        e.matmul(ps1[j][:, :t1 - t0], ones16[:, :],
                                 xh16[:, ci, t0:t1],
                                 start=(ci == 0), stop=(ci == CH - 1))
                        mm = e.matmul(ps2[j][:, :t1 - t0], ones16[:, :],
                                      xq16[ci % 2][:, t0:t1],
                                      start=(ci == 0), stop=(ci == CH - 1))
                    inc(mm, "smm")

            @blk.vector
            def _(e):
                e.wait_ge(sem["smm"], b["smm"] + CH)
                for j, (t0, t1) in enumerate(TC_STG):
                    e.tensor_scalar(stat2[:, t0:t1], ps1[j][:, :t1 - t0],
                                    1.0 / C, None, OP.mult)
                    e.tensor_scalar(stat[:, t0:t1], ps2[j][:, :t1 - t0],
                                    1.0 / C, None, OP.mult)
                e.drain()
                e.tensor_tensor(stat3[:, :], stat2[:, :], stat2[:, :], OP.mult)
                e.drain()
                e.tensor_tensor(stat[:, :], stat[:, :], stat3[:, :], OP.subtract)
                e.drain()
                inc(e.tensor_scalar(stat[:, :], stat[:, :], LN_EPS, None,
                                    OP.add), "sv")
                e.wait_ge(sem["ss"], b["ss"] + CH + 1)
                e.drain()
                inc(e.tensor_tensor(stat2[:, :], stat2[:, :], stat[:, :],
                                    OP.mult), "sv")

        with contextlib.ExitStack() as esB:
            ecB = esB.enter_context
            pb1 = ecB(nc.psum_tensor("pb1", [128, 512], F32))
            pb2 = ecB(nc.psum_tensor("pb2", [128, 512], F32))
            SCW = min(512, HC)
            ps_sc = ecB(nc.psum_tensor("ps_sc", [1, SCW], F32))
            ps_bc = ecB(nc.psum_tensor("ps_bc", [128, 4], F32))
            habs = [ecB(nc.sbuf_tensor("fs_a%d" % i, [128, HC], F16))
                    for i in range(2)]
            sctile = ecB(nc.sbuf_tensor("sctile", [1, SCW], F32))
            sctiny = ecB(nc.sbuf_tensor("sctiny", [1, 4], F32))
            blk = esB.enter_context(nc.Block())
            b = snap()
            QC = _tchunks(HC)
            nbc = len(TC_STG)
            NSL = 2 * CH
            NSLK = NSL + CH

            @blk.tensor
            def _(e):
                for j, (t0, t1) in enumerate(TC_STG):
                    if j >= 1:
                        e.wait_ge(sem["sev"], b["sev"] + j)
                    e.matmul(pb1[:, :t1 - t0], onesK1[:, :], stat[:, t0:t1],
                             start=True, stop=True)
                    inc(e.matmul(pb2[:, :t1 - t0], onesK1[:, :],
                                 stat2[:, t0:t1], start=True, stop=True), "smm")
                for sl_ in range(NSL):
                    e.wait_ge(sem["scv"], b["scv"] + sl_ + 1)
                    for j, (q0, q1) in enumerate(QC):
                        mm = e.matmul(ps_sc[:, :q1 - q0], ones16[:, :],
                                      habs[sl_ % 2][:, q0:q1],
                                      start=(sl_ == 0 and j == 0),
                                      stop=(sl_ == NSL - 1 and j == len(QC) - 1))
                    inc(mm, "smm")
                e.wait_ge(sem["srd"], b["srd"] + 1)
                e.matmul(ps_bc[:, 0:1], onesK1[:, :], sctiny[:, 0:1],
                         start=True, stop=True)
                e.matmul(ps_bc[:, 1:2], onesK1[:, :], sctiny[:, 1:2],
                         start=True, stop=True)
                inc(e.matmul(ps_bc[:, 2:3], onesK1[:, :], sctiny[:, 2:3],
                             start=True, stop=True), "smm")

            @blk.sync
            def _(e):
                for sl_ in range(NSLK):
                    if sl_ >= 2:
                        if sl_ - 2 < NSL:
                            e.wait_ge(sem["scv"], b["scv"] + sl_ - 1)
                        else:
                            e.wait_ge(sem["scs"],
                                      b["scs"] + (sl_ - 2 - NSL) + 1)
                    if sl_ < NSL:
                        ic, h = sl_ // 2, sl_ % 2
                        src = w_d["wf"][ic * 128:(ic + 1) * 128,
                                        h * HC:(h + 1) * HC]
                    else:
                        ic = sl_ - NSL
                        src = w_d["wk"][ic * 128:(ic + 1) * 128, 0:HC]
                    inc(e.dma_start(g1mwstg[sl_ % 2][:, :], src),
                        "sw%d" % (sl_ % 2), 16)

            @blk.scalar
            def _(e):
                for j, (t0, t1) in enumerate(TC_STG):
                    e.wait_ge(sem["smm"], b["smm"] + j + 1)
                    e.activation(rb_sb[:, t0:t1], pb1[:, :t1 - t0], AF.Copy)
                    inc(e.activation(m2b_sb[:, t0:t1], pb2[:, :t1 - t0],
                                     AF.Copy), "sev")
                for sl_ in range(NSLK):
                    swp = "sw%d" % (sl_ % 2)
                    e.wait_ge(sem[swp], b[swp] + 16 * (sl_ // 2 + 1))
                    if sl_ < NSL:
                        if sl_ >= 2:
                            e.wait_ge(sem["smm"], b["smm"] + nbc + sl_ - 1)
                        inc(e.activation(habs[sl_ % 2][:, :],
                                         g1mwstg[sl_ % 2][:, :], AF.Abs),
                            "scv")
                    else:
                        ic = sl_ - NSL
                        inc(e.activation(g1wbf0[:, ic, :],
                                         g1mwstg[sl_ % 2][:, :], AF.Copy),
                            "scs")
                        g1_pre_recs.append(("S", "scs", cnt["scs"]))
                e.wait_ge(sem["smm"], b["smm"] + nbc + NSL + 1)
                e.activation(scale_sb[:, :], ps_bc[:, 0:1], AF.Copy)
                e.activation(thr_sb[:, :], ps_bc[:, 1:2], AF.Copy)
                e.activation(nthr_sb[:, :], ps_bc[:, 2:3], AF.Copy)

            @blk.vector
            def _(e):
                e.wait_ge(sem["sev"], b["sev"] + len(TC_STG))
                for ci in range(CH):
                    if ci >= 2:
                        stp = "st%d" % (ci % 2)
                        e.wait_ge(sem[stp], b[stp] + 80 * (ci // 2))
                    s = ci % 2
                    e.drain()
                    e.tensor_tensor(xnf[:, :], xh16[:, ci, :], rb_sb[:, :],
                                    OP.mult)
                    e.drain()
                    e.tensor_tensor(xnf[:, :], xnf[:, :], m2b_sb[:, :],
                                    OP.subtract)
                    e.drain()
                    e.tensor_tensor(xxf[:, :], xnf[:, 0:T_ALL], xnf[:, 1:T_STG],
                                    OP.subtract)
                    e.drain()
                    e.scalar_tensor_tensor(ik_stg[s][:, :], xxf[:, :],
                                           vec_sb["tmk"][:, ci:ci + 1],
                                           xnf[:, 1:T_STG], OP.mult, OP.add)
                    e.scalar_tensor_tensor(iv_stg[s][:, :], xxf[:, :],
                                           vec_sb["tmv"][:, ci:ci + 1],
                                           xnf[:, 1:T_STG], OP.mult, OP.add)
                    e.scalar_tensor_tensor(ir_stg[s][:, :], xxf[:, LANE:],
                                           vec_sb["tmr"][:, ci:ci + 1],
                                           xnf[:, 1 + LANE:T_STG], OP.mult,
                                           OP.add)
                    e.scalar_tensor_tensor(ig_stg[s][:, :], xxf[:, LANE:],
                                           vec_sb["tmg"][:, ci:ci + 1],
                                           xnf[:, 1 + LANE:T_STG], OP.mult,
                                           OP.add)
                    inc(e.tensor_copy(xn_stg[s][:, :], xnf[:, 1 + LANE:T_STG]),
                        "sv")
                e.wait_ge(sem["smm"], b["smm"] + nbc + NSL)
                e.tensor_copy(sctile[:, :], ps_sc[:, :])
                e.drain()
                e.reduce_sum(sctiny[:, 3:4], sctile[:, :], axis=AX.X)
                e.drain()
                e.tensor_scalar(sctiny[:, 0:1], sctiny[:, 3:4], 1.0 / (C * C),
                                None, OP.mult)
                e.drain()
                e.tensor_scalar(sctiny[:, 1:2], sctiny[:, 0:1], FFN_EPS, 0.5,
                                OP.add, OP.mult)
                e.drain()
                inc(e.tensor_scalar(sctiny[:, 2:3], sctiny[:, 1:2], -1.0, None,
                                    OP.mult), "srd")

            @blk.gpsimd
            def _(e):
                for ci in range(CH):
                    e.wait_ge(sem["sv"], b["sv"] + ci + 1)
                    s = ci % 2
                    stp = "st%d" % s
                    inc(e.dma_start(sxk[ci, :, :], ik_stg[s][:, :]), stp, 16)
                    inc(e.dma_start(sxv[ci, :, :], iv_stg[s][:, :]), stp, 16)
                    inc(e.dma_start(sxr[ci, :, :], ir_stg[s][:, :]), stp, 16)
                    inc(e.dma_start(sxg[ci, :, :], ig_stg[s][:, :]), stp, 16)
                    inc(e.dma_start(xn_sb[:, ci, :], xn_stg[s][:, :]), stp, 16)
                e.wait_ge(sem["st0"], cnt["st0"])
                e.wait_ge(sem["st1"], cnt["st1"])

    with contextlib.ExitStack() as es:
        ec = es.enter_context
        mov_kv = ec(nc.sbuf_tensor("mov_kv", [128, CH, T_ALL], BF16))
        sh = {"es": es,
              "wbf": [g1wbf0,
                      ec(nc.sbuf_tensor("g1wbf1", [128, CH, HC], BF16))],
              "mwstg": g1mwstg,
              "mwm2": [ec(nc.sbuf_tensor("g1mwm2%d" % i, [128, HC], F32))
                       for i in range(2)],
              "rowst": [ec(nc.sbuf_tensor("g1row%d" % i, [128, T_ALL], F16))
                        for i in range(2)],
              "sigst": None,
              "g2st": [ec(nc.sbuf_tensor("g1g2s%d" % i, [128, 512], BF16))
                       for i in range(2)],
              "ps": [ec(nc.psum_tensor("g1ps%d" % i, [128, 512], F32))
                     for i in range(8)]}
        mm_group([
            spec(w_d["wk"], TC_ALL, "exp_row", mov_kv, mov_load=sxk,
                 store_ap=sek),
            spec(w_d["wv"], TC_ALL, "copy_row", mov_kv, mov_load=sxv,
                 store_ap=svd),
            spec(w_d["wf"], TC_OWN, "silu_g2", xn_sb, quant=True,
                 bias_vec=vec_sb["bfv"], store_ap=sg2),
        ], sh, pre=dict(preL=CH, recs=g1_pre_recs))
    es_pre.close()
    es1.close()

    es_o = contextlib.ExitStack()
    sr_sb = es_o.enter_context(
        nc.sbuf_tensor("sr_sb", [128, CH, T_OWN], BF16, side="right"))

    with contextlib.ExitStack() as es:
        ec = es.enter_context
        wkv_sb = ec(nc.sbuf_tensor("wkv_sb", [128, CH, T_OWN], BF16))
        mov_rg = ec(nc.sbuf_tensor("mov_rg", [128, CH, T_OWN], BF16))
        SW = min(512, C)
        NSLAB = C // SW
        OSL = SW // 128
        wbf = [ec(nc.sbuf_tensor("rgo_wbf%d" % i, [128, CH, SW], BF16))
               for i in range(2)]
        mwstg = [ec(nc.sbuf_tensor("rgo_mw%d" % i, [128, SW], F32))
                 for i in range(2)]
        NSG = 8
        sgst = [ec(nc.sbuf_tensor("rgo_sg%d" % i, [128, 512], BF16))
                for i in range(NSG)]
        NROW = 4
        ekr = [ec(nc.sbuf_tensor("r_ekr%d" % i, [128, T_ALL], F16))
               for i in range(NROW)]
        vvr = [ec(nc.sbuf_tensor("r_vvr%d" % i, [128, T_ALL], F16))
               for i in range(NROW)]
        zar = ec(nc.sbuf_tensor("zar", [128, 1 + T_ALL], F32))
        zbr = ec(nc.sbuf_tensor("zbr", [128, 1 + T_ALL], F32))
        ps = [ec(nc.psum_tensor("rgo_ps%d" % i, [128, 512], F32))
              for i in range(8)]
        blk = es.enter_context(nc.Block())
        b = snap()
        NBK = len(ps)
        cnt_scv0 = cnt["scv"]

        ntc = len(TC_OWN)
        NGp = NSLAB * OSL * ntc
        NR, NRG, NG = NGp, 2 * NGp, 3 * NGp
        phs = [dict(wt=w_d["wr"], mov=mov_rg, mov_load=sxr),
               dict(wt=w_d["wg"], mov=mov_rg, mov_load=sxg),
               dict(wt=w_d["wo"], mov=wkv_sb, mov_load=None)]

        def g_info(g):
            p, r = g // NGp, g % NGp
            return p, r // (OSL * ntc), (r // ntc) % OSL, r % ntc

        NSS = 3 * NSLAB
        GENDs = [(ss + 1) * OSL * ntc for ss in range(NSS)]
        release = [0 if ss < 2 else GENDs[ss - 2] for ss in range(NSS)]

        castrec = []
        cv, cs_ = cnt["scv"], cnt["scs"]
        CVs, CSs = [0] * NSS, [0] * NSS
        for ss in range(NSS):
            for ic in range(CH):
                cs_ += 1
                castrec.append(("S", "scs", cs_))
            CVs[ss], CSs[ss] = cv, cs_

        v_events = []
        for i in range(CH + NROW):
            cj = i - NROW
            if 0 <= cj < CH:
                v_events.append(("q", cj))
                v_events.append(("sr", cj))
            if i < CH:
                v_events.append(("chain", i))
        for k in range(NGp):
            v_events.append(("gmul", k))

        svord = {}
        i = 0
        for kind, l in v_events:
            if kind != "chain":
                i += 1
                svord[(kind, l)] = i
        NSV = i
        RP = {ci: min(4 + 2 * ci, NRG - 1) for ci in range(CH)}

        swtot = [b["sw0"], b["sw1"]]
        swtarget = [0] * (NSS * CH)
        for jj in range(NSS * CH):
            swtot[jj % 2] += 16
            swtarget[jj] = swtot[jj % 2]
        cnt["sw0"], cnt["sw1"] = swtot[0], swtot[1]

        sttot = [b["st0"], b["st1"]]
        st_row = {}
        for ci in range(CH):
            sttot[ci % 2] += 32
            st_row[ci] = sttot[ci % 2]
        cnt["st0"], cnt["st1"] = sttot[0], sttot[1]

        def emit_cast(e, jj):
            ss, ic = jj // CH, jj % CH
            if release[ss] > 0:
                e.wait_ge(sem["smm"], b["smm"] + release[ss])
            e.wait_ge(sem["sw%d" % (jj % 2)], swtarget[jj])
            if castrec[jj][0] == "V":
                e.tensor_copy(wbf[ss % 2][:, ic, :],
                              mwstg[jj % 2][:, :]).then_inc(sem["scv"], 1)
            else:
                e.activation(wbf[ss % 2][:, ic, :], mwstg[jj % 2][:, :],
                             AF.Copy).then_inc(sem["scs"], 1)

        v_casts_at, s_casts_at = {}, {}
        for jj in range(NSS * CH):
            (v_casts_at if castrec[jj][0] == "V" else s_casts_at
             ).setdefault(release[jj // CH], []).append(jj)

        @blk.sync
        def _(e):
            for jj in range(NSS * CH):
                ss, ic = jj // CH, jj % CH
                p, sl = ss // NSLAB, ss % NSLAB
                if jj >= 2:
                    eng, sname, tgt = castrec[jj - 2]
                    e.wait_ge(sem[sname], tgt)
                e.dma_start(mwstg[jj % 2][:, :],
                            phs[p]["wt"][ic * 128:(ic + 1) * 128,
                                         sl * SW:(sl + 1) * SW]
                            ).then_inc(sem["sw%d" % (jj % 2)], 16)

        @blk.gpsimd
        def _(e):
            for ci in range(CH):
                if ci >= NROW:
                    e.wait_ge(sem["sv"], b["sv"] + svord[("q", ci - NROW)])
                elif ci >= 2:
                    e.wait_ge(sem["sden"], b["sden"] + ci - 1)
                stp = "st%d" % (ci % 2)
                e.dma_start(ekr[ci % NROW][:, :],
                            sek[ci, :, :]).then_inc(sem[stp], 16)
                e.dma_start(vvr[ci % NROW][:, :],
                            svd[ci, :, :]).then_inc(sem[stp], 16)

        @blk.vector
        def _(e):
            e.memset(zar[:, 0:1], 0.0)
            e.memset(zbr[:, 0:1], 0.0)
            e.drain()
            for kind, l in v_events:
                if kind == "chain":
                    m = l % NROW
                    e.wait_ge(sem["st%d" % (l % 2)], st_row[l])
                    e.drain()
                    e.tensor_scalar(ekr[m][:, 0:LANE], ekr[m][:, 0:LANE],
                                    cs_sb[:, 0:1], None, OP.mult)
                    e.drain()
                    e.tensor_tensor(vvr[m][:, :], ekr[m][:, :],
                                    vvr[m][:, :], OP.mult)
                    e.drain()
                    wbc = w_sb[:, l:l + 1].broadcast_to([128, T_ALL])
                    e.tensor_tensor_scan(zar[:, 1:], wbc, vvr[m][:, :],
                                         0.0, OP.mult, OP.add)
                    e.tensor_tensor_scan(zbr[:, 1:], wbc, ekr[m][:, :],
                                         0.0, OP.mult, OP.add)
                    e.drain()
                    e.scalar_tensor_tensor(vvr[m][:, LANE:], vvr[m][:, LANE:],
                                           eu_sb[:, l:l + 1],
                                           zar[:, LANE:T_ALL],
                                           OP.mult, OP.add)
                    inc(e.scalar_tensor_tensor(ekr[m][:, LANE:],
                                               ekr[m][:, LANE:],
                                               eu_sb[:, l:l + 1],
                                               zbr[:, LANE:T_ALL],
                                               OP.mult, OP.add), "sden")
                elif kind == "q":
                    m = l % NROW
                    e.wait_ge(sem["srd"], b["srd"] + l + 1)
                    e.drain()
                    inc(e.tensor_tensor(wkv_sb[:, l, :],
                                        vvr[m][:, LANE:],
                                        ekr[m][:, LANE:],
                                        OP.mult), "sv")
                elif kind == "sr":
                    e.wait_ge(sem["sev"], b["sev"] + (l + 1) * ntc)
                    e.drain()
                    inc(e.tensor_tensor(wkv_sb[:, l, :],
                                        wkv_sb[:, l, :],
                                        sr_sb[:, l, :], OP.mult), "sv")
                else:
                    _p, sl_g, oc_g, tj_g = g_info(NR + l)
                    t0k, t1k = TC_OWN[tj_g]
                    ocg_k = sl_g * OSL + oc_g
                    e.wait_ge(sem["sev"], b["sev"] + NR + l + 1)
                    e.drain()
                    inc(e.tensor_tensor(wkv_sb[:, ocg_k, t0k:t1k],
                                        wkv_sb[:, ocg_k, t0k:t1k],
                                        sgst[l % NSG][:, :t1k - t0k],
                                        OP.mult), "sv")

        @blk.scalar
        def _(e):
            recip_at = {}
            for ci in range(CH):
                recip_at.setdefault(RP[ci], []).append(ci)
            for g in range(NG):
                p, sl, oc, tj = g_info(g)
                if g == 0:
                    for ci in range(CH):
                        inc(e.dma_start(mov_rg[:, ci, :], sxr[ci, :, :]),
                            "sl", 16)
                if g == NR:
                    for ci in range(CH):
                        inc(e.dma_start(mov_rg[:, ci, :], sxg[ci, :, :]),
                            "sl", 16)
                for jj in s_casts_at.get(g, []):
                    emit_cast(e, jj)
                for ci_ in recip_at.get(g, []):
                    eb = ekr[ci_ % NROW]
                    e.wait_ge(sem["sden"], b["sden"] + ci_ + 1)
                    e.activation(eb[:, LANE:], eb[:, LANE:], AF.Ln)
                    e.drain()
                    inc(e.activation(eb[:, LANE:], eb[:, LANE:],
                                     AF.Exp, scale=-1.0), "srd")
                t0, t1 = TC_OWN[tj]
                ocg = sl * OSL + oc
                bank = ps[g % NBK]
                e.wait_ge(sem["smm"], b["smm"] + g + 1)
                if p == 0:
                    inc(e.activation(sr_sb[:, ocg, t0:t1], bank[:, :t1 - t0],
                                     AF.Sigmoid), "sev")
                elif p == 1:
                    k = g - NR
                    if k >= NSG:
                        e.wait_ge(sem["sv"],
                                  b["sv"] + svord[("gmul", k - NSG)])
                    inc(e.activation(sgst[k % NSG][:, :t1 - t0],
                                     bank[:, :t1 - t0], AF.Sigmoid), "sev")
                else:
                    inc(e.activation(sr_sb[:, ocg, t0:t1], bank[:, :t1 - t0],
                                     AF.Copy), "sev")
            for jj in s_casts_at.get(NG, []):
                emit_cast(e, jj)

        @blk.tensor
        def _(e):
            for p in range(3):
                if phs[p]["mov_load"] is not None:
                    e.wait_ge(sem["sl"], b["sl"] + 16 * CH * (p + 1))
                if p == 2:
                    e.wait_ge(sem["sv"], b["sv"] + NSV)
                for sl in range(NSLAB):
                    ss = p * NSLAB + sl
                    if CVs[ss] > cnt_scv0:
                        e.wait_ge(sem["scv"], CVs[ss])
                    e.wait_ge(sem["scs"], CSs[ss])
                    for oc in range(OSL):
                        for tj, (t0, t1) in enumerate(TC_OWN):
                            g = p * NGp + sl * OSL * ntc + oc * ntc + tj
                            bank = ps[g % NBK]
                            if g >= NBK:
                                e.wait_ge(sem["sev"], b["sev"] + g - NBK + 1)
                            for ic in range(CH):
                                mm = e.matmul(bank[:, :t1 - t0],
                                              wbf[ss % 2][:, ic,
                                                          oc * 128:(oc + 1) * 128],
                                              phs[p]["mov"][:, ic, t0:t1],
                                              start=(ic == 0),
                                              stop=(ic == CH - 1))
                            inc(mm, "smm")
        cnt["scv"], cnt["scs"] = cv, cs_

    o_sb = sr_sb

    with contextlib.ExitStack() as es:
        ec = es.enter_context
        oh16 = [ec(nc.sbuf_tensor("oh16_%d" % i, [128, T_OWN], F16))
                for i in range(2)]
        oq16 = [ec(nc.sbuf_tensor("oq16_%d" % i, [128, T_OWN], F16))
                for i in range(2)]
        stv = ec(nc.sbuf_tensor("stv", [1, T_OWN], F32))
        stm = ec(nc.sbuf_tensor("stm", [1, T_OWN], F32))
        stq = ec(nc.sbuf_tensor("stq", [1, T_OWN], F32))
        rb2 = ec(nc.sbuf_tensor("rb2", [128, T_OWN], F32))
        m2b2 = ec(nc.sbuf_tensor("m2b2", [128, T_OWN], F32))
        ost = [ec(nc.sbuf_tensor("ost%d" % i, [128, T_OWN], F32))
               for i in range(2)]

        with contextlib.ExitStack() as esA:
            ecA = esA.enter_context
            ps1 = [ecA(nc.psum_tensor("qs1_%d" % j, [1, 512], F32))
                   for j in range(len(TC_OWN))]
            ps2 = [ecA(nc.psum_tensor("qs2_%d" % j, [1, 512], F32))
                   for j in range(len(TC_OWN))]
            blk = esA.enter_context(nc.Block())
            b = snap()

            @blk.scalar
            def _(e):
                for ci in range(CH):
                    if ci >= 2:
                        e.wait_ge(sem["smm"], b["smm"] + ci - 1)
                    e.activation(oh16[ci % 2][:, :], o_sb[:, ci, :], AF.Copy)
                    inc(e.activation(oq16[ci % 2][:, :], o_sb[:, ci, :],
                                     AF.Square), "ss")
                e.wait_ge(sem["sv"], b["sv"] + 1)
                e.activation(stv[:, :], stv[:, :], AF.Ln)
                e.drain()
                inc(e.activation(stv[:, :], stv[:, :], AF.Exp, scale=-0.5),
                    "ss")

            @blk.tensor
            def _(e):
                for ci in range(CH):
                    e.wait_ge(sem["ss"], b["ss"] + ci + 1)
                    for j, (t0, t1) in enumerate(TC_OWN):
                        e.matmul(ps1[j][:, :t1 - t0], ones16[:, :],
                                 oh16[ci % 2][:, t0:t1],
                                 start=(ci == 0), stop=(ci == CH - 1))
                        mm = e.matmul(ps2[j][:, :t1 - t0], ones16[:, :],
                                      oq16[ci % 2][:, t0:t1],
                                      start=(ci == 0), stop=(ci == CH - 1))
                    inc(mm, "smm")

            @blk.vector
            def _(e):
                e.wait_ge(sem["smm"], b["smm"] + CH)
                for j, (t0, t1) in enumerate(TC_OWN):
                    e.tensor_scalar(stm[:, t0:t1], ps1[j][:, :t1 - t0],
                                    1.0 / C, None, OP.mult)
                    e.tensor_scalar(stv[:, t0:t1], ps2[j][:, :t1 - t0],
                                    1.0 / C, None, OP.mult)
                e.drain()
                e.tensor_tensor(stq[:, :], stm[:, :], stm[:, :], OP.mult)
                e.drain()
                e.tensor_tensor(stv[:, :], stv[:, :], stq[:, :], OP.subtract)
                e.drain()
                inc(e.tensor_scalar(stv[:, :], stv[:, :], LN_EPS, None,
                                    OP.add), "sv")
                e.wait_ge(sem["ss"], b["ss"] + CH + 1)
                e.drain()
                inc(e.tensor_tensor(stm[:, :], stm[:, :], stv[:, :], OP.mult),
                    "sv")

        with contextlib.ExitStack() as esB:
            ecB = esB.enter_context
            pb1 = ecB(nc.psum_tensor("qb1", [128, 512], F32))
            pb2 = ecB(nc.psum_tensor("qb2", [128, 512], F32))
            sstg = [ecB(nc.sbuf_tensor("sstg%d" % i, [128, T_OWN], F32))
                    for i in range(2)]
            xst = [ecB(nc.sbuf_tensor("xst%d" % i, [128, T_OWN], F32))
                   for i in range(2)]
            g2ld = [ecB(nc.sbuf_tensor("g2ld%d" % i, [128, T_OWN], BF16))
                    for i in range(2)]
            outst = [ecB(nc.sbuf_tensor("outst%d" % i, [128, T_OWN], F32))
                     for i in range(2)]
            blk = esB.enter_context(nc.Block())
            b = snap()
            ntc2 = len(TC_OWN)

            @blk.tensor
            def _(e):
                for j, (t0, t1) in enumerate(TC_OWN):
                    if j >= 1:
                        e.wait_ge(sem["sev"], b["sev"] + j)
                    e.matmul(pb1[:, :t1 - t0], onesK1[:, :], stv[:, t0:t1],
                             start=True, stop=True)
                    inc(e.matmul(pb2[:, :t1 - t0], onesK1[:, :], stm[:, t0:t1],
                                 start=True, stop=True), "smm")

            @blk.sync
            def _(e):
                for ci in range(CH):
                    if ci >= 2:
                        e.wait_ge(sem["sden"], b["sden"] + ci - 1)
                    inc(e.dma_start(xst[ci % 2][:, :],
                                    x_d[ci * 128:(ci + 1) * 128,
                                        1 + LANE:T_STG]),
                        "sw%d" % (ci % 2), 16)
                    inc(e.dma_start(g2ld[ci % 2][:, :], sg2[ci, :, :]),
                        "sw%d" % (ci % 2), 16)

            @blk.scalar
            def _(e):
                for j, (t0, t1) in enumerate(TC_OWN):
                    e.wait_ge(sem["smm"], b["smm"] + j + 1)
                    e.activation(rb2[:, t0:t1], pb1[:, :t1 - t0], AF.Copy)
                    inc(e.activation(m2b2[:, t0:t1], pb2[:, :t1 - t0],
                                     AF.Copy), "sev")
                for ci in range(CH):
                    e.wait_ge(sem["sv"], b["sv"] + ci + 1)
                    if ci >= 2:
                        e.wait_ge(sem["sden"], b["sden"] + ci - 1)
                    inc(e.activation(sstg[ci % 2][:, :], ost[ci % 2][:, :],
                                     AF.Identity,
                                     bias=vec_sb["lnb"][:, ci:ci + 1],
                                     scale=vec_sb["lng"][:, ci:ci + 1]), "ss")

            def final_part(e, ci):
                swp = "sw%d" % (ci % 2)
                e.wait_ge(sem["ss"], b["ss"] + ci + 1)
                e.wait_ge(sem[swp], b[swp] + 32 * (ci // 2 + 1))
                if ci >= 2:
                    stp = "st%d" % (ci % 2)
                    e.wait_ge(sem[stp], b[stp] + 16 * (ci // 2))
                e.drain()
                e.tensor_tensor(outst[ci % 2][:, :], sstg[ci % 2][:, :],
                                g2ld[ci % 2][:, :], OP.mult)
                e.drain()
                inc(e.tensor_tensor(outst[ci % 2][:, :], outst[ci % 2][:, :],
                                    xst[ci % 2][:, :], OP.add), "sden")

            @blk.vector
            def _(e):
                e.wait_ge(sem["sev"], b["sev"] + ntc2)
                for ci in range(CH):
                    if ci >= 2:
                        e.wait_ge(sem["ss"], b["ss"] + ci - 1)
                    e.drain()
                    e.tensor_tensor(ost[ci % 2][:, :], o_sb[:, ci, :],
                                    rb2[:, :], OP.mult)
                    e.drain()
                    inc(e.tensor_tensor(ost[ci % 2][:, :], ost[ci % 2][:, :],
                                        m2b2[:, :], OP.subtract), "sv")
                    if ci >= 1:
                        final_part(e, ci - 1)
                final_part(e, CH - 1)

            @blk.gpsimd
            def _(e):
                for ci in range(CH):
                    e.wait_ge(sem["sden"], b["sden"] + ci + 1)
                    inc(e.dma_start(y_d[ci, :, :], outst[ci % 2][:, :]),
                        "st%d" % (ci % 2), 16)
                e.wait_ge(sem["st0"], cnt["st0"])
                e.wait_ge(sem["st1"], cnt["st1"])

    es_o.close()
    es0.close()
    return nc



def stage_inputs(inputs, B, T, NCHUNK, C, T_OWN, LANE):
    CH = C // 128
    T_ALL = LANE + T_OWN
    T_STG = T_ALL + 1
    x = np.ascontiguousarray(np.asarray(inputs["L0_attn_x"], np.float32))

    wmap = {"wk": "L0_attn_Wk", "wv": "L0_attn_Wv", "wr": "L0_attn_Wr",
            "wg": "L0_attn_Wg", "wo": "L0_attn_Wo", "wf": "L0_ffn_weight"}
    vmap = {"tmk": "L0_attn_tm_k", "tmv": "L0_attn_tm_v", "tmr": "L0_attn_tm_r",
            "tmg": "L0_attn_tm_g", "td": "L0_attn_time_decay",
            "tf": "L0_attn_time_first", "lng": "L0_attn_ln_g",
            "lnb": "L0_attn_ln_b", "bfv": "L0_ffn_bias"}
    wts = {k: np.ascontiguousarray(np.asarray(inputs[n], np.float32).T)
           for k, n in wmap.items()}
    vecs = {k: np.ascontiguousarray(
        np.asarray(inputs[n], np.float32).reshape(CH, 128).T)
        for k, n in vmap.items()}

    in_maps = []
    for core in range(B * NCHUNK):
        bi, ch = core // NCHUNK, core % NCHUNK
        t0 = ch * T_OWN
        stg = np.zeros((T_STG, C), np.float32)
        if ch == 0:
            stg[LANE] = x[bi, 1]
        else:
            stg[0] = x[bi, t0 - LANE - 1]
            stg[1:1 + LANE] = x[bi, t0 - LANE:t0]
        stg[1 + LANE:] = x[bi, t0:t0 + T_OWN]
        m = {"x": np.ascontiguousarray(stg.T),
             "cs": np.full((128, 1), 0.0 if ch == 0 else 1.0, np.float32)}
        m.update(wts)
        m.update(vecs)
        in_maps.append(m)
    return in_maps


_CACHE = {}


def kernel(**inputs):
    C, B, T = 2048, 2, 4096
    NCHUNK, LANE = 4, 128
    T_OWN = T // NCHUNK
    key = (C, T_OWN, LANE)
    if key not in _CACHE:
        _CACHE[key] = build_nc(C, T_OWN, LANE)
    nc = _CACHE[key]
    in_maps = stage_inputs(inputs, B, T, NCHUNK, C, T_OWN, LANE)
    res = run_bass_kernel_spmd(nc, in_maps, core_ids=list(range(B * NCHUNK)))
    out = np.empty((B, T, C), np.float32)
    for core in range(B * NCHUNK):
        bi, ch = core // NCHUNK, core % NCHUNK
        t0 = ch * T_OWN
        yc = np.asarray(res.results[core]["y"], np.float32).reshape(C, T_OWN)
        out[bi, t0:t0 + T_OWN, :] = yc.T
    return out


# version 17 (final)
# speedup vs baseline: 1.1594x; 1.0202x over previous
import contextlib

import numpy as np

import concourse.bass as bass
import concourse.mybir as mybir
from concourse.bass_utils import run_bass_kernel_spmd

F32 = mybir.dt.float32
BF16 = mybir.dt.bfloat16
F16 = mybir.dt.float16
AF = mybir.ActivationFunctionType
OP = mybir.AluOpType
AX = mybir.AxisListType

LN_EPS = 1e-5
FFN_EPS = 1e-10


def _tchunks(total, step=512):
    return [(s, min(s + step, total)) for s in range(0, total, step)]


def build_nc(C, T_OWN, LANE):
    CH = C // 128
    HC = C // 2
    OH = HC // 128
    NL = T_OWN // LANE
    T_ALL = LANE + T_OWN
    T_STG = T_ALL + 1
    TC_ALL = _tchunks(T_ALL)
    TC_OWN = _tchunks(T_OWN)
    TC_STG = _tchunks(T_STG)
    NB = 4

    nc = bass.Bass("TRN2", target_bir_lowering=False)

    x_d = nc.dram_tensor("x", [C, T_STG], F32, kind="ExternalInput").ap()
    w_d = {k: nc.dram_tensor(k, [C, C], F32, kind="ExternalInput").ap()
           for k in ("wk", "wv", "wr", "wg", "wo", "wf")}
    vnames = ("tmk", "tmv", "tmr", "tmg", "td", "tf", "lng", "lnb", "bfv")
    vec_d = {k: nc.dram_tensor(k, [128, CH], F32, kind="ExternalInput").ap()
             for k in vnames}
    cs_d = nc.dram_tensor("cs", [128, 1], F32, kind="ExternalInput").ap()
    y_d = nc.dram_tensor("y", [CH, 128, T_OWN], F32, kind="ExternalOutput").ap()

    sxk = nc.dram_tensor("sxk", [CH, 128, T_ALL], BF16).ap()
    sxv = nc.dram_tensor("sxv", [CH, 128, T_ALL], BF16).ap()
    sxr = nc.dram_tensor("sxr", [CH, 128, T_OWN], BF16).ap()
    sxg = nc.dram_tensor("sxg", [CH, 128, T_OWN], BF16).ap()
    sek = nc.dram_tensor("sek", [CH, 128, T_ALL], F16).ap()
    svd = nc.dram_tensor("svd", [CH, 128, T_ALL], F16).ap()
    sg2 = nc.dram_tensor("sg2", [CH, 128, T_OWN], BF16).ap()

    es0 = contextlib.ExitStack()
    e0 = es0.enter_context

    sem = {n: e0(nc.semaphore(n)) for n in
           ("sl", "sw0", "sw1", "scv", "scs", "smm", "sev", "st0", "st1",
            "sv", "ss", "sden", "srd", "sg0", "sg1")}
    cnt = {n: 0 for n in sem}

    def snap():
        return dict(cnt)

    def inc(instr, name, amt=1):
        instr.then_inc(sem[name], amt)
        cnt[name] += amt

    vec_sb = {k: e0(nc.sbuf_tensor("v_" + k, [128, CH], F32)) for k in vnames}
    cs_sb = e0(nc.sbuf_tensor("cs_sb", [128, 1], F32))
    w_sb = e0(nc.sbuf_tensor("w_sb", [128, CH], F32))
    eu_sb = e0(nc.sbuf_tensor("eu_sb", [128, CH], F32))
    ones16 = e0(nc.sbuf_tensor("ones16", [128, 1], F16))
    onesK1 = e0(nc.sbuf_tensor("onesK1", [1, 128], F32))
    scale_sb = e0(nc.sbuf_tensor("scale_sb", [128, 1], F32))
    thr_sb = e0(nc.sbuf_tensor("thr_sb", [128, 1], F32))
    nthr_sb = e0(nc.sbuf_tensor("nthr_sb", [128, 1], F32))

    with nc.Block() as blk:
        b = snap()

        @blk.sync
        def _(e):
            for k in vnames:
                inc(e.dma_start(vec_sb[k][:, :], vec_d[k][:, :]), "sl", 16)
            inc(e.dma_start(cs_sb[:, :], cs_d[:, :]), "sl", 16)

        @blk.vector
        def _(e):
            e.memset(ones16[:, :], 1.0)
            inc(e.memset(onesK1[:, :], 1.0), "sv")

        @blk.scalar
        def _(e):
            e.wait_ge(sem["sl"], b["sl"] + 16 * (len(vnames) + 1))
            e.activation(w_sb[:, :], vec_sb["td"][:, :], AF.Exp)
            e.drain()
            e.activation(w_sb[:, :], w_sb[:, :], AF.Exp, scale=-1.0)
            e.drain()
            inc(e.activation(eu_sb[:, :], vec_sb["tf"][:, :], AF.Exp), "ss")

    group_id = [0]

    def mm_group(specs, shared, pre=None, tail_sync=None, tail_scalar=None):
        group_id[0] += 1
        P = len(specs)
        ntc = [len(s["tcs"]) for s in specs]
        ngh = [OH * n for n in ntc]
        GRP0 = [0] * P
        for p in range(1, P):
            GRP0[p] = GRP0[p - 1] + 2 * ngh[p - 1]
        NG = GRP0[-1] + 2 * ngh[-1]

        def g_info(g):
            p = 0
            while p + 1 < P and g >= GRP0[p + 1]:
                p += 1
            r = g - GRP0[p]
            h = r // ngh[p]
            r2 = r % ngh[p]
            oc = r2 // ntc[p]
            tj = r2 % ntc[p]
            return p, h, oc, tj

        NH = 2 * P
        GEND = [0] * NH
        for hh in range(NH):
            p, h = hh // 2, hh % 2
            GEND[hh] = GRP0[p] + (h + 1) * ngh[p]
        release = [0 if hh < 2 else GEND[hh - 2] for hh in range(NH)]

        preL = pre["preL"] if pre else 0
        castrec = list(pre["recs"]) if pre else []
        cv = cnt["scv"]
        cs_ = cnt["scs"]
        b0scv, b0scs = cnt["scv"], cnt["scs"]
        CV = [0] * NH
        CS = [0] * NH
        for hh in range(NH):
            p = hh // 2
            for ic in range(CH):
                jj = hh * CH + ic
                if jj < preL:
                    continue
                if specs[p]["quant"] or ic % 2 == 0:
                    cv += 1
                    castrec.append(("V", "scv", cv))
                else:
                    cs_ += 1
                    castrec.append(("S", "scs", cs_))
            CV[hh], CS[hh] = cv, cs_

        movpre = pre.get("mov_sl", {}) if pre else {}
        slt = {}
        run_sl = cnt["sl"]
        for p, s in enumerate(specs):
            if p in movpre:
                slt[p] = movpre[p]
            elif s["mov_load"] is not None:
                run_sl += 16 * CH
                slt[p] = run_sl

        signum = {}
        k = 0
        for g in range(NG):
            p, h, oc, tj = g_info(g)
            if specs[p]["evac"] in ("sig_attn", "sig_y_res"):
                signum[g] = k
                k += 1
        nsig = k
        rownum = {}
        g2num = {}
        sk = 0
        for p, s in enumerate(specs):
            if s["evac"] in ("exp_row", "copy_row"):
                for ocg in range(CH):
                    rownum[(p, ocg)] = sk
                    sk += 1
        for g in range(NG):
            p, h, oc, tj = g_info(g)
            if specs[p]["evac"] == "silu_g2":
                g2num[g] = sk
                sk += 1

        b = snap()
        NBK = len(shared["ps"])
        wbf = shared["wbf"]
        mwstg = shared["mwstg"]
        mwm2 = shared["mwm2"]
        rowst = shared["rowst"]
        sigst = shared["sigst"]
        g2st = shared["g2st"]
        ps = shared["ps"]
        blk = shared["es"].enter_context(nc.Block())

        mov_at = {}
        for p, s in enumerate(specs):
            if s["mov_load"] is not None and p not in movpre:
                mov_at.setdefault(GRP0[p], []).append(p)

        movsync = pre.get("mov_sync", {}) if pre else {}

        @blk.sync
        def _(e):
            for p, n in movsync.items():
                if GRP0[p] == 0:
                    for ci in range(n):
                        inc(e.dma_start(specs[p]["mov"][:, ci, :],
                                        specs[p]["mov_load"][ci, :, :]),
                            "sl", 16)
            for p, s in enumerate(specs):
                if p >= 1 and GRP0[p - 1] > 0 and movsync.get(p - 1):
                    e.wait_ge(sem["smm"], b["smm"] + GRP0[p - 1])
                    for ci in range(movsync[p - 1]):
                        inc(e.dma_start(specs[p - 1]["mov"][:, ci, :],
                                        specs[p - 1]["mov_load"][ci, :, :]),
                            "sl", 16)
                for h in range(2):
                    hh = 2 * p + h
                    for ic in range(CH):
                        jj = hh * CH + ic
                        if jj < preL:
                            continue
                        if jj >= 2:
                            eng, sname, tgt = castrec[jj - 2]
                            e.wait_ge(sem[sname], tgt)
                        inc(e.dma_start(mwstg[jj % 2][:, :],
                                        s["wt"][ic * 128:(ic + 1) * 128,
                                                (h * HC):(h + 1) * HC]),
                            "sw%d" % (jj % 2), 16)
            lp = len(specs) - 1
            if GRP0[lp] > 0 and movsync.get(lp):
                e.wait_ge(sem["smm"], b["smm"] + GRP0[lp])
                for ci in range(movsync[lp]):
                    inc(e.dma_start(specs[lp]["mov"][:, ci, :],
                                    specs[lp]["mov_load"][ci, :, :]),
                        "sl", 16)
            if tail_sync is not None:
                tail_sync(e)

        swtot = [b["sw0"], b["sw1"]]
        swtarget = [0] * (NH * CH)
        for jj in range(NH * CH):
            if jj < preL:
                continue
            swtot[jj % 2] += 16
            swtarget[jj] = swtot[jj % 2]
        cnt["sw0"] = swtot[0]
        cnt["sw1"] = swtot[1]

        def emit_cast(e, jj):
            hh, ic = jj // CH, jj % CH
            p, h = hh // 2, hh % 2
            s = specs[p]
            if release[hh] > 0:
                e.wait_ge(sem["smm"], b["smm"] + release[hh])
            e.wait_ge(sem["sw%d" % (jj % 2)], swtarget[jj])
            dstsl = wbf[hh % 2][:, ic, :]
            if s["quant"]:
                e.tensor_single_scalar(mwm2[jj % 2][:, :], mwstg[jj % 2][:, :],
                                       thr_sb[:, :], OP.is_gt)
                e.tensor_single_scalar(mwstg[jj % 2][:, :], mwstg[jj % 2][:, :],
                                       nthr_sb[:, :], OP.is_lt)
                e.drain()
                e.tensor_tensor(dstsl, mwm2[jj % 2][:, :], mwstg[jj % 2][:, :],
                                OP.subtract).then_inc(sem["scv"], 1)
            elif jj % 1 == 0 and castrec[jj][0] == "V":
                e.tensor_copy(dstsl, mwstg[jj % 2][:, :]).then_inc(sem["scv"], 1)
            else:
                e.activation(dstsl, mwstg[jj % 2][:, :],
                             AF.Copy).then_inc(sem["scs"], 1)

        v_casts_at = {}
        s_casts_at = {}
        for jj in range(preL, NH * CH):
            r = release[jj // CH]
            (v_casts_at if castrec[jj][0] == "V" else s_casts_at
             ).setdefault(r, []).append(jj)

        @blk.vector
        def _(e):
            prev_evac = None
            for g in range(NG):
                for jj in v_casts_at.get(g, []):
                    emit_cast(e, jj)
                p, h, oc, tj = g_info(g)
                s = specs[p]
                if s["evac"] in ("sig_attn", "sig_y_res"):
                    t0, t1 = s["tcs"][tj]
                    ocg = h * OH + oc
                    k = signum[g]
                    if s["evac"] != prev_evac:
                        e.drain()
                        prev_evac = s["evac"]
                    e.wait_ge(sem["sev"], b["sev"] + g + 1)
                    if s["evac"] == "sig_attn":
                        inc(e.tensor_tensor(s["wkv_bf"][:, ocg, t0:t1],
                                            sigst[k % 2][:, :t1 - t0],
                                            s["wkv_bf"][:, ocg, t0:t1],
                                            OP.mult), "sv")
                    else:
                        inc(e.tensor_tensor(s["y_dst"][:, ocg, t0:t1],
                                            sigst[k % 2][:, :t1 - t0],
                                            s["wkv_bf"][:, ocg, t0:t1],
                                            OP.mult), "sv")
            for jj in v_casts_at.get(NG, []):
                emit_cast(e, jj)

        @blk.scalar
        def _(e):
            for g in range(NG):
                for mp in mov_at.get(g, []):
                    for ci in range(movsync.get(mp, 0), CH):
                        inc(e.dma_start(specs[mp]["mov"][:, ci, :],
                                        specs[mp]["mov_load"][ci, :, :]),
                            "sl", 16)
                for jj in s_casts_at.get(g, []):
                    emit_cast(e, jj)
                p, h, oc, tj = g_info(g)
                s = specs[p]
                t0, t1 = s["tcs"][tj]
                ocg = h * OH + oc
                bank = ps[g % NBK]
                e.wait_ge(sem["smm"], b["smm"] + g + 1)
                ev = s["evac"]
                if ev in ("exp_row", "copy_row"):
                    ro = rownum[(p, ocg)]
                    if tj == 0 and ro >= 2:
                        stp = "st%d" % (ro % 2)
                        e.wait_ge(sem[stp], b[stp] + 16 * (ro // 2))
                    fn = AF.Exp if ev == "exp_row" else AF.Copy
                    inc(e.activation(rowst[ro % 2][:, t0:t1],
                                     bank[:, :t1 - t0], fn), "sev")
                elif ev == "silu_g2":
                    gk = g2num[g]
                    if gk >= 2:
                        stp = "st%d" % (gk % 2)
                        e.wait_ge(sem[stp], b[stp] + 16 * (gk // 2))
                    inc(e.activation(g2st[gk % 2][:, :t1 - t0],
                                     bank[:, :t1 - t0], AF.Silu,
                                     bias=s["bias_vec"][:, ocg:ocg + 1],
                                     scale=scale_sb[:, 0:1]), "sev")
                elif ev in ("sig_attn", "sig_y_res"):
                    k = signum[g]
                    if k >= 2:
                        e.wait_ge(sem["sv"], b["sv"] + k - 1)
                    inc(e.activation(sigst[k % 2][:, :t1 - t0],
                                     bank[:, :t1 - t0], AF.Sigmoid), "sev")
                elif ev == "copy_o":
                    inc(e.activation(s["dst"][:, ocg, t0:t1],
                                     bank[:, :t1 - t0], AF.Copy), "sev")
            for jj in s_casts_at.get(NG, []):
                emit_cast(e, jj)
            if tail_scalar is not None:
                tail_scalar(e)

        @blk.gpsimd
        def _(e):
            any_store = False
            for p, s in enumerate(specs):
                if s["evac"] in ("exp_row", "copy_row"):
                    any_store = True
                    for ocg in range(CH):
                        ro = rownum[(p, ocg)]
                        e.wait_ge(sem["sev"],
                                  b["sev"] + GRP0[p] + (ocg + 1) * ntc[p])
                        inc(e.dma_start(s["store_ap"][ocg, :, :],
                                        rowst[ro % 2][:, :]),
                            "st%d" % (ro % 2), 16)
                elif s["evac"] == "silu_g2":
                    any_store = True
                    for g in range(GRP0[p], GRP0[p] + 2 * ngh[p]):
                        _p, h, oc, tj = g_info(g)
                        t0, t1 = s["tcs"][tj]
                        ocg = h * OH + oc
                        gk = g2num[g]
                        e.wait_ge(sem["sev"], b["sev"] + g + 1)
                        inc(e.dma_start(s["store_ap"][ocg, :, t0:t1],
                                        g2st[gk % 2][:, :t1 - t0]),
                            "st%d" % (gk % 2), 16)
            if any_store:
                e.wait_ge(sem["st0"], cnt["st0"])
                e.wait_ge(sem["st1"], cnt["st1"])

        @blk.tensor
        def _(e):
            for p, s in enumerate(specs):
                if s["mov_load"] is not None:
                    e.wait_ge(sem["sl"], slt[p])
                if s.get("wait_y"):
                    e.wait_ge(sem["sv"], b["sv"] + nsig)
                for h in range(2):
                    hh = 2 * p + h
                    e.wait_ge(sem["scv"], CV[hh])
                    if CS[hh] > b0scs:
                        e.wait_ge(sem["scs"], CS[hh])
                    for oc in range(OH):
                        for tj, (t0, t1) in enumerate(s["tcs"]):
                            g = GRP0[p] + h * ngh[p] + oc * ntc[p] + tj
                            bank = ps[g % NBK]
                            if g >= NBK:
                                e.wait_ge(sem["sev"], b["sev"] + g - NBK + 1)
                            for ic in range(CH):
                                mm = e.matmul(bank[:, :t1 - t0],
                                              wbf[hh % 2][:, ic,
                                                          oc * 128:(oc + 1) * 128],
                                              s["mov"][:, ic, t0:t1],
                                              start=(ic == 0),
                                              stop=(ic == CH - 1))
                            inc(mm, "smm")
        cnt["scv"] = cv
        cnt["scs"] = cs_

    def spec(wt, tcs, evac, mov, mov_load=None, quant=False, store_ap=None,
             bias_vec=None, dst=None, wkv_bf=None, y_dst=None, wait_y=False):
        return dict(wt=wt, tcs=tcs, evac=evac, mov=mov, mov_load=mov_load,
                    quant=quant, store_ap=store_ap, bias_vec=bias_vec,
                    dst=dst, wkv_bf=wkv_bf, y_dst=y_dst, wait_y=wait_y)

    def make_shared(es, tag, n_tok, want_quant, want_row, want_sig, want_g2,
                    nps=8):
        ec = es.enter_context
        sh = {"es": es}
        sh["wbf"] = [ec(nc.sbuf_tensor(tag + "wbf%d" % i, [128, CH, HC], BF16))
                     for i in range(2)]
        sh["mwstg"] = [ec(nc.sbuf_tensor(tag + "mwstg%d" % i, [128, HC], F32))
                       for i in range(2)]
        sh["mwm2"] = ([ec(nc.sbuf_tensor(tag + "mwm2%d" % i, [128, HC], F32))
                       for i in range(2)] if want_quant else None)
        sh["rowst"] = ([ec(nc.sbuf_tensor(tag + "row%d" % i, [128, n_tok], F32))
                        for i in range(2)] if want_row else None)
        sh["sigst"] = ([ec(nc.sbuf_tensor(tag + "sig%d" % i, [128, 512], F32))
                        for i in range(2)] if want_sig else None)
        sh["g2st"] = ([ec(nc.sbuf_tensor(tag + "g2s%d" % i, [128, 512], BF16))
                       for i in range(2)] if want_g2 else None)
        sh["ps"] = [ec(nc.psum_tensor(tag + "ps%d" % i, [128, 512], F32))
                    for i in range(nps)]
        return sh

    es1 = contextlib.ExitStack()
    xn_sb = es1.enter_context(nc.sbuf_tensor("xn_sb", [128, CH, T_OWN], BF16))

    es_pre = contextlib.ExitStack()
    g1wbf0 = es_pre.enter_context(
        nc.sbuf_tensor("g1wbf0", [128, CH, HC], BF16))
    g1mwstg = [es_pre.enter_context(
        nc.sbuf_tensor("g1mwstg%d" % i, [128, HC], F32)) for i in range(2)]
    g1_pre_recs = []

    with contextlib.ExitStack() as es:
        ec = es.enter_context
        xh16 = ec(nc.sbuf_tensor("xh16", [128, CH, T_STG], F16))
        xstg = [ec(nc.sbuf_tensor("xstg%d" % i, [128, T_STG], F32))
                for i in range(2)]
        xq16 = [ec(nc.sbuf_tensor("xq16_%d" % i, [128, T_STG], F16))
                for i in range(2)]
        stat = ec(nc.sbuf_tensor("stat", [1, T_STG], F32))
        stat2 = ec(nc.sbuf_tensor("stat2", [1, T_STG], F32))
        stat3 = ec(nc.sbuf_tensor("stat3", [1, T_STG], F32))
        rb_sb = ec(nc.sbuf_tensor("rb_sb", [128, T_STG], F32))
        m2b_sb = ec(nc.sbuf_tensor("m2b_sb", [128, T_STG], F32))
        xnf = ec(nc.sbuf_tensor("xnf", [128, T_STG], F32))
        xxf = ec(nc.sbuf_tensor("xxf", [128, T_ALL], F32))
        xn_stg = [ec(nc.sbuf_tensor("xn_stg%d" % i, [128, T_OWN], BF16))
                  for i in range(2)]
        ik_stg = [ec(nc.sbuf_tensor("ik_stg%d" % i, [128, T_ALL], BF16))
                  for i in range(2)]
        iv_stg = [ec(nc.sbuf_tensor("iv_stg%d" % i, [128, T_ALL], BF16))
                  for i in range(2)]
        ir_stg = [ec(nc.sbuf_tensor("ir_stg%d" % i, [128, T_OWN], BF16))
                  for i in range(2)]
        ig_stg = [ec(nc.sbuf_tensor("ig_stg%d" % i, [128, T_OWN], BF16))
                  for i in range(2)]

        with contextlib.ExitStack() as esA:
            ecA = esA.enter_context
            ps1 = [ecA(nc.psum_tensor("ps1_%d" % j, [1, 512], F32))
                   for j in range(len(TC_STG))]
            ps2 = [ecA(nc.psum_tensor("ps2_%d" % j, [1, 512], F32))
                   for j in range(len(TC_STG))]
            blk = esA.enter_context(nc.Block())
            b = snap()

            @blk.sync
            def _(e):
                for ci in range(CH):
                    if ci >= 2:
                        e.wait_ge(sem["ss"], b["ss"] + ci - 1)
                    inc(e.dma_start(xstg[ci % 2][:, :],
                                    x_d[ci * 128:(ci + 1) * 128, :]),
                        "sw%d" % (ci % 2), 16)

            @blk.scalar
            def _(e):
                for ci in range(CH):
                    swp = "sw%d" % (ci % 2)
                    e.wait_ge(sem[swp], b[swp] + 16 * (ci // 2 + 1))
                    if ci >= 2:
                        e.wait_ge(sem["smm"], b["smm"] + ci - 1)
                    e.activation(xh16[:, ci, :], xstg[ci % 2][:, :], AF.Copy)
                    inc(e.activation(xq16[ci % 2][:, :], xstg[ci % 2][:, :],
                                     AF.Square), "ss")
                e.wait_ge(sem["sv"], b["sv"] + 1)
                e.activation(stat[:, :], stat[:, :], AF.Ln)
                e.drain()
                inc(e.activation(stat[:, :], stat[:, :], AF.Exp, scale=-0.5),
                    "ss")

            @blk.tensor
            def _(e):
                for ci in range(CH):
                    e.wait_ge(sem["ss"], b["ss"] + ci + 1)
                    for j, (t0, t1) in enumerate(TC_STG):
                        e.matmul(ps1[j][:, :t1 - t0], ones16[:, :],
                                 xh16[:, ci, t0:t1],
                                 start=(ci == 0), stop=(ci == CH - 1))
                        mm = e.matmul(ps2[j][:, :t1 - t0], ones16[:, :],
                                      xq16[ci % 2][:, t0:t1],
                                      start=(ci == 0), stop=(ci == CH - 1))
                    inc(mm, "smm")

            @blk.vector
            def _(e):
                e.wait_ge(sem["smm"], b["smm"] + CH)
                for j, (t0, t1) in enumerate(TC_STG):
                    e.tensor_scalar(stat2[:, t0:t1], ps1[j][:, :t1 - t0],
                                    1.0 / C, None, OP.mult)
                    e.tensor_scalar(stat[:, t0:t1], ps2[j][:, :t1 - t0],
                                    1.0 / C, None, OP.mult)
                e.drain()
                e.tensor_tensor(stat3[:, :], stat2[:, :], stat2[:, :], OP.mult)
                e.drain()
                e.tensor_tensor(stat[:, :], stat[:, :], stat3[:, :], OP.subtract)
                e.drain()
                inc(e.tensor_scalar(stat[:, :], stat[:, :], LN_EPS, None,
                                    OP.add), "sv")
                e.wait_ge(sem["ss"], b["ss"] + CH + 1)
                e.drain()
                inc(e.tensor_tensor(stat2[:, :], stat2[:, :], stat[:, :],
                                    OP.mult), "sv")

        with contextlib.ExitStack() as esB:
            ecB = esB.enter_context
            pb1 = ecB(nc.psum_tensor("pb1", [128, 512], F32))
            pb2 = ecB(nc.psum_tensor("pb2", [128, 512], F32))
            SCW = min(512, HC)
            ps_sc = ecB(nc.psum_tensor("ps_sc", [1, SCW], F32))
            ps_bc = ecB(nc.psum_tensor("ps_bc", [128, 4], F32))
            habs = [ecB(nc.sbuf_tensor("fs_a%d" % i, [128, HC], F16))
                    for i in range(2)]
            sctile = ecB(nc.sbuf_tensor("sctile", [1, SCW], F32))
            sctiny = ecB(nc.sbuf_tensor("sctiny", [1, 4], F32))
            blk = esB.enter_context(nc.Block())
            b = snap()
            QC = _tchunks(HC)
            nbc = len(TC_STG)
            NSL = 2 * CH
            NSLK = NSL + CH

            @blk.tensor
            def _(e):
                for j, (t0, t1) in enumerate(TC_STG):
                    if j >= 1:
                        e.wait_ge(sem["sev"], b["sev"] + j)
                    e.matmul(pb1[:, :t1 - t0], onesK1[:, :], stat[:, t0:t1],
                             start=True, stop=True)
                    inc(e.matmul(pb2[:, :t1 - t0], onesK1[:, :],
                                 stat2[:, t0:t1], start=True, stop=True), "smm")
                for sl_ in range(NSL):
                    e.wait_ge(sem["scv"], b["scv"] + sl_ + 1)
                    for j, (q0, q1) in enumerate(QC):
                        mm = e.matmul(ps_sc[:, :q1 - q0], ones16[:, :],
                                      habs[sl_ % 2][:, q0:q1],
                                      start=(sl_ == 0 and j == 0),
                                      stop=(sl_ == NSL - 1 and j == len(QC) - 1))
                    inc(mm, "smm")
                e.wait_ge(sem["srd"], b["srd"] + 1)
                e.matmul(ps_bc[:, 0:1], onesK1[:, :], sctiny[:, 0:1],
                         start=True, stop=True)
                e.matmul(ps_bc[:, 1:2], onesK1[:, :], sctiny[:, 1:2],
                         start=True, stop=True)
                inc(e.matmul(ps_bc[:, 2:3], onesK1[:, :], sctiny[:, 2:3],
                             start=True, stop=True), "smm")

            @blk.sync
            def _(e):
                for sl_ in range(NSLK):
                    if sl_ >= 2:
                        if sl_ - 2 < NSL:
                            e.wait_ge(sem["scv"], b["scv"] + sl_ - 1)
                        else:
                            e.wait_ge(sem["scs"],
                                      b["scs"] + (sl_ - 2 - NSL) + 1)
                    if sl_ < NSL:
                        ic, h = sl_ // 2, sl_ % 2
                        src = w_d["wf"][ic * 128:(ic + 1) * 128,
                                        h * HC:(h + 1) * HC]
                    else:
                        ic = sl_ - NSL
                        src = w_d["wk"][ic * 128:(ic + 1) * 128, 0:HC]
                    inc(e.dma_start(g1mwstg[sl_ % 2][:, :], src),
                        "sw%d" % (sl_ % 2), 16)

            @blk.scalar
            def _(e):
                for j, (t0, t1) in enumerate(TC_STG):
                    e.wait_ge(sem["smm"], b["smm"] + j + 1)
                    e.activation(rb_sb[:, t0:t1], pb1[:, :t1 - t0], AF.Copy)
                    inc(e.activation(m2b_sb[:, t0:t1], pb2[:, :t1 - t0],
                                     AF.Copy), "sev")
                for sl_ in range(NSLK):
                    swp = "sw%d" % (sl_ % 2)
                    e.wait_ge(sem[swp], b[swp] + 16 * (sl_ // 2 + 1))
                    if sl_ < NSL:
                        if sl_ >= 2:
                            e.wait_ge(sem["smm"], b["smm"] + nbc + sl_ - 1)
                        inc(e.activation(habs[sl_ % 2][:, :],
                                         g1mwstg[sl_ % 2][:, :], AF.Abs),
                            "scv")
                    else:
                        ic = sl_ - NSL
                        inc(e.activation(g1wbf0[:, ic, :],
                                         g1mwstg[sl_ % 2][:, :], AF.Copy),
                            "scs")
                        g1_pre_recs.append(("S", "scs", cnt["scs"]))
                e.wait_ge(sem["smm"], b["smm"] + nbc + NSL + 1)
                e.activation(scale_sb[:, :], ps_bc[:, 0:1], AF.Copy)
                e.activation(thr_sb[:, :], ps_bc[:, 1:2], AF.Copy)
                e.activation(nthr_sb[:, :], ps_bc[:, 2:3], AF.Copy)

            @blk.vector
            def _(e):
                e.wait_ge(sem["sev"], b["sev"] + len(TC_STG))
                for ci in range(CH):
                    if ci >= 2:
                        stp = "st%d" % (ci % 2)
                        e.wait_ge(sem[stp], b[stp] + 80 * (ci // 2))
                    s = ci % 2
                    e.drain()
                    e.tensor_tensor(xnf[:, :], xh16[:, ci, :], rb_sb[:, :],
                                    OP.mult)
                    e.drain()
                    e.tensor_tensor(xnf[:, :], xnf[:, :], m2b_sb[:, :],
                                    OP.subtract)
                    e.drain()
                    e.tensor_tensor(xxf[:, :], xnf[:, 0:T_ALL], xnf[:, 1:T_STG],
                                    OP.subtract)
                    e.drain()
                    e.scalar_tensor_tensor(ik_stg[s][:, :], xxf[:, :],
                                           vec_sb["tmk"][:, ci:ci + 1],
                                           xnf[:, 1:T_STG], OP.mult, OP.add)
                    e.scalar_tensor_tensor(iv_stg[s][:, :], xxf[:, :],
                                           vec_sb["tmv"][:, ci:ci + 1],
                                           xnf[:, 1:T_STG], OP.mult, OP.add)
                    e.scalar_tensor_tensor(ir_stg[s][:, :], xxf[:, LANE:],
                                           vec_sb["tmr"][:, ci:ci + 1],
                                           xnf[:, 1 + LANE:T_STG], OP.mult,
                                           OP.add)
                    e.scalar_tensor_tensor(ig_stg[s][:, :], xxf[:, LANE:],
                                           vec_sb["tmg"][:, ci:ci + 1],
                                           xnf[:, 1 + LANE:T_STG], OP.mult,
                                           OP.add)
                    inc(e.tensor_copy(xn_stg[s][:, :], xnf[:, 1 + LANE:T_STG]),
                        "sv")
                e.wait_ge(sem["smm"], b["smm"] + nbc + NSL)
                e.tensor_copy(sctile[:, :], ps_sc[:, :])
                e.drain()
                e.reduce_sum(sctiny[:, 3:4], sctile[:, :], axis=AX.X)
                e.drain()
                e.tensor_scalar(sctiny[:, 0:1], sctiny[:, 3:4], 1.0 / (C * C),
                                None, OP.mult)
                e.drain()
                e.tensor_scalar(sctiny[:, 1:2], sctiny[:, 0:1], FFN_EPS, 0.5,
                                OP.add, OP.mult)
                e.drain()
                inc(e.tensor_scalar(sctiny[:, 2:3], sctiny[:, 1:2], -1.0, None,
                                    OP.mult), "srd")

            @blk.gpsimd
            def _(e):
                for ci in range(CH):
                    e.wait_ge(sem["sv"], b["sv"] + ci + 1)
                    s = ci % 2
                    stp = "st%d" % s
                    inc(e.dma_start(sxk[ci, :, :], ik_stg[s][:, :]), stp, 16)
                    inc(e.dma_start(sxv[ci, :, :], iv_stg[s][:, :]), stp, 16)
                    inc(e.dma_start(sxr[ci, :, :], ir_stg[s][:, :]), stp, 16)
                    inc(e.dma_start(sxg[ci, :, :], ig_stg[s][:, :]), stp, 16)
                    inc(e.dma_start(xn_sb[:, ci, :], xn_stg[s][:, :]), stp, 16)
                e.wait_ge(sem["st0"], cnt["st0"])
                e.wait_ge(sem["st1"], cnt["st1"])

    with contextlib.ExitStack() as es:
        ec = es.enter_context
        mov_kv = ec(nc.sbuf_tensor("mov_kv", [128, CH, T_ALL], BF16))
        sh = {"es": es,
              "wbf": [g1wbf0,
                      ec(nc.sbuf_tensor("g1wbf1", [128, CH, HC], BF16))],
              "mwstg": g1mwstg,
              "mwm2": [ec(nc.sbuf_tensor("g1mwm2%d" % i, [128, HC], F32))
                       for i in range(2)],
              "rowst": [ec(nc.sbuf_tensor("g1row%d" % i, [128, T_ALL], F16))
                        for i in range(2)],
              "sigst": None,
              "g2st": [ec(nc.sbuf_tensor("g1g2s%d" % i, [128, 512], BF16))
                       for i in range(2)],
              "ps": [ec(nc.psum_tensor("g1ps%d" % i, [128, 512], F32))
                     for i in range(8)]}
        mm_group([
            spec(w_d["wk"], TC_ALL, "exp_row", mov_kv, mov_load=sxk,
                 store_ap=sek),
            spec(w_d["wv"], TC_ALL, "copy_row", mov_kv, mov_load=sxv,
                 store_ap=svd),
            spec(w_d["wf"], TC_OWN, "silu_g2", xn_sb, quant=True,
                 bias_vec=vec_sb["bfv"], store_ap=sg2),
        ], sh, pre=dict(preL=CH, recs=g1_pre_recs))
    es_pre.close()
    es1.close()

    es_o = contextlib.ExitStack()
    sr_sb = es_o.enter_context(
        nc.sbuf_tensor("sr_sb", [128, CH, T_OWN], BF16, side="right"))

    with contextlib.ExitStack() as es:
        ec = es.enter_context
        wkv_sb = ec(nc.sbuf_tensor("wkv_sb", [128, CH, T_OWN], BF16))
        mov_rg = ec(nc.sbuf_tensor("mov_rg", [128, CH, T_OWN], BF16))
        SW = min(512, C)
        NSLAB = C // SW
        OSL = SW // 128
        wbf = [ec(nc.sbuf_tensor("rgo_wbf%d" % i, [128, CH, SW], BF16))
               for i in range(2)]
        mwstg = [ec(nc.sbuf_tensor("rgo_mw%d" % i, [128, SW], F32))
                 for i in range(2)]
        NSG = 8
        sgst = [ec(nc.sbuf_tensor("rgo_sg%d" % i, [128, 512], BF16))
                for i in range(NSG)]
        NROW = 4
        ekr = [ec(nc.sbuf_tensor("r_ekr%d" % i, [128, T_ALL], F16))
               for i in range(NROW)]
        vvr = [ec(nc.sbuf_tensor("r_vvr%d" % i, [128, T_ALL], F16))
               for i in range(NROW)]
        zar = ec(nc.sbuf_tensor("zar", [128, 1 + T_ALL], F32))
        zbr = ec(nc.sbuf_tensor("zbr", [128, 1 + T_ALL], F32))
        ps = [ec(nc.psum_tensor("rgo_ps%d" % i, [128, 512], F32))
              for i in range(8)]
        blk = es.enter_context(nc.Block())
        b = snap()
        NBK = len(ps)
        cnt_scv0 = cnt["scv"]

        ntc = len(TC_OWN)
        NGp = NSLAB * OSL * ntc
        NR, NRG, NG = NGp, 2 * NGp, 3 * NGp
        phs = [dict(wt=w_d["wr"], mov=mov_rg, mov_load=sxr),
               dict(wt=w_d["wg"], mov=mov_rg, mov_load=sxg),
               dict(wt=w_d["wo"], mov=wkv_sb, mov_load=None)]

        def g_info(g):
            p, r = g // NGp, g % NGp
            return p, r // (OSL * ntc), (r // ntc) % OSL, r % ntc

        NSS = 3 * NSLAB
        GENDs = [(ss + 1) * OSL * ntc for ss in range(NSS)]
        release = [0 if ss < 2 else GENDs[ss - 2] for ss in range(NSS)]

        castrec = []
        cv, cs_ = cnt["scv"], cnt["scs"]
        CVs, CSs = [0] * NSS, [0] * NSS
        for ss in range(NSS):
            for ic in range(CH):
                cs_ += 1
                castrec.append(("S", "scs", cs_))
            CVs[ss], CSs[ss] = cv, cs_

        QLAG = 2
        v_events = []
        for i in range(CH + QLAG):
            if i < CH:
                v_events.append(("chain", i))
            cj = i - QLAG
            if 0 <= cj < CH:
                v_events.append(("q", cj))
                v_events.append(("sr", cj))
        for k in range(NGp):
            v_events.append(("gmul", k))

        svord = {}
        i = 0
        for kind, l in v_events:
            if kind != "chain":
                i += 1
                svord[(kind, l)] = i
        NSV = i
        RP = {ci: min(4 + 2 * ci, NRG - 1) for ci in range(CH)}

        swtot = [b["sw0"], b["sw1"]]
        swtarget = [0] * (NSS * CH)
        for jj in range(NSS * CH):
            swtot[jj % 2] += 16
            swtarget[jj] = swtot[jj % 2]
        cnt["sw0"], cnt["sw1"] = swtot[0], swtot[1]

        sttot = [b["st0"], b["st1"]]
        st_row = {}
        for ci in range(CH):
            sttot[ci % 2] += 32
            st_row[ci] = sttot[ci % 2]
        cnt["st0"], cnt["st1"] = sttot[0], sttot[1]

        def emit_cast(e, jj):
            ss, ic = jj // CH, jj % CH
            if release[ss] > 0:
                e.wait_ge(sem["smm"], b["smm"] + release[ss])
            e.wait_ge(sem["sw%d" % (jj % 2)], swtarget[jj])
            if castrec[jj][0] == "V":
                e.tensor_copy(wbf[ss % 2][:, ic, :],
                              mwstg[jj % 2][:, :]).then_inc(sem["scv"], 1)
            else:
                e.activation(wbf[ss % 2][:, ic, :], mwstg[jj % 2][:, :],
                             AF.Copy).then_inc(sem["scs"], 1)

        v_casts_at, s_casts_at = {}, {}
        for jj in range(NSS * CH):
            (v_casts_at if castrec[jj][0] == "V" else s_casts_at
             ).setdefault(release[jj // CH], []).append(jj)

        @blk.sync
        def _(e):
            for jj in range(NSS * CH):
                ss, ic = jj // CH, jj % CH
                p, sl = ss // NSLAB, ss % NSLAB
                if jj >= 2:
                    eng, sname, tgt = castrec[jj - 2]
                    e.wait_ge(sem[sname], tgt)
                e.dma_start(mwstg[jj % 2][:, :],
                            phs[p]["wt"][ic * 128:(ic + 1) * 128,
                                         sl * SW:(sl + 1) * SW]
                            ).then_inc(sem["sw%d" % (jj % 2)], 16)

        @blk.gpsimd
        def _(e):
            for ci in range(CH):
                if ci >= NROW:
                    e.wait_ge(sem["sv"], b["sv"] + svord[("q", ci - NROW)])
                elif ci >= 2:
                    e.wait_ge(sem["sden"], b["sden"] + ci - 1)
                stp = "st%d" % (ci % 2)
                e.dma_start(ekr[ci % NROW][:, :],
                            sek[ci, :, :]).then_inc(sem[stp], 16)
                e.dma_start(vvr[ci % NROW][:, :],
                            svd[ci, :, :]).then_inc(sem[stp], 16)

        @blk.vector
        def _(e):
            e.memset(zar[:, 0:1], 0.0)
            e.memset(zbr[:, 0:1], 0.0)
            e.drain()
            for kind, l in v_events:
                if kind == "chain":
                    m = l % NROW
                    e.wait_ge(sem["st%d" % (l % 2)], st_row[l])
                    e.drain()
                    e.tensor_scalar(ekr[m][:, 0:LANE], ekr[m][:, 0:LANE],
                                    cs_sb[:, 0:1], None, OP.mult)
                    e.drain()
                    e.tensor_tensor(vvr[m][:, :], ekr[m][:, :],
                                    vvr[m][:, :], OP.mult)
                    e.drain()
                    wbc = w_sb[:, l:l + 1].broadcast_to([128, T_ALL])
                    e.tensor_tensor_scan(zar[:, 1:], wbc, vvr[m][:, :],
                                         0.0, OP.mult, OP.add)
                    e.tensor_tensor_scan(zbr[:, 1:], wbc, ekr[m][:, :],
                                         0.0, OP.mult, OP.add)
                    e.drain()
                    e.scalar_tensor_tensor(vvr[m][:, LANE:], vvr[m][:, LANE:],
                                           eu_sb[:, l:l + 1],
                                           zar[:, LANE:T_ALL],
                                           OP.mult, OP.add)
                    inc(e.scalar_tensor_tensor(ekr[m][:, LANE:],
                                               ekr[m][:, LANE:],
                                               eu_sb[:, l:l + 1],
                                               zbr[:, LANE:T_ALL],
                                               OP.mult, OP.add), "sden")
                elif kind == "q":
                    m = l % NROW
                    e.wait_ge(sem["srd"], b["srd"] + l + 1)
                    e.drain()
                    inc(e.tensor_tensor(wkv_sb[:, l, :],
                                        vvr[m][:, LANE:],
                                        ekr[m][:, LANE:],
                                        OP.mult), "sv")
                elif kind == "sr":
                    e.wait_ge(sem["sev"], b["sev"] + (l + 1) * ntc)
                    e.drain()
                    inc(e.tensor_tensor(wkv_sb[:, l, :],
                                        wkv_sb[:, l, :],
                                        sr_sb[:, l, :], OP.mult), "sv")
                else:
                    _p, sl_g, oc_g, tj_g = g_info(NR + l)
                    t0k, t1k = TC_OWN[tj_g]
                    ocg_k = sl_g * OSL + oc_g
                    e.wait_ge(sem["sev"], b["sev"] + NR + l + 1)
                    e.drain()
                    inc(e.tensor_tensor(wkv_sb[:, ocg_k, t0k:t1k],
                                        wkv_sb[:, ocg_k, t0k:t1k],
                                        sgst[l % NSG][:, :t1k - t0k],
                                        OP.mult), "sv")

        @blk.scalar
        def _(e):
            recip_at = {}
            for ci in range(CH):
                recip_at.setdefault(RP[ci], []).append(ci)
            for g in range(NG):
                p, sl, oc, tj = g_info(g)
                if g == 0:
                    for ci in range(CH):
                        inc(e.dma_start(mov_rg[:, ci, :], sxr[ci, :, :]),
                            "sl", 16)
                if g == NR:
                    for ci in range(CH):
                        inc(e.dma_start(mov_rg[:, ci, :], sxg[ci, :, :]),
                            "sl", 16)
                for jj in s_casts_at.get(g, []):
                    emit_cast(e, jj)
                for ci_ in recip_at.get(g, []):
                    eb = ekr[ci_ % NROW]
                    e.wait_ge(sem["sden"], b["sden"] + ci_ + 1)
                    e.activation(eb[:, LANE:], eb[:, LANE:], AF.Ln)
                    e.drain()
                    inc(e.activation(eb[:, LANE:], eb[:, LANE:],
                                     AF.Exp, scale=-1.0), "srd")
                t0, t1 = TC_OWN[tj]
                ocg = sl * OSL + oc
                bank = ps[g % NBK]
                e.wait_ge(sem["smm"], b["smm"] + g + 1)
                if p == 0:
                    inc(e.activation(sr_sb[:, ocg, t0:t1], bank[:, :t1 - t0],
                                     AF.Sigmoid), "sev")
                elif p == 1:
                    k = g - NR
                    if k >= NSG:
                        e.wait_ge(sem["sv"],
                                  b["sv"] + svord[("gmul", k - NSG)])
                    inc(e.activation(sgst[k % NSG][:, :t1 - t0],
                                     bank[:, :t1 - t0], AF.Sigmoid), "sev")
                else:
                    inc(e.activation(sr_sb[:, ocg, t0:t1], bank[:, :t1 - t0],
                                     AF.Copy), "sev")
            for jj in s_casts_at.get(NG, []):
                emit_cast(e, jj)

        @blk.tensor
        def _(e):
            for p in range(3):
                if phs[p]["mov_load"] is not None:
                    e.wait_ge(sem["sl"], b["sl"] + 16 * CH * (p + 1))
                if p == 2:
                    e.wait_ge(sem["sv"], b["sv"] + NSV)
                for sl in range(NSLAB):
                    ss = p * NSLAB + sl
                    if CVs[ss] > cnt_scv0:
                        e.wait_ge(sem["scv"], CVs[ss])
                    e.wait_ge(sem["scs"], CSs[ss])
                    for oc in range(OSL):
                        for tj, (t0, t1) in enumerate(TC_OWN):
                            g = p * NGp + sl * OSL * ntc + oc * ntc + tj
                            bank = ps[g % NBK]
                            if g >= NBK:
                                e.wait_ge(sem["sev"], b["sev"] + g - NBK + 1)
                            for ic in range(CH):
                                mm = e.matmul(bank[:, :t1 - t0],
                                              wbf[ss % 2][:, ic,
                                                          oc * 128:(oc + 1) * 128],
                                              phs[p]["mov"][:, ic, t0:t1],
                                              start=(ic == 0),
                                              stop=(ic == CH - 1))
                            inc(mm, "smm")
        cnt["scv"], cnt["scs"] = cv, cs_

    o_sb = sr_sb

    with contextlib.ExitStack() as es:
        ec = es.enter_context
        oh16 = [ec(nc.sbuf_tensor("oh16_%d" % i, [128, T_OWN], F16))
                for i in range(2)]
        oq16 = [ec(nc.sbuf_tensor("oq16_%d" % i, [128, T_OWN], F16))
                for i in range(2)]
        stv = ec(nc.sbuf_tensor("stv", [1, T_OWN], F32))
        stm = ec(nc.sbuf_tensor("stm", [1, T_OWN], F32))
        stq = ec(nc.sbuf_tensor("stq", [1, T_OWN], F32))
        rb2 = ec(nc.sbuf_tensor("rb2", [128, T_OWN], F32))
        m2b2 = ec(nc.sbuf_tensor("m2b2", [128, T_OWN], F32))
        ost = [ec(nc.sbuf_tensor("ost%d" % i, [128, T_OWN], F32))
               for i in range(2)]

        with contextlib.ExitStack() as esA:
            ecA = esA.enter_context
            ps1 = [ecA(nc.psum_tensor("qs1_%d" % j, [1, 512], F32))
                   for j in range(len(TC_OWN))]
            ps2 = [ecA(nc.psum_tensor("qs2_%d" % j, [1, 512], F32))
                   for j in range(len(TC_OWN))]
            blk = esA.enter_context(nc.Block())
            b = snap()

            @blk.scalar
            def _(e):
                for ci in range(CH):
                    if ci >= 2:
                        e.wait_ge(sem["smm"], b["smm"] + ci - 1)
                    e.activation(oh16[ci % 2][:, :], o_sb[:, ci, :], AF.Copy)
                    inc(e.activation(oq16[ci % 2][:, :], o_sb[:, ci, :],
                                     AF.Square), "ss")
                e.wait_ge(sem["sv"], b["sv"] + 1)
                e.activation(stv[:, :], stv[:, :], AF.Ln)
                e.drain()
                inc(e.activation(stv[:, :], stv[:, :], AF.Exp, scale=-0.5),
                    "ss")

            @blk.tensor
            def _(e):
                for ci in range(CH):
                    e.wait_ge(sem["ss"], b["ss"] + ci + 1)
                    for j, (t0, t1) in enumerate(TC_OWN):
                        e.matmul(ps1[j][:, :t1 - t0], ones16[:, :],
                                 oh16[ci % 2][:, t0:t1],
                                 start=(ci == 0), stop=(ci == CH - 1))
                        mm = e.matmul(ps2[j][:, :t1 - t0], ones16[:, :],
                                      oq16[ci % 2][:, t0:t1],
                                      start=(ci == 0), stop=(ci == CH - 1))
                    inc(mm, "smm")

            @blk.vector
            def _(e):
                e.wait_ge(sem["smm"], b["smm"] + CH)
                for j, (t0, t1) in enumerate(TC_OWN):
                    e.tensor_scalar(stm[:, t0:t1], ps1[j][:, :t1 - t0],
                                    1.0 / C, None, OP.mult)
                    e.tensor_scalar(stv[:, t0:t1], ps2[j][:, :t1 - t0],
                                    1.0 / C, None, OP.mult)
                e.drain()
                e.tensor_tensor(stq[:, :], stm[:, :], stm[:, :], OP.mult)
                e.drain()
                e.tensor_tensor(stv[:, :], stv[:, :], stq[:, :], OP.subtract)
                e.drain()
                inc(e.tensor_scalar(stv[:, :], stv[:, :], LN_EPS, None,
                                    OP.add), "sv")
                e.wait_ge(sem["ss"], b["ss"] + CH + 1)
                e.drain()
                inc(e.tensor_tensor(stm[:, :], stm[:, :], stv[:, :], OP.mult),
                    "sv")

        with contextlib.ExitStack() as esB:
            ecB = esB.enter_context
            pb1 = ecB(nc.psum_tensor("qb1", [128, 512], F32))
            pb2 = ecB(nc.psum_tensor("qb2", [128, 512], F32))
            sstg = [ecB(nc.sbuf_tensor("sstg%d" % i, [128, T_OWN], F32))
                    for i in range(2)]
            xst = [ecB(nc.sbuf_tensor("xst%d" % i, [128, T_OWN], F32))
                   for i in range(2)]
            g2ld = [ecB(nc.sbuf_tensor("g2ld%d" % i, [128, T_OWN], BF16))
                    for i in range(2)]
            outst = [ecB(nc.sbuf_tensor("outst%d" % i, [128, T_OWN], F32))
                     for i in range(2)]
            blk = esB.enter_context(nc.Block())
            b = snap()
            ntc2 = len(TC_OWN)

            @blk.tensor
            def _(e):
                for j, (t0, t1) in enumerate(TC_OWN):
                    if j >= 1:
                        e.wait_ge(sem["sev"], b["sev"] + j)
                    e.matmul(pb1[:, :t1 - t0], onesK1[:, :], stv[:, t0:t1],
                             start=True, stop=True)
                    inc(e.matmul(pb2[:, :t1 - t0], onesK1[:, :], stm[:, t0:t1],
                                 start=True, stop=True), "smm")

            @blk.sync
            def _(e):
                for ci in range(CH):
                    if ci >= 2:
                        e.wait_ge(sem["sden"], b["sden"] + ci - 1)
                    inc(e.dma_start(xst[ci % 2][:, :],
                                    x_d[ci * 128:(ci + 1) * 128,
                                        1 + LANE:T_STG]),
                        "sw%d" % (ci % 2), 16)
                    inc(e.dma_start(g2ld[ci % 2][:, :], sg2[ci, :, :]),
                        "sw%d" % (ci % 2), 16)

            @blk.scalar
            def _(e):
                for j, (t0, t1) in enumerate(TC_OWN):
                    e.wait_ge(sem["smm"], b["smm"] + j + 1)
                    e.activation(rb2[:, t0:t1], pb1[:, :t1 - t0], AF.Copy)
                    inc(e.activation(m2b2[:, t0:t1], pb2[:, :t1 - t0],
                                     AF.Copy), "sev")
                for ci in range(CH):
                    e.wait_ge(sem["sv"], b["sv"] + ci + 1)
                    if ci >= 2:
                        e.wait_ge(sem["sden"], b["sden"] + ci - 1)
                    inc(e.activation(sstg[ci % 2][:, :], ost[ci % 2][:, :],
                                     AF.Identity,
                                     bias=vec_sb["lnb"][:, ci:ci + 1],
                                     scale=vec_sb["lng"][:, ci:ci + 1]), "ss")

            def final_part(e, ci):
                swp = "sw%d" % (ci % 2)
                e.wait_ge(sem["ss"], b["ss"] + ci + 1)
                e.wait_ge(sem[swp], b[swp] + 32 * (ci // 2 + 1))
                if ci >= 2:
                    stp = "st%d" % (ci % 2)
                    e.wait_ge(sem[stp], b[stp] + 16 * (ci // 2))
                e.drain()
                e.tensor_tensor(outst[ci % 2][:, :], sstg[ci % 2][:, :],
                                g2ld[ci % 2][:, :], OP.mult)
                e.drain()
                inc(e.tensor_tensor(outst[ci % 2][:, :], outst[ci % 2][:, :],
                                    xst[ci % 2][:, :], OP.add), "sden")

            @blk.vector
            def _(e):
                e.wait_ge(sem["sev"], b["sev"] + ntc2)
                for ci in range(CH):
                    if ci >= 2:
                        e.wait_ge(sem["ss"], b["ss"] + ci - 1)
                    e.drain()
                    e.tensor_tensor(ost[ci % 2][:, :], o_sb[:, ci, :],
                                    rb2[:, :], OP.mult)
                    e.drain()
                    inc(e.tensor_tensor(ost[ci % 2][:, :], ost[ci % 2][:, :],
                                        m2b2[:, :], OP.subtract), "sv")
                    if ci >= 1:
                        final_part(e, ci - 1)
                final_part(e, CH - 1)

            @blk.gpsimd
            def _(e):
                for ci in range(CH):
                    e.wait_ge(sem["sden"], b["sden"] + ci + 1)
                    inc(e.dma_start(y_d[ci, :, :], outst[ci % 2][:, :]),
                        "st%d" % (ci % 2), 16)
                e.wait_ge(sem["st0"], cnt["st0"])
                e.wait_ge(sem["st1"], cnt["st1"])

    es_o.close()
    es0.close()
    return nc



def stage_inputs(inputs, B, T, NCHUNK, C, T_OWN, LANE):
    CH = C // 128
    T_ALL = LANE + T_OWN
    T_STG = T_ALL + 1
    x = np.ascontiguousarray(np.asarray(inputs["L0_attn_x"], np.float32))

    wmap = {"wk": "L0_attn_Wk", "wv": "L0_attn_Wv", "wr": "L0_attn_Wr",
            "wg": "L0_attn_Wg", "wo": "L0_attn_Wo", "wf": "L0_ffn_weight"}
    vmap = {"tmk": "L0_attn_tm_k", "tmv": "L0_attn_tm_v", "tmr": "L0_attn_tm_r",
            "tmg": "L0_attn_tm_g", "td": "L0_attn_time_decay",
            "tf": "L0_attn_time_first", "lng": "L0_attn_ln_g",
            "lnb": "L0_attn_ln_b", "bfv": "L0_ffn_bias"}
    wts = {k: np.ascontiguousarray(np.asarray(inputs[n], np.float32).T)
           for k, n in wmap.items()}
    vecs = {k: np.ascontiguousarray(
        np.asarray(inputs[n], np.float32).reshape(CH, 128).T)
        for k, n in vmap.items()}

    in_maps = []
    for core in range(B * NCHUNK):
        bi, ch = core // NCHUNK, core % NCHUNK
        t0 = ch * T_OWN
        stg = np.zeros((T_STG, C), np.float32)
        if ch == 0:
            stg[LANE] = x[bi, 1]
        else:
            stg[0] = x[bi, t0 - LANE - 1]
            stg[1:1 + LANE] = x[bi, t0 - LANE:t0]
        stg[1 + LANE:] = x[bi, t0:t0 + T_OWN]
        m = {"x": np.ascontiguousarray(stg.T),
             "cs": np.full((128, 1), 0.0 if ch == 0 else 1.0, np.float32)}
        m.update(wts)
        m.update(vecs)
        in_maps.append(m)
    return in_maps


_CACHE = {}


def kernel(**inputs):
    C, B, T = 2048, 2, 4096
    NCHUNK, LANE = 4, 128
    T_OWN = T // NCHUNK
    key = (C, T_OWN, LANE)
    if key not in _CACHE:
        _CACHE[key] = build_nc(C, T_OWN, LANE)
    nc = _CACHE[key]
    in_maps = stage_inputs(inputs, B, T, NCHUNK, C, T_OWN, LANE)
    res = run_bass_kernel_spmd(nc, in_maps, core_ids=list(range(B * NCHUNK)))
    out = np.empty((B, T, C), np.float32)
    for core in range(B * NCHUNK):
        bi, ch = core // NCHUNK, core % NCHUNK
        t0 = ch * T_OWN
        yc = np.asarray(res.results[core]["y"], np.float32).reshape(C, T_OWN)
        out[bi, t0:t0 + T_OWN, :] = yc.T
    return out
